# version 16
# baseline (speedup 1.0000x reference)
import numpy as np
from contextlib import ExitStack

import concourse.bass as bass
import concourse.mybir as mybir
from concourse.bass_utils import run_bass_kernel_spmd

F32 = mybir.dt.float32
BF16 = mybir.dt.bfloat16
I32 = mybir.dt.int32
AF = mybir.ActivationFunctionType
OP = mybir.AluOpType

L = 4096
DM = 2048
DI = 4096
DXB = 1024
NQ = 4
TQ = L // NQ
TH = TQ // 2
NBLK = DI // 128
EPS = 1e-5

ENGINES = ("sync", "tensor", "vector", "scalar", "gpsimd")


class Plan:
    def __init__(self, nc, sems):
        self.nc = nc
        self.sems = sems
        self.count = {k: 0 for k in sems}
        self.ops = {e: [] for e in ENGINES}

    def op(self, eng, fn, waits=(), inc=None, inc_amt=1):
        waits = [(s, v) for (s, v) in waits if v > 0]
        self.ops[eng].append((fn, tuple(waits), inc, inc_amt))
        if inc is not None:
            self.count[inc] += inc_amt
            return self.count[inc]
        return None

    def dma(self, queue, out, in_, waits=(), sem="dma"):
        def fn(e, out=out, in_=in_):
            return e.dma_start(out=out, in_=in_)
        return self.op(queue, fn, waits=waits, inc=sem, inc_amt=16)

    def val(self, sem):
        return self.count[sem]

    def flush(self):
        ops = self.ops
        self.ops = {e: [] for e in ENGINES}
        sems = self.sems
        with self.nc.Block() as blk:
            for ename in ENGINES:
                if not ops[ename]:
                    continue

                def body(eng, _ops=ops[ename]):
                    for fn, waits, inc, amt in _ops:
                        for (s, v) in waits:
                            eng.wait_ge(sems[s], v)
                        inst = fn(eng)
                        if inc is not None:
                            inst.then_inc(sems[inc], amt)

                getattr(blk, ename)(body)


def build_program(debug=False):
    nc = bass.Bass("TRN2", target_bir_lowering=False)

    d = {}
    def din(name, shape, dt):
        d[name] = nc.dram_tensor(name, list(shape), dt, kind="ExternalInput")
        return d[name]

    x0 = din("x0", (L, DM), F32)
    win_t = din("win_t", (80, 16, 128, 128), F32)
    w1_t = din("w1_t", (16, 128, 128), F32)
    w2_t = din("w2_t", (32, 128, 128), F32)
    wout_t = din("wout_t", (32, 128, 2048), F32)
    wq_up = din("wq_up", (16, 128, 8192), I32)
    sc_up = din("sc_up", (16, 128, 8192), F32)
    wq_dn = din("wq_dn", (64, 128, 2048), I32)
    sc_dn = din("sc_dn", (64, 128, 2048), F32)
    bup_h = din("bup_h", (128, 64), F32)
    bdn_h = din("bdn_h", (128, 16), F32)
    aup_h = din("aup_h", (16, 128, 16), F32)
    bupl_h = din("bupl_h", (16, 8192), F32)
    adn_h = din("adn_h", (64, 128, 16), F32)
    bdnl_h = din("bdnl_h", (16, 2048), F32)
    dtb_h = din("dtb_h", (128, 32), F32)
    cw_h = din("cw_h", (128, 128), F32)
    cb_h = din("cb_h", (128, 32), F32)
    dvec_h = din("dvec_h", (128, 32), F32)
    alog_h = din("alog_h", (128, 512), F32)
    selx_h = din("selx_h", (4, 128, 128), F32)

    out_d = nc.dram_tensor("out", [L, DM], F32, kind="ExternalOutput")

    def scr(name, shape, dt):
        if debug and dt == F32:
            return nc.dram_tensor(name, list(shape), dt, kind="ExternalOutput")
        return nc.dram_tensor(name, list(shape), dt)

    win_bf = scr("win_bf", (80, 128, 16, 128), BF16)
    wout_bf = scr("wout_bf", (8, 128, 32, 256), BF16)
    wup_bf = scr("wup_bf", (64, 128, 16, 128), BF16)
    wdn_bf = scr("wdn_bf", (16, 128, 64, 128), BF16)
    sz_scr = scr("sz_scr", (NQ, DI, TQ), BF16)
    xact_scr = scr("xact_scr", (NQ, DI, TQ), BF16)
    b_scr = scr("b_scr", (NQ, DXB, TQ), BF16)
    c_scr = scr("c_scr", (NQ, DI, TQ), BF16)
    yg_scr = scr("yg_scr", (NQ, DI, TQ), BF16)
    r1_scr = scr("r1_scr", (L, DM), F32)
    dt1_scr = scr("dt1_scr", (NQ, 128, TQ), BF16)
    FINAL_STORES = (["sout", "sr10", "sr11", "gst0", "gst1", "sdbg", "st"] +
                    [f"sev{i}" for i in range(4)] + ["sxao0", "sxao1", "dqs0", "dqs1"])
    if debug:
        dbg_hg = nc.dram_tensor("dbg_hg0", [128, 16 * TQ], BF16)
        dbg_ug = nc.dram_tensor("dbg_ug0", [128, 16 * TQ], BF16)
        dbg_pg = nc.dram_tensor("dbg_pg0", [128, 16 * TQ], BF16)
        dbg_t1 = nc.dram_tensor("dbg_t10", [128, 8 * TQ], BF16)
        dbg_yb = nc.dram_tensor("dbg_yb0", [128, 4 * TQ], F32, kind="ExternalOutput")
        dbg_dA = nc.dram_tensor("dbg_dA0", [128, 16 * TQ], F32, kind="ExternalOutput")
        dbg_dt = nc.dram_tensor("dbg_dt0", [128, TQ], F32, kind="ExternalOutput")
        dbg_oat = nc.dram_tensor("dbg_oat0", [2, 128, 4 * 2048], BF16)
        dbg_mv = nc.dram_tensor("dbg_mv0", [2, 128, 16], F32, kind="ExternalOutput")
        dbg_rr = nc.dram_tensor("dbg_rr0", [2, 128, 2048], F32, kind="ExternalOutput")
        dbg_sbn = nc.dram_tensor("dbg_sbn0", [128, 96], F32, kind="ExternalOutput")
    else:
        dbg_hg = dbg_ug = dbg_dA = dbg_dt = dbg_oat = dbg_mv = None

    stack = ExitStack()
    sem_names = (["dma", "tr", "pe", "dve", "act", "fA", "fB", "fC", "fU", "ln", "st", "gp",
                  "sdt1", "sgt", "swos", "sx0f", "sr1r", "sout", "sdbg"] +
                 [f"{p}{i}" for p in ("sx0t", "swb", "sxao", "bxz", "br", "cr", "gst",
                                      "tra", "trd", "sr1", "swf", "swd", "dqf", "dqs")
                  for i in range(2)] + [f"sev{i}" for i in range(4)])
    sems = {n: stack.enter_context(nc.semaphore(n)) for n in sem_names}
    P = Plan(nc, sems)

    _uid = [0]

    def sb(name, shape, dt, st=stack):
        _uid[0] += 1
        return st.enter_context(nc.sbuf_tensor(f"{name}_u{_uid[0]}", list(shape), dt))

    w1T = sb("w1T", (128, 16 * 128), BF16)
    w2T = sb("w2T", (128, 32 * 128), BF16)
    a_sc = sb("a_sc", (128, 512), F32)
    state = sb("state", (128, 512), F32)
    cw = sb("cw", (128, 128), F32)
    cb = sb("cb", (128, 32), F32)
    dvec = sb("dvec", (128, 32), F32)
    dtb = sb("dtb", (128, 32), F32)
    bupT = sb("bupT", (128, 64), F32)
    bdnT = sb("bdnT", (128, 16), F32)
    aupT = sb("aupT", (128, 16 * 16), BF16)
    bupL = sb("bupL", (16, 8192), BF16)
    adnT = sb("adnT", (128, 64 * 16), BF16)
    bdnL = sb("bdnL", (16, 2048), BF16)
    selx = sb("selx", (128, 4 * 128), BF16)
    halo = sb("halo", (128, 256), BF16)
    pA = [stack.enter_context(nc.psum_tensor(f"pA{i}", [128, 512], F32)) for i in range(2)]
    pB = [stack.enter_context(nc.psum_tensor(f"pB{i}", [128, 512], F32)) for i in range(2)]
    pC = [stack.enter_context(nc.psum_tensor(f"pC{i}", [128, 512], F32)) for i in range(2)]
    pU = [stack.enter_context(nc.psum_tensor(f"pU{i}", [128, 512], F32)) for i in range(2)]

    def stage_w():
        st = ExitStack()
        t1 = sb("w_t1", (128, 2048), F32, st)
        t2 = sb("w_t2", (128, 4096), F32, st)
        t3 = sb("w_t3", (128, 512), F32, st)
        t4 = sb("w_t4", (128, 1280), F32, st)
        t5 = sb("w_t5", (128, 512), F32, st)
        t6 = sb("w_t6", (16, 8192), F32, st)
        t7 = sb("w_t7", (16, 2048), F32, st)
        P.dma("sync", t1[:, :], bass.AP(w1_t, 0, [[128, 128], [128 * 128, 16], [1, 128]]))
        P.dma("sync", t2[:, :], bass.AP(w2_t, 0, [[128, 128], [128 * 128, 32], [1, 128]]))
        P.dma("sync", t3[:, :], alog_h.ap())
        P.dma("sync", t4[:, 0:256], bass.AP(aup_h, 0, [[16, 128], [128 * 16, 16], [1, 16]]))
        P.dma("sync", t4[:, 256:1280], bass.AP(adn_h, 0, [[16, 128], [128 * 16, 64], [1, 16]]))
        P.dma("sync", t5[:, :], bass.AP(selx_h, 0, [[128, 128], [128 * 128, 4], [1, 128]]))
        P.dma("sync", t6[:, :], bupl_h.ap())
        P.dma("sync", t7[:, :], bdnl_h.ap())
        for (dst, src) in [(cw, cw_h), (cb, cb_h), (dvec, dvec_h), (dtb, dtb_h),
                           (bupT, bup_h), (bdnT, bdn_h)]:
            P.dma("sync", dst[:, :], src.ap())
        v = P.val("dma")
        P.op("vector", lambda e: e.tensor_copy(out=w1T[:, :], in_=t1[:, :]), waits=[("dma", v)])
        P.op("vector", lambda e: e.tensor_copy(out=w2T[:, :], in_=t2[:, :]))
        P.op("vector", lambda e: e.tensor_copy(out=aupT[:, :], in_=t4[:, 0:256]))
        P.op("vector", lambda e: e.tensor_copy(out=adnT[:, :], in_=t4[:, 256:1280]))
        P.op("vector", lambda e: e.tensor_copy(out=selx[:, :], in_=t5[:, :]))
        P.op("vector", lambda e: e.tensor_copy(out=bupL[:, :], in_=t6[:, :]))
        P.op("vector", lambda e: e.tensor_copy(out=bdnL[:, :], in_=t7[:, :]))
        P.op("vector", lambda e: e.memset(state[:, :], 0.0))
        P.op("vector", lambda e: e.memset(halo[:, :], 0.0))
        av = P.op("scalar", lambda e: e.activation(out=a_sc[:, :], in_=t3[:, :], func=AF.Exp),
                  waits=[("dma", v)], inc="act")
        P.op("vector", lambda e: e.tensor_scalar(out=a_sc[:, :], in0=a_sc[:, :], scalar1=-1.0,
                                                 scalar2=None, op0=OP.mult),
             waits=[("act", av)])
        P.flush()
        st.close()

    class DqJobs:
        def __init__(self, buf_f, buf_b, qb, sc):
            self.buf_f, self.buf_b, self.qb, self.sc = buf_f, buf_b, qb, sc
            self.cast = {0: [], 1: []}
            self.store = {0: [], 1: []}
            self.n = 0

        def emit(self, loads, store_ap, quant, extra_lw=()):
            bi = self.n % 2
            self.n += 1
            lw = list(extra_lw)
            prev = self.cast[bi][-1] if self.cast[bi] else 0
            if prev:
                lw.append(("fA", prev))
            for i, (dst, src) in enumerate(loads(bi)):
                P.dma("sync", dst, src, waits=(lw if i == 0 else ()), sem=f"dqf{bi}")
            ldv = [(f"dqf{bi}", P.val(f"dqf{bi}"))]
            sw = [(f"dqs{bi}", self.store[bi][-1])] if self.store[bi] else []
            if quant:
                av = P.op("scalar", lambda e, bi=bi: e.activation(
                    out=self.buf_f[bi][:, :], in_=self.qb[bi][:, :], func=AF.Copy,
                    bias=-7.5), waits=ldv + ([("fA", prev)] if prev else []), inc="act")
                cv = P.op("vector", lambda e, bi=bi: e.tensor_tensor(
                    out=self.buf_b[bi][:, :], in0=self.buf_f[bi][:, :],
                    in1=self.sc[bi][:, :], op=OP.mult), waits=[("act", av)] + sw,
                    inc="fA")
            else:
                cv = P.op("vector", lambda e, bi=bi: e.tensor_copy(
                    out=self.buf_b[bi][:, :], in_=self.buf_f[bi][:, :]),
                    waits=ldv + sw, inc="fA")
            self.cast[bi].append(cv)
            sv = P.dma("gpsimd", store_ap, self.buf_b[bi][:, :], waits=[("fA", cv)],
                       sem=f"dqs{bi}")
            self.store[bi].append(sv)

    def dq_rest_jobs(dqj):
        jobs = []
        for ki in range(32):
            jobs.append(lambda lw=(), ki=ki: dqj.emit(
                lambda bi, ki=ki: [(dqj.buf_f[bi][:, :],
                                    bass.AP(wout_t, ki * 128 * 2048,
                                            [[2048, 128], [1, 2048]]))],
                bass.AP(wout_bf, ki * 256,
                        [[32 * 256, 128], [128 * 32 * 256, 8], [1, 256]]),
                quant=False, extra_lw=lw))
        for ki in range(16):
            for c4 in range(4):
                jobs.append(lambda lw=(), ki=ki, c4=c4: dqj.emit(
                    lambda bi, ki=ki, c4=c4: [
                        (dqj.qb[bi][:, :],
                         bass.AP(wq_up, ki * 128 * 8192 + c4 * 2048,
                                 [[8192, 128], [1, 2048]])),
                        (dqj.sc[bi][:, :],
                         bass.AP(sc_up, ki * 128 * 8192 + c4 * 2048,
                                 [[8192, 128], [1, 2048]]))],
                    bass.AP(wup_bf, (c4 * 16) * 128 * 2048 + ki * 128,
                            [[2048, 128], [128 * 2048, 16], [1, 128]]),
                    quant=True, extra_lw=lw))
        for ki in range(64):
            jobs.append(lambda lw=(), ki=ki: dqj.emit(
                lambda bi, ki=ki: [
                    (dqj.qb[bi][:, :], wq_dn[ki, :, :]),
                    (dqj.sc[bi][:, :], sc_dn[ki, :, :])],
                bass.AP(wdn_bf, ki * 128,
                        [[64 * 128, 128], [128 * 64 * 128, 16], [1, 128]]),
                quant=True, extra_lw=lw))
        return jobs

    def stage_dq_rest():
        st = ExitStack()
        buf_f = [sb(f"dqr_f{i}", (128, 2048), F32, st) for i in range(2)]
        buf_b = [sb(f"dqr_b{i}", (128, 2048), BF16, st) for i in range(2)]
        qb = [sb(f"dqr_q{i}", (128, 2048), I32, st) for i in range(2)]
        sc = [sb(f"dqr_s{i}", (128, 2048), F32, st) for i in range(2)]
        for job in dq_rest_jobs(DqJobs(buf_f, buf_b, qb, sc)):
            job()
        P.op("sync", lambda e: e.nop(),
             waits=[("dqs0", P.val("dqs0")), ("dqs1", P.val("dqs1"))])
        P.flush()
        st.close()

    def stage_dq_win():
        st = ExitStack()
        buf_f = [sb(f"dq_f{i}", (128, 2048), F32, st) for i in range(2)]
        buf_b = [sb(f"dq_b{i}", (128, 2048), BF16, st) for i in range(2)]
        dqj = DqJobs(buf_f, buf_b, None, None)
        for mi in range(80):
            dqj.emit(
                lambda bi, mi=mi: [(buf_f[bi][:, :],
                                    bass.AP(win_t, mi * 16 * 128 * 128,
                                            [[128, 128], [128 * 128, 16], [1, 128]]))],
                bass.AP(win_bf, mi * 128 * 2048, [[2048, 128], [1, 2048]]),
                quant=False)
        P.op("sync", lambda e: e.nop(),
             waits=[("dqs0", P.val("dqs0")), ("dqs1", P.val("dqs1"))])
        P.flush()
        st.close()

    def stage_a(q, with_dq=False):
        st = ExitStack()
        x0t = [sb(f"a_x0t{i}", (128, 2048), F32, st) for i in range(2)]
        xtm = [sb(f"a_xtm{i}", (128, 2048), BF16, st) for i in range(2)]
        xb = sb("a_xb", (128, 16 * TQ), BF16, st)
        wbuf = [sb(f"a_w{i}", (128, 4096), BF16, st) for i in range(2)]
        ev = [sb(f"a_ev{i}", (128, 512), BF16, st) for i in range(4)]
        xsm = sb("a_xsm", (128, 8 * TQ), BF16, st)
        xfe = [sb(f"a_xfe{i}", (128, TH + 8), BF16, st) for i in range(2)]
        ctmp = [sb(f"a_ct{i}", (128, TH), F32, st) for i in range(8)]
        xao = [sb(f"a_xao{i}", (128, TH), BF16, st) for i in range(2)]

        cast_hist = []
        for tt in range(8):
            bi = tt % 2
            lw = [("dve", cast_hist[-2])] if len(cast_hist) >= 2 else []
            P.dma("sync", x0t[bi][:, :], x0[(q * 8 + tt) * 128:(q * 8 + tt + 1) * 128, :],
                  waits=lw, sem=f"sx0t{bi}")
            trw = [("tr", P.val("tr"))] if tt >= 2 else []
            cv = P.op("vector", lambda e, bi=bi: e.tensor_copy(out=xtm[bi][:, :], in_=x0t[bi][:, :]),
                      waits=[(f"sx0t{bi}", P.val(f"sx0t{bi}"))] + trw, inc="dve")
            cast_hist.append(cv)
            for ki in range(16):
                def trf(e, bi=bi, ki=ki, tt=tt):
                    return e.dma_start_transpose(
                        xb[:, ki * TQ + tt * 128: ki * TQ + tt * 128 + 128],
                        xtm[bi][:, ki * 128:(ki + 1) * 128])
                P.op("scalar", trf, waits=[("dve", cv)] if ki == 0 else (), inc="tr", inc_amt=16)
        xb_ready = P.val("tr")

        dq_jobs = []
        if with_dq:
            qb2 = [sb(f"a_qb{i}", (128, 2048), I32, st) for i in range(2)]
            sc2 = [sb(f"a_scl{i}", (128, 2048), F32, st) for i in range(2)]
            dqf2 = [sb(f"a_dqf{i}", (128, 2048), F32, st) for i in range(2)]
            dqb2 = [sb(f"a_dqb{i}", (128, 2048), BF16, st) for i in range(2)]
            dq_jobs = dq_rest_jobs(DqJobs(dqf2, dqb2, qb2, sc2))
        dq_ji = [0]

        def dq_step():
            ji = dq_ji[0]
            if ji < len(dq_jobs):
                dq_ji[0] = ji + 1
                dq_jobs[ji]()

        mm_hist = []
        ev_hist = []
        st_by_ebi = {}
        xao_st = []
        for half in range(2):
            c0 = half * TH
            for mi in range(81):
                dq_step()
                par = mi % 2
                ebi = mi % 4
                wv = None
                if mi < 80:
                    if mi % 2 == 0:
                        ww = [("pe", mm_hist[-3])] if len(mm_hist) >= 3 else []
                        P.dma("sync", wbuf[(mi // 2) % 2][:, :],
                              bass.AP(win_bf, mi * 128 * 2048,
                                      [[2048, 128], [128 * 2048, 2], [1, 2048]]),
                              waits=ww, sem=f"swb{(mi // 2) % 2}")
                    wv = (f"swb{(mi // 2) % 2}", P.val(f"swb{(mi // 2) % 2}"))

                def mmf(e, mi=mi, par=par, c0=c0):
                    for ki in range(16):
                        lhs = (wbuf[(mi // 2) % 2][:, (mi % 2) * 2048 + ki * 128:
                                                   (mi % 2) * 2048 + (ki + 1) * 128]
                               if mi < 80 else w1T[:, ki * 128:(ki + 1) * 128])
                        inst = e.matmul(pA[par][:, :], lhs,
                                        xb[:, ki * TQ + c0: ki * TQ + c0 + TH],
                                        start=(ki == 0), stop=(ki == 15))
                    return inst
                waits = [("tr", xb_ready)]
                if len(ev_hist) >= 2:
                    waits.append(("fA", ev_hist[-2]))
                if wv is not None:
                    waits.append(wv)
                pv = P.op("tensor", mmf, waits=waits, inc="pe")
                mm_hist.append(pv)
                col = q * TQ + c0
                ebw = [(f"sev{ebi}", P.val(f"sev{ebi}"))] if ebi in st_by_ebi else []
                if mi < 32:
                    av = P.op("scalar", lambda e, par=par, ebi=ebi: e.activation(
                        out=ev[ebi][:, :], in_=pA[par][:, :], func=AF.Silu),
                        waits=[("pe", pv)] + ebw, inc="fA")
                    st_by_ebi[ebi] = P.dma(
                        "sync", bass.AP(sz_scr, q * DI * TQ + (mi * 128) * TQ + c0,
                                        [[TQ, 128], [1, TH]]),
                        ev[ebi][:, :], waits=[("fA", av)], sem=f"sev{ebi}")
                elif mi < 40:
                    av = P.op("vector", lambda e, par=par, mi=mi, c0=c0: e.tensor_copy(
                        out=xsm[:, (mi - 32) * TQ + c0:(mi - 32) * TQ + c0 + TH],
                        in_=pA[par][:, :]), waits=[("pe", pv)], inc="fA")
                elif mi < 48:
                    av = P.op("vector", lambda e, par=par, ebi=ebi: e.tensor_copy(
                        out=ev[ebi][:, :], in_=pA[par][:, :]), waits=[("pe", pv)] + ebw, inc="fA")
                    st_by_ebi[ebi] = P.dma(
                        "sync", bass.AP(b_scr, q * DXB * TQ + ((mi - 40) * 128) * TQ + c0,
                                        [[TQ, 128], [1, TH]]),
                        ev[ebi][:, :], waits=[("fA", av)], sem=f"sev{ebi}")
                elif mi < 80:
                    av = P.op("vector", lambda e, par=par, ebi=ebi: e.tensor_copy(
                        out=ev[ebi][:, :], in_=pA[par][:, :]), waits=[("pe", pv)] + ebw, inc="fA")
                    st_by_ebi[ebi] = P.dma(
                        "sync", bass.AP(c_scr, q * DI * TQ + ((mi - 48) * 128) * TQ + c0,
                                        [[TQ, 128], [1, TH]]),
                        ev[ebi][:, :], waits=[("fA", av)], sem=f"sev{ebi}")
                else:
                    av = P.op("vector", lambda e, par=par, ebi=ebi: e.tensor_copy(
                        out=ev[ebi][:, :], in_=pA[par][:, :]), waits=[("pe", pv)] + ebw, inc="fA")
                    st_by_ebi[ebi] = P.dma(
                        "sync", bass.AP(dt1_scr, q * 128 * TQ + c0, [[TQ, 128], [1, TH]]),
                        ev[ebi][:, :], waits=[("fA", av)], sem=f"sev{ebi}")
                ev_hist.append(av)

            silu_hist = []
            for bi2 in range(NBLK):
                par = (81 + bi2) % 2
                xpar = bi2 % 2
                xi = bi2 // 4

                def selmm(e, par=par, xi=xi, bi2=bi2, c0=c0):
                    return e.matmul(pA[par][:, :],
                                    selx[:, (bi2 % 4) * 128:(bi2 % 4) * 128 + 128],
                                    xsm[:, xi * TQ + c0: xi * TQ + c0 + TH],
                                    start=True, stop=True)
                waits = [("fA", ev_hist[-1])]
                if len(ev_hist) >= 2:
                    waits.append(("fB", max(0, P.count["fB"] - 1)))
                pv = P.op("tensor", selmm, waits=waits, inc="pe")
                mm_hist.append(pv)
                hw = [("act", silu_hist[-2])] if len(silu_hist) >= 2 else []
                P.op("vector", lambda e, xpar=xpar, bi2=bi2: e.tensor_copy(
                    out=xfe[xpar][:, 0:8], in_=halo[:, bi2 * 8:bi2 * 8 + 8]),
                    waits=hw, inc="dve")
                P.op("vector", lambda e, xpar=xpar, par=par: e.tensor_copy(
                    out=xfe[xpar][:, 8:8 + TH], in_=pA[par][:, :]),
                    waits=[("pe", pv)], inc="fB")
                P.op("vector", lambda e, par=par, bi2=bi2: e.tensor_copy(
                    out=halo[:, bi2 * 8:bi2 * 8 + 8], in_=pA[par][:, TH - 8:TH]), inc="dve")
                cbase = xpar * 4
                P.op("vector", lambda e, xpar=xpar, bi2=bi2, cbase=cbase: e.tensor_scalar(
                    out=ctmp[cbase][:, :], in0=xfe[xpar][:, 5:5 + TH],
                    scalar1=cw[:, bi2 * 4:bi2 * 4 + 1], scalar2=None, op0=OP.mult), inc="dve")
                for k in range(1, 4):
                    P.op("vector", lambda e, xpar=xpar, bi2=bi2, k=k, cbase=cbase:
                         e.scalar_tensor_tensor(
                             out=ctmp[cbase + k][:, :], in0=xfe[xpar][:, 5 + k:5 + k + TH],
                             scalar=cw[:, bi2 * 4 + k:bi2 * 4 + k + 1],
                             in1=ctmp[cbase + k - 1][:, :], op0=OP.mult, op1=OP.add), inc="dve")
                dv = P.val("dve")
                xw = [(f"sxao{xpar}", P.val(f"sxao{xpar}"))] if len(xao_st) >= 2 else []
                av = P.op("scalar", lambda e, xpar=xpar, bi2=bi2, cbase=cbase: e.activation(
                    out=xao[xpar][:, :], in_=ctmp[cbase + 3][:, :], func=AF.Silu,
                    bias=cb[:, bi2:bi2 + 1]), waits=[("dve", dv)] + xw, inc="act")
                silu_hist.append(av)
                xao_st.append(P.dma(
                    "sync", bass.AP(xact_scr, q * DI * TQ + (bi2 * 128) * TQ + c0,
                                    [[TQ, 128], [1, TH]]),
                    xao[xpar][:, :], waits=[("act", av)], sem=f"sxao{xpar}"))
        P.op("sync", lambda e: e.nop(),
             waits=[(f"sev{i}", P.val(f"sev{i}")) for i in range(4)] +
                   [("sxao0", P.val("sxao0")), ("sxao1", P.val("sxao1"))])
        P.flush()
        st.close()

    def stage_b(q):
        st = ExitStack()
        spA = sb("b_spA", (128, TQ), F32, st)
        spB = sb("b_spB", (128, TQ), F32, st)
        dt_sb = sb("b_dt", (128, TQ), F32, st)
        dtx = sb("b_dtx", (128, TQ), BF16, st)
        xactb = [sb(f"b_xa{i}", (128, TQ), BF16, st) for i in range(2)]
        szb = [sb(f"b_sz{i}", (128, TQ), BF16, st) for i in range(2)]
        brep = [sb(f"b_br{i}", (128, 4 * TQ), BF16, st) for i in range(2)]
        crep = [sb(f"b_cr{i}", (128, 4 * TQ), BF16, st) for i in range(2)]
        dAg = [sb(f"b_dA{i}", (128, 4 * TQ), F32, st) for i in range(2)]
        ug = sb("b_u", (128, 4 * TQ), BF16, st)
        hg = sb("b_h", (128, 4 * TQ), BF16, st)
        pg = sb("b_p", (128, 4 * TQ), BF16, st)
        t1 = sb("b_t1", (128, 2 * TQ), BF16, st)
        t2 = sb("b_t2", (128, TQ), F32, st)
        t3 = sb("b_t3", (128, TQ), F32, st)
        dt1b = sb("b_dt1", (128, TQ), BF16, st)
        bjunk = sb("b_junk", (128, 4), F32, st)
        yb = [sb(f"b_y{i}", (128, TQ), F32, st) for i in range(2)]
        gat = [sb(f"b_g{i}", (128, TQ), BF16, st) for i in range(2)]

        u_hist, p_hist, scan_hist, gate_hist, dtv_hist, dA_hist, store_hist = \
            [], [], [], [], [], [], []
        t1_hist, t3_hist = [], []
        br_vals, cr_vals = {}, {}

        def emit_bc_loads(k):
            bi2, g2 = divmod(k, 4)
            gp2 = g2 % 2
            n02 = g2 * 4
            bw = [("dve", u_hist[-1])] if u_hist else []
            P.dma("gpsimd", brep[gp2][:, :],
                  bass.AP(b_scr, q * DXB * TQ + (32 * bi2 + n02) * TQ,
                          [[16 * TQ, 2], [0, 64], [1, 4 * TQ]]), waits=bw, sem=f"br{gp2}")
            br_vals[k] = P.val(f"br{gp2}")
            cw_ = [("dve", p_hist[-1])] if p_hist else []
            P.dma("gpsimd", crep[gp2][:, :],
                  bass.AP(c_scr, q * DI * TQ + (128 * bi2 + n02) * TQ,
                          [[16 * TQ, 8], [0, 16], [1, 4 * TQ]]), waits=cw_, sem=f"cr{gp2}")
            cr_vals[k] = P.val(f"cr{gp2}")
        gpw = ([("pe", P.val("pe"))] +
               [(f"sev{i}", P.val(f"sev{i}")) for i in range(4)] +
               [("sxao0", P.val("sxao0")), ("sxao1", P.val("sxao1"))])
        if q == 0:
            bw0 = [("dqs0", P.val("dqs0")), ("dqs1", P.val("dqs1"))]
            P.op("sync", lambda e: e.nop(), waits=bw0)
            gpw += bw0
        else:
            gpw += [("sout", P.val("sout")), ("trd0", P.val("trd0")),
                    ("trd1", P.val("trd1"))]
        P.op("gpsimd", lambda e: e.nop(), waits=gpw)
        P.dma("sync", dt1b[:, :], bass.AP(dt1_scr, q * 128 * TQ, [[TQ, 128], [1, TQ]]),
              sem="sdt1")
        dt1v = P.val("sdt1")
        for bi in range(NBLK):
            par = bi % 2
            lw = [("dve", gate_hist[-2])] if len(gate_hist) >= 2 else []
            P.dma("sync", xactb[par][:, :],
                  bass.AP(xact_scr, q * DI * TQ + bi * 128 * TQ, [[TQ, 128], [1, TQ]]),
                  waits=lw, sem=f"bxz{par}")
            P.dma("sync", szb[par][:, :],
                  bass.AP(sz_scr, q * DI * TQ + bi * 128 * TQ, [[TQ, 128], [1, TQ]]),
                  sem=f"bxz{par}")
            xzv = (f"bxz{par}", P.val(f"bxz{par}"))

            def dtmm(e, bi=bi):
                for j in range(2):
                    inst = e.matmul(pB[j][:, :], w2T[:, bi * 128:(bi + 1) * 128],
                                    dt1b[:, j * 512:(j + 1) * 512],
                                    start=True, stop=True)
                return inst
            pw = [("act", dA_hist[-1])] if len(dA_hist) >= 1 else []
            if bi == 0:
                pw.append(("sdt1", dt1v))
            pv = P.op("tensor", dtmm, waits=pw, inc="pe")
            aw = [("pe", pv)]
            if len(dtv_hist) >= 1:
                aw.append(("dve", dtv_hist[-1]))
            for j in range(2):
                P.op("scalar", lambda e, j=j, bi=bi: e.activation(
                    out=spA[:, j * 512:(j + 1) * 512], in_=pB[j][:, :], func=AF.Abs,
                    bias=dtb[:, bi:bi + 1]), waits=aw if j == 0 else (), inc="act")
            P.op("scalar", lambda e: e.activation(out=spA[:, :], in_=spA[:, :], func=AF.Exp,
                                                  scale=-1.0), inc="act")
            P.op("scalar", lambda e: e.activation(out=spA[:, :], in_=spA[:, :], func=AF.Ln,
                                                  bias=1.0), inc="act")
            for j in range(2):
                P.op("scalar", lambda e, j=j, bi=bi: e.activation(
                    out=spB[:, j * 512:(j + 1) * 512], in_=pB[j][:, :], func=AF.Relu,
                    bias=dtb[:, bi:bi + 1]), inc="act")
            P.op("scalar", lambda e: e.activation(out=bjunk[:, 0:1], in_=bjunk[:, 1:2],
                                                  func=AF.Exp), inc="act")
            av = P.val("act")
            dw = [("act", av), xzv]
            P.op("vector", lambda e: e.tensor_tensor(out=dt_sb[:, :], in0=spA[:, :],
                                                     in1=spB[:, :], op=OP.add),
                 waits=dw, inc="dve")
            dtv = P.op("vector", lambda e, par=par: e.tensor_tensor(
                out=dtx[:, :], in0=dt_sb[:, :], in1=xactb[par][:, :], op=OP.mult), inc="dve")
            dtv_hist.append(dtv)

            for g in range(4):
                gp = g % 2
                n0 = g * 4
                gi = bi * 4 + g
                if gi == 0:
                    emit_bc_loads(0)
                    emit_bc_loads(1)
                bv = (f"br{gp}", br_vals[gi])
                cvv = (f"cr{gp}", cr_vals[gi])
                acts = []
                for n in range(4):
                    aw2 = []
                    if n == 0:
                        aw2.append(("dve", dtv))
                        if len(scan_hist) >= 2:
                            aw2.append(("dve", scan_hist[-2]))
                    a = P.op("scalar", lambda e, gp=gp, n=n, bi=bi, n0=n0: e.activation(
                        out=dAg[gp][:, n * TQ:(n + 1) * TQ], in_=dt_sb[:, :], func=AF.Exp,
                        scale=a_sc[:, bi * 16 + n0 + n: bi * 16 + n0 + n + 1]),
                        waits=aw2, inc="act")
                    acts.append(a)
                dA_hist.append(acts[-1])
                uv = P.op("vector", lambda e, gp=gp: e.tensor_tensor(
                    out=ug[:, :].rearrange("p (n t) -> p n t", n=4),
                    in0=dtx[:, :].unsqueeze(1).broadcast_to([128, 4, TQ]),
                    in1=brep[gp][:, :].rearrange("p (n t) -> p n t", n=4), op=OP.mult),
                    waits=[bv], inc="dve")
                u_hist.append(uv)
                sv = None
                for n in range(4):
                    sv = P.op("vector", lambda e, gp=gp, n=n, bi=bi, n0=n0: e.tensor_tensor_scan(
                        out=hg[:, n * TQ:(n + 1) * TQ], data0=dAg[gp][:, n * TQ:(n + 1) * TQ],
                        data1=ug[:, n * TQ:(n + 1) * TQ],
                        initial=state[:, bi * 16 + n0 + n: bi * 16 + n0 + n + 1],
                        op0=OP.mult, op1=OP.add),
                        waits=[("act", acts[n])], inc="dve")
                scan_hist.append(sv)
                P.op("vector", lambda e, bi=bi, n0=n0: e.tensor_copy(
                    out=state[:, bi * 16 + n0: bi * 16 + n0 + 4],
                    in_=hg[:, TQ - 1:4 * TQ:TQ]), inc="dve")
                pw2 = [cvv]
                if t1_hist:
                    pw2.append(("gp", t1_hist[-1]))
                pvv = P.op("vector", lambda e, gp=gp: e.tensor_tensor(
                    out=pg[:, :], in0=hg[:, :], in1=crep[gp][:, :], op=OP.mult),
                    waits=pw2, inc="dve")
                p_hist.append(pvv)
                if gi + 2 < 128:
                    emit_bc_loads(gi + 2)
                if debug and q == 0 and bi == 0:
                    sv1 = P.dma("sync", bass.AP(dbg_hg, g * 4 * TQ, [[16 * TQ, 128], [1, 4 * TQ]]),
                                hg[:, :], waits=[("dve", pvv)], sem="sdbg")
                    sv1 = P.dma("sync", bass.AP(dbg_ug, g * 4 * TQ, [[16 * TQ, 128], [1, 4 * TQ]]),
                                ug[:, :], sem="sdbg")
                    sv1 = P.dma("sync", bass.AP(dbg_dA, g * 4 * TQ, [[16 * TQ, 128], [1, 4 * TQ]]),
                                dAg[gp][:, :], sem="sdbg")
                    sv1 = P.dma("sync", bass.AP(dbg_pg, g * 4 * TQ, [[16 * TQ, 128], [1, 4 * TQ]]),
                                pg[:, :], sem="sdbg")
                    if g == 0:
                        sv1 = P.dma("sync", dbg_dt.ap(), dt_sb[:, :], sem="sdbg")
                    P.op("vector", lambda e: e.nop(), waits=[("sdbg", P.val("sdbg"))])
                t1v = None
                for h2 in range(2):
                    t1v = P.op("gpsimd", lambda e, h2=h2: e.tensor_tensor(
                        out=t1[:, h2 * TQ:(h2 + 1) * TQ],
                        in0=pg[:, h2 * TQ:(h2 + 1) * TQ],
                        in1=pg[:, (2 + h2) * TQ:(3 + h2) * TQ], op=OP.add),
                        waits=[("dve", pvv)] if h2 == 0 else (), inc="gp")
                t1_hist.append(t1v)
                if g == 0:
                    tv = P.op("gpsimd", lambda e: e.tensor_tensor(
                        out=yb[0][:, :], in0=t1[:, 0:TQ], in1=t1[:, TQ:2 * TQ],
                        op=OP.add), inc="gp")
                else:
                    P.op("gpsimd", lambda e: e.tensor_tensor(
                        out=t2[:, :], in0=t1[:, 0:TQ], in1=t1[:, TQ:2 * TQ],
                        op=OP.add), inc="gp")
                    yw = [("dve", t3_hist[-1])] if (g == 1 and t3_hist) else []
                    tv = P.op("gpsimd", lambda e, g=g: e.tensor_tensor(
                        out=yb[g % 2][:, :], in0=yb[(g - 1) % 2][:, :], in1=t2[:, :],
                        op=OP.add), waits=yw, inc="gp")
                if debug and q == 0 and bi == 0:
                    P.dma("sync", bass.AP(dbg_t1, g * 2 * TQ, [[8 * TQ, 128], [1, 2 * TQ]]),
                                t1[:, :], waits=[("dve", tv)], sem="sdbg")
                    P.dma("sync", bass.AP(dbg_yb, g * TQ, [[4 * TQ, 128], [1, TQ]]),
                                yb[g % 2][:, :], sem="sdbg")
                    P.op("vector", lambda e: e.nop(), waits=[("sdbg", P.val("sdbg"))])
            if debug and q == 0 and bi == 0:
                dvv = P.val("dve")
                avv = P.val("act")
                for g in range(4):
                    pass
            gw = [("gp", tv)]
            if len(store_hist) >= 2:
                gw.append((f"gst{par}", P.val(f"gst{par}")))
            t3v = P.op("vector", lambda e, par=par, bi=bi: e.scalar_tensor_tensor(
                out=t3[:, :], in0=xactb[par][:, :], scalar=dvec[:, bi:bi + 1],
                in1=yb[1][:, :], op0=OP.mult, op1=OP.add), waits=gw, inc="dve")
            t3_hist.append(t3v)
            gv = P.op("vector", lambda e, par=par: e.tensor_tensor(
                out=gat[par][:, :], in0=t3[:, :], in1=szb[par][:, :], op=OP.mult), inc="dve")
            gate_hist.append(gv)
            sv2 = P.dma("sync", bass.AP(yg_scr, q * DI * TQ + bi * 128 * TQ,
                                        [[TQ, 128], [1, TQ]]),
                        gat[par][:, :], waits=[("dve", gv)], sem=f"gst{par}")
            store_hist.append(sv2)
        P.op("sync", lambda e: e.nop(),
             waits=[("gst0", P.val("gst0")), ("gst1", P.val("gst1"))])
        P.flush()
        st.close()

    def stage_c(q):
        stO = ExitStack()
        r1T = sb("c_r1T", (128, 16 * TQ), BF16, stO)
        if q == 0:
            P.op("sync", lambda e: e.nop(),
                 waits=[("dqs0", P.val("dqs0")), ("dqs1", P.val("dqs1"))])

        st1 = ExitStack()
        gtq = sb("c_gt", (128, 32 * TH), BF16, st1)
        wos = sb("c_wos", (128, 32 * 256), BF16, st1)
        oat = sb("c_oat", (128, 4 * 2048), BF16, st1)
        x0f = [sb(f"c_x0{i}", (128, 2048), F32, st1) for i in range(1)]
        rr = [sb(f"c_rr{i}", (128, 2048), F32, st1) for i in range(2)]
        r1f = [sb(f"c_r1f{i}", (128, 2048), F32, st1) for i in range(2)]
        r1b = [sb(f"c_r1b{i}", (128, 2048), BF16, st1) for i in range(2)]
        sbn = sb("c_sbn", (128, 96), F32, st1)
        cjunk = sb("c_junk", (128, 1032), F32, st1)
        mvv = sb("c_mv", (128, 8), F32, st1)
        rstd = sb("c_rstd", (128, 4), F32, st1)
        lnt = sb("c_lnt", (128, 4), F32, st1)
        musc = sb("c_musc", (128, 4), F32, st1)

        opmm_hist = []
        rr_hist = []
        ap_hist_st = []
        ap_hist_tr = []
        for half in range(2):
            c0 = half * TH
            gw = [("pe", opmm_hist[-1])] if opmm_hist else []
            P.dma("sync", gtq[:, :],
                  bass.AP(yg_scr, q * DI * TQ + c0, [[TQ, 128], [128 * TQ, 32], [1, TH]]),
                  waits=gw, sem="sgt")
            gv = ("sgt", P.val("sgt"))
            evh = []
            for cs in range(8):
                ww = [("pe", opmm_hist[-1])] if opmm_hist else []
                P.dma("sync", wos[:, :],
                      bass.AP(wout_bf, cs * 128 * 32 * 256, [[8192, 128], [1, 8192]]),
                      waits=ww, sem="swos")
                wv = ("swos", P.val("swos"))
                for mt in range(4):
                    par = mt % 2

                    def opmm(e, mt=mt, par=par):
                        for ki in range(32):
                            inst = e.matmul(pC[par][:, 0:256],
                                            gtq[:, ki * TH + mt * 128: ki * TH + mt * 128 + 128],
                                            wos[:, ki * 256:(ki + 1) * 256],
                                            start=(ki == 0), stop=(ki == 31))
                        return inst
                    waits = [wv, gv]
                    if len(evh) >= 2:
                        waits.append(("fC", evh[-2]))
                    pv = P.op("tensor", opmm, waits=waits, inc="pe")
                    opmm_hist.append(pv)
                    e1 = P.op("vector", lambda e, mt=mt, par=par, cs=cs: e.tensor_copy(
                        out=oat[:, mt * 2048 + cs * 256: mt * 2048 + (cs + 1) * 256],
                        in_=pC[par][:, 0:256]), waits=[("pe", pv)], inc="fC")
                    evh.append(e1)
            opev = evh[-1]
            if debug and q == 0:
                P.dma("sync", bass.AP(dbg_oat, half * 128 * 8192, [[8192, 128], [1, 8192]]),
                      oat[:, :], waits=[("fC", opev)], sem="sdbg")
            for pr in range(2):
                mts = [2 * pr, 2 * pr + 1]
                for mt in mts:
                    par = mt % 2
                    row = q * TQ + c0 + mt * 128
                    lw = [("dve", rr_hist[-1])] if len(rr_hist) >= 1 else []
                    P.dma("sync", x0f[0][:, :], x0[row:row + 128, :], waits=lw, sem="sx0f")
                    rv = P.op("vector", lambda e, mt=mt, par=par: e.tensor_tensor(
                        out=rr[par][:, :], in0=x0f[0][:, :],
                        in1=oat[:, mt * 2048:(mt + 1) * 2048], op=OP.add),
                        waits=[("sx0f", P.val("sx0f")), ("fC", opev)], inc="dve")
                    for s4 in range(4):
                        P.op("vector", lambda e, mt=mt, par=par, s4=s4: e.bn_stats(
                            out=sbn[:, mt * 24 + s4 * 6: mt * 24 + (s4 + 1) * 6],
                            in_=rr[par][:, s4 * 512:(s4 + 1) * 512]), inc="dve")
                    rr_hist.append(rv)
                for _ in range(2):
                    P.op("vector", lambda e: e.tensor_copy(out=cjunk[:, 8:520], in_=cjunk[:, 520:1032]),
                         inc="dve")
                stats_done = P.val("dve")
                if debug and q == 1 and half == 0 and pr == 0:
                    P.dma("sync", bass.AP(dbg_rr, 0, [[2048, 128], [1, 2048]]),
                          rr[0][:, :], waits=[("dve", stats_done)], sem="sdbg")
                    P.dma("sync", bass.AP(dbg_rr, 128 * 2048, [[2048, 128], [1, 2048]]),
                          rr[1][:, :], sem="sdbg")
                    P.dma("sync", dbg_sbn.ap(), sbn[:, :], sem="sdbg")
                    P.op("vector", lambda e: e.nop(), waits=[("sdbg", P.val("sdbg"))])
                for mt in mts:
                    P.op("vector", lambda e, mt=mt: e.bn_aggr(
                        out=mvv[:, mt * 2:(mt + 1) * 2], in_=sbn[:, mt * 24:(mt + 1) * 24]),
                        waits=[("dve", stats_done)] if mt == mts[0] else (), inc="dve")
                for _ in range(2):
                    P.op("vector", lambda e: e.tensor_copy(out=cjunk[:, 8:520], in_=cjunk[:, 520:1032]),
                         inc="dve")
                P.op("vector", lambda e, pr=pr: e.tensor_scalar(
                    out=lnt[:, pr * 2:pr * 2 + 2], in0=mvv[:, 4 * pr + 1: 4 * pr + 4: 2],
                    scalar1=EPS, scalar2=None, op0=OP.add), inc="dve")
                for _ in range(2):
                    P.op("vector", lambda e: e.tensor_copy(out=cjunk[:, 8:520], in_=cjunk[:, 520:1032]),
                         inc="dve")
                bv = P.val("dve")
                P.op("scalar", lambda e, pr=pr: e.activation(
                    out=lnt[:, pr * 2:pr * 2 + 2], in_=lnt[:, pr * 2:pr * 2 + 2], func=AF.Ln),
                    waits=[("dve", bv)], inc="act")
                for _ in range(2):
                    P.op("scalar", lambda e: e.activation(out=cjunk[:, 8:520], in_=cjunk[:, 520:1032],
                                                          func=AF.Relu), inc="act")
                P.op("scalar", lambda e, pr=pr: e.activation(
                    out=rstd[:, pr * 2:pr * 2 + 2], in_=lnt[:, pr * 2:pr * 2 + 2], func=AF.Exp,
                    scale=-0.5), inc="act")
                for _ in range(2):
                    P.op("scalar", lambda e: e.activation(out=cjunk[:, 8:520], in_=cjunk[:, 520:1032],
                                                          func=AF.Relu), inc="act")
                agg_done = P.val("act")
                P.op("vector", lambda e, pr=pr: e.tensor_tensor(
                    out=musc[:, pr * 2:pr * 2 + 2], in0=mvv[:, 4 * pr: 4 * pr + 3: 2],
                    in1=rstd[:, pr * 2:pr * 2 + 2], op=OP.mult),
                    waits=[("act", agg_done)], inc="ln")
                for _ in range(2):
                    P.op("vector", lambda e: e.tensor_copy(out=cjunk[:, 8:520], in_=cjunk[:, 520:1032]),
                         inc="dve")
                lnv = P.val("ln")
                if debug and q == 1 and half == 0:
                    P.dma("sync", bass.AP(dbg_mv, pr * 128 * 16, [[16, 128], [1, 8]]),
                          mvv[:, :], waits=[("ln", P.val("ln"))], sem="sdbg")
                    P.dma("sync", bass.AP(dbg_mv, pr * 128 * 16 + 8, [[16, 128], [1, 4]]),
                          rstd[:, :], sem="sdbg")
                    P.dma("sync", bass.AP(dbg_mv, pr * 128 * 16 + 12, [[16, 128], [1, 4]]),
                          musc[:, :], sem="sdbg")
                if debug and q == 0 and pr == 1 and False:
                    P.dma("sync", bass.AP(dbg_mv, half * 128 * 16, [[16, 128], [1, 8]]),
                          mvv[:, :], waits=[("ln", lnv)], sem="sdbg")
                    P.dma("sync", bass.AP(dbg_mv, half * 128 * 16 + 8, [[16, 128], [1, 4]]),
                          rstd[:, :], waits=[("ln", lnv)], sem="sdbg")
                    P.dma("sync", bass.AP(dbg_mv, half * 128 * 16 + 12, [[16, 128], [1, 4]]),
                          musc[:, :], waits=[("ln", lnv)], sem="sdbg")
                for mt in mts:
                    par = mt % 2
                    row = q * TQ + c0 + mt * 128
                    aw = [("ln", lnv)] if mt == mts[0] else []
                    if len(ap_hist_st) >= 2:
                        aw.append((f"sr1{par}", P.val(f"sr1{par}")))
                    P.op("vector", lambda e, mt=mt, par=par: e.tensor_tensor(
                        out=r1f[par][:, :], in0=rr[par][:, :],
                        in1=rstd[:, mt:mt + 1].broadcast_to([128, 2048]), op=OP.mult),
                        waits=aw, inc="dve")
                    a1 = P.op("vector", lambda e, mt=mt, par=par: e.tensor_tensor(
                        out=r1f[par][:, :], in0=r1f[par][:, :],
                        in1=musc[:, mt:mt + 1].broadcast_to([128, 2048]), op=OP.subtract),
                        waits=(), inc="dve")
                    a2 = P.op("vector", lambda e, par=par: e.tensor_copy(
                        out=r1b[par][:, :], in_=r1f[par][:, :]),
                        waits=[(f"tra{par}", P.val(f"tra{par}"))] if len(ap_hist_tr) >= 2 else (),
                        inc="dve")
                    ap_hist_st.append(P.dma("sync", r1_scr[row:row + 128, :], r1f[par][:, :],
                                            waits=[("dve", a2)], sem=f"sr1{par}"))
                    for ki in range(16):
                        def trf(e, ki=ki, par=par, c0=c0, mt=mt):
                            return e.dma_start_transpose(
                                r1T[:, ki * TQ + c0 + mt * 128: ki * TQ + c0 + mt * 128 + 128],
                                r1b[par][:, ki * 128:(ki + 1) * 128])
                        P.op("scalar", trf, waits=[("dve", a2)] if ki == 0 else (),
                             inc=f"tra{par}", inc_amt=16)
                    ap_hist_tr.append(1)
        P.op("sync", lambda e: e.nop(),
             waits=[("sr10", P.val("sr10")), ("sr11", P.val("sr11")),
                    ("tra0", P.val("tra0")), ("tra1", P.val("tra1"))])
        P.flush()
        st1.close()
        r1T_ready = [("tra0", P.val("tra0")), ("tra1", P.val("tra1"))]

        st2 = ExitStack()
        h2sb = sb("c_h2", (128, 64 * TH), BF16, st2)
        wfb = [sb(f"c_wf{i}", (128, 16 * 128), BF16, st2) for i in range(2)]
        wdb = [sb(f"c_wd{i}", (128, 16 * 128), BF16, st2) for i in range(2)]
        xA = sb("c_xA", (16, TH), BF16, st2)
        xA2 = sb("c_xA2", (16, TH), BF16, st2)
        dcol = [sb(f"c_dc{i}", (128, 512), BF16, st2) for i in range(2)]
        dtok = sb("c_dtok", (128, 4 * 2048), BF16, st2)
        r1rd = [sb(f"c_r1r{i}", (128, 2048), F32, st2) for i in range(1)]
        r2f = [sb(f"c_r2{i}", (128, 2048), F32, st2) for i in range(2)]
        outf = [sb(f"c_of{i}", (128, 2048), F32, st2) for i in range(1)]
        sbn2 = sb("c_sbn2", (128, 96), F32, st2)
        cjunk2 = sb("c_junk2", (128, 520), F32, st2)
        mv2 = sb("c_mv2", (128, 8), F32, st2)
        rstd2 = sb("c_rstd2", (128, 4), F32, st2)
        lnt2 = sb("c_lnt2", (128, 4), F32, st2)
        musc2 = sb("c_musc2", (128, 4), F32, st2)

        dn_pv_hist = []
        r2_hist = []
        ost_hist = []
        for half in range(2):
            c0 = half * TH

            def xamm(e, c0=c0):
                for ki in range(16):
                    inst = e.matmul(pU[0][0:16, :], aupT[:, ki * 16:(ki + 1) * 16],
                                    r1T[:, ki * TQ + c0: ki * TQ + c0 + TH],
                                    start=(ki == 0), stop=(ki == 15))
                return inst
            xw = r1T_ready + [("fU", P.val("fU"))]
            pv = P.op("tensor", xamm, waits=xw, inc="pe")
            xev = P.op("vector", lambda e: e.tensor_copy(out=xA[:, :], in_=pU[0][0:16, :]),
                       waits=[("pe", pv)], inc="fU")
            up_mm_hist = []
            up_ev_hist = []
            for mi in range(64):
                par = mi % 2
                ww = [("pe", up_mm_hist[-2])] if len(up_mm_hist) >= 2 else []
                P.dma("sync", wfb[par][:, :],
                      bass.AP(wup_bf, mi * 128 * 2048, [[2048, 128], [1, 2048]]), waits=ww,
                      sem=f"swf{par}")
                wv = (f"swf{par}", P.val(f"swf{par}"))

                def upmm(e, mi=mi, par=par, c0=c0):
                    for ki in range(16):
                        e.matmul(pU[par][:, :], wfb[par][:, ki * 128:(ki + 1) * 128],
                                 r1T[:, ki * TQ + c0: ki * TQ + c0 + TH],
                                 start=(ki == 0), stop=False)
                    return e.matmul(pU[par][:, :], bupL[:, mi * 128:(mi + 1) * 128],
                                    xA[0:16, :], start=False, stop=True)
                waits = [wv, ("fU", xev)]
                if len(up_ev_hist) >= 2:
                    waits.append(("fU", up_ev_hist[-2]))
                if half > 0 and mi == 0 and dn_pv_hist:
                    waits.append(("pe", dn_pv_hist[-1]))
                pv = P.op("tensor", upmm, waits=waits, inc="pe")
                up_mm_hist.append(pv)
                e1 = P.op("scalar", lambda e, mi=mi, par=par: e.activation(
                    out=h2sb[:, mi * TH:(mi + 1) * TH], in_=pU[par][:, :], func=AF.Relu,
                    bias=bupT[:, mi:mi + 1]), waits=[("pe", pv)], inc="fU")
                up_ev_hist.append(e1)
            upv = up_ev_hist[-1]

            def xa2mm(e):
                for ki in range(64):
                    inst = e.matmul(pC[0][0:16, :], adnT[:, ki * 16:(ki + 1) * 16],
                                    h2sb[:, ki * TH:(ki + 1) * TH],
                                    start=(ki == 0), stop=(ki == 63))
                return inst
            pv = P.op("tensor", xa2mm, waits=[("fU", upv), ("fC", P.val("fC"))], inc="pe")
            xa2v = P.op("vector", lambda e: e.tensor_copy(out=xA2[:, :], in_=pC[0][0:16, :]),
                        waits=[("pe", pv)], inc="fC")
            dn_ev_hist = []
            dn_tr_hist = []
            for mi in range(16):
                par = mi % 2
                for kg in range(4):
                    ww = [("pe", dn_pv_hist[-2])] if len(dn_pv_hist) >= 2 else []
                    P.dma("sync", wdb[kg % 2][:, :],
                          bass.AP(wdn_bf, mi * 128 * 8192 + kg * 16 * 128,
                                  [[8192, 128], [1, 2048]]), waits=ww,
                          sem=f"swd{kg % 2}")
                    wv = (f"swd{kg % 2}", P.val(f"swd{kg % 2}"))

                    def dnmm(e, mi=mi, par=par, kg=kg):
                        for k2 in range(16):
                            ki = kg * 16 + k2
                            inst = e.matmul(pC[par][:, :],
                                            wdb[kg % 2][:, k2 * 128:(k2 + 1) * 128],
                                            h2sb[:, ki * TH:(ki + 1) * TH],
                                            start=(ki == 0), stop=False)
                        return inst
                    waits = [wv, ("fC", xa2v)]
                    if len(dn_ev_hist) >= 2 and kg == 0:
                        waits.append(("fC", dn_ev_hist[-2]))
                    pvk = P.op("tensor", dnmm, waits=waits, inc="pe")
                    dn_pv_hist.append(pvk)

                def dnlora(e, mi=mi, par=par):
                    return e.matmul(pC[par][:, :], bdnL[:, mi * 128:(mi + 1) * 128],
                                    xA2[0:16, :], start=False, stop=True)
                pv = P.op("tensor", dnlora, inc="pe")
                dn_pv_hist.append(pv)
                aw = [("pe", pv)]
                if P.val(f"trd{par}") > 0:
                    aw.append((f"trd{par}", P.val(f"trd{par}")))
                if half > 0 and mi == 0 and r2_hist:
                    aw.append(("dve", r2_hist[-1]))
                av = P.op("vector", lambda e, mi=mi, par=par: e.tensor_scalar(
                    out=dcol[par][:, :], in0=pC[par][:, :], scalar1=bdnT[:, mi:mi + 1],
                    scalar2=None, op0=OP.add), waits=aw, inc="fC")
                dn_ev_hist.append(av)
                for mtl in range(4):
                    def trf(e, mi=mi, par=par, mtl=mtl):
                        return e.dma_start_transpose(
                            dtok[:, mtl * 2048 + mi * 128: mtl * 2048 + (mi + 1) * 128],
                            dcol[par][:, mtl * 128:(mtl + 1) * 128])
                    P.op("scalar", trf, waits=[("fC", av)] if mtl == 0 else (),
                         inc=f"trd{par}", inc_amt=16)
                dn_tr_hist.append(1)
            dnv = [("trd0", P.val("trd0")), ("trd1", P.val("trd1"))]
            for pr in range(2):
                mtls = [2 * pr, 2 * pr + 1]
                for mtl in mtls:
                    par = mtl % 2
                    row = q * TQ + c0 + mtl * 128
                    lw = [("dve", r2_hist[-1])] if len(r2_hist) >= 1 else []
                    P.dma("sync", r1rd[0][:, :], r1_scr[row:row + 128, :], waits=lw, sem="sr1r")
                    vv = P.op("vector", lambda e, mtl=mtl, par=par: e.tensor_tensor(
                        out=r2f[par][:, :], in0=r1rd[0][:, :],
                        in1=dtok[:, mtl * 2048:(mtl + 1) * 2048], op=OP.add),
                        waits=[("sr1r", P.val("sr1r"))] + dnv, inc="dve")
                    for s4 in range(4):
                        P.op("vector", lambda e, mtl=mtl, par=par, s4=s4: e.bn_stats(
                            out=sbn2[:, mtl * 24 + s4 * 6: mtl * 24 + (s4 + 1) * 6],
                            in_=r2f[par][:, s4 * 512:(s4 + 1) * 512]), inc="dve")
                    r2_hist.append(vv)
                for _ in range(2):
                    P.op("vector", lambda e: e.tensor_copy(out=cjunk2[:, 8:264], in_=cjunk2[:, 264:520]),
                         inc="dve")
                stats2 = P.val("dve")
                for mtl in mtls:
                    P.op("vector", lambda e, mtl=mtl: e.bn_aggr(
                        out=mv2[:, mtl * 2:(mtl + 1) * 2], in_=sbn2[:, mtl * 24:(mtl + 1) * 24]),
                        waits=[("dve", stats2)] if mtl == mtls[0] else (), inc="dve")
                for _ in range(2):
                    P.op("vector", lambda e: e.tensor_copy(out=cjunk2[:, 8:264], in_=cjunk2[:, 264:520]),
                         inc="dve")
                P.op("vector", lambda e, pr=pr: e.tensor_scalar(
                    out=lnt2[:, pr * 2:pr * 2 + 2], in0=mv2[:, 4 * pr + 1: 4 * pr + 4: 2],
                    scalar1=EPS, scalar2=None, op0=OP.add), inc="dve")
                for _ in range(2):
                    P.op("vector", lambda e: e.tensor_copy(out=cjunk2[:, 8:264], in_=cjunk2[:, 264:520]),
                         inc="dve")
                bv = P.val("dve")
                P.op("scalar", lambda e, pr=pr: e.activation(
                    out=lnt2[:, pr * 2:pr * 2 + 2], in_=lnt2[:, pr * 2:pr * 2 + 2], func=AF.Ln),
                    waits=[("dve", bv)], inc="act")
                for _ in range(2):
                    P.op("scalar", lambda e: e.activation(out=cjunk2[:, 8:264], in_=cjunk2[:, 264:520],
                                                          func=AF.Relu), inc="act")
                P.op("scalar", lambda e, pr=pr: e.activation(
                    out=rstd2[:, pr * 2:pr * 2 + 2], in_=lnt2[:, pr * 2:pr * 2 + 2], func=AF.Exp,
                    scale=-0.5), inc="act")
                for _ in range(2):
                    P.op("scalar", lambda e: e.activation(out=cjunk2[:, 8:264], in_=cjunk2[:, 264:520],
                                                          func=AF.Relu), inc="act")
                agg2 = P.val("act")
                P.op("vector", lambda e, pr=pr: e.tensor_tensor(
                    out=musc2[:, pr * 2:pr * 2 + 2], in0=mv2[:, 4 * pr: 4 * pr + 3: 2],
                    in1=rstd2[:, pr * 2:pr * 2 + 2], op=OP.mult),
                    waits=[("act", agg2)], inc="ln")
                for _ in range(2):
                    P.op("vector", lambda e: e.tensor_copy(out=cjunk2[:, 8:264], in_=cjunk2[:, 264:520]),
                         inc="dve")
                ln2 = P.val("ln")
                for mtl in mtls:
                    par = mtl % 2
                    row = q * TQ + c0 + mtl * 128
                    aw = [("ln", ln2)] if mtl == mtls[0] else []
                    if len(ost_hist) >= 1:
                        aw.append(("sout", P.val("sout")))
                    P.op("vector", lambda e, mtl=mtl, par=par: e.tensor_tensor(
                        out=outf[0][:, :], in0=r2f[par][:, :],
                        in1=rstd2[:, mtl:mtl + 1].broadcast_to([128, 2048]), op=OP.mult),
                        waits=aw, inc="dve")
                    a1 = P.op("vector", lambda e, mtl=mtl, par=par: e.tensor_tensor(
                        out=outf[0][:, :], in0=outf[0][:, :],
                        in1=musc2[:, mtl:mtl + 1].broadcast_to([128, 2048]), op=OP.subtract),
                        waits=(), inc="dve")
                    ost_hist.append(P.dma("sync", out_d[row:row + 128, :], outf[0][:, :],
                                          waits=[("dve", a1)], sem="sout"))
        P.op("sync", lambda e: e.nop(),
             waits=[("sout", P.val("sout")), ("trd0", P.val("trd0")),
                    ("trd1", P.val("trd1"))])
        P.flush()
        st2.close()
        stO.close()

    stage_w()
    stage_dq_win()
    stage_dq_rest()
    for q in range(NQ):
        stage_a(q, with_dq=False)
        stage_b(q)
        stage_c(q)
    FINAL_STORES = (["sout", "sr10", "sr11", "gst0", "gst1", "sdbg", "st"] +
                    [f"sev{i}" for i in range(4)] + ["sxao0", "sxao1", "dqs0", "dqs1"])
    if debug:
        dbg_targets = [("sz_scr", sz_scr, NQ * DI * TQ), ("xact_scr", xact_scr, NQ * DI * TQ),
                       ("b_scr", b_scr, NQ * DXB * TQ), ("c_scr", c_scr, NQ * DI * TQ),
                       ("yg_scr", yg_scr, NQ * DI * TQ), ("dt1_scr", dt1_scr, NQ * 128 * TQ),
                       ("wup_bf", wup_bf, 64 * 16 * 128 * 128),
                       ("hg0", dbg_hg, 128 * 16 * TQ), ("ug0", dbg_ug, 128 * 16 * TQ),
                       ("pg0", dbg_pg, 128 * 16 * TQ), ("t10", dbg_t1, 128 * 8 * TQ),
                       ("oat0", dbg_oat, 2 * 128 * 8192)]
        st = ExitStack()
        db = sb("dbg_b", (128, 4096), BF16, st)
        df = sb("dbg_f", (128, 4096), F32, st)
        for name, t, nelem in dbg_targets:
            o = nc.dram_tensor("dbgo_" + name, [nelem // 4096, 4096], F32, kind="ExternalOutput")
            ntile = nelem // (128 * 4096)
            for i in range(ntile):
                P.dma("sync", db[:, :],
                      bass.AP(t, i * 128 * 4096, [[4096, 128], [1, 4096]]), sem="sdbg",
                      waits=[("sdbg", P.val("sdbg"))])
                cv = P.op("vector", lambda e: e.tensor_copy(out=df[:, :], in_=db[:, :]),
                          waits=[("sdbg", P.val("sdbg"))], inc="dve")
                P.dma("sync", bass.AP(o, i * 128 * 4096, [[4096, 128], [1, 4096]]),
                      df[:, :], waits=[("dve", cv)], sem="sdbg")
        P.op("sync", lambda e: e.nop(),
             waits=[(nm, P.val(nm)) for nm in FINAL_STORES])
        P.flush()
        st.close()
        stack.close()
        return nc
    P.op("sync", lambda e: e.nop(),
         waits=[(nm, P.val(nm)) for nm in FINAL_STORES])
    P.flush()
    stack.close()
    return nc


_NC_CACHE = None


def _get_nc():
    global _NC_CACHE
    if _NC_CACHE is None:
        _NC_CACHE = build_program()
    return _NC_CACHE


def kernel(L0_attn_hidden_states, L0_attn_in_proj_w, L0_attn_dt_in_proj_w,
           L0_attn_dt_proj_w, L0_attn_dt_proj_b, L0_attn_conv_w, L0_attn_conv_b,
           L0_attn_A_log, L0_attn_D, L0_attn_out_proj_w, L0_ffn_w_up_q,
           L0_ffn_w_up_scale, L0_ffn_b_up, L0_ffn_w_up_lora_a, L0_ffn_w_up_lora_b,
           L0_ffn_w_down_q, L0_ffn_w_down_scale, L0_ffn_b_down,
           L0_ffn_w_down_lora_a, L0_ffn_w_down_lora_b):
    f32 = lambda a: np.ascontiguousarray(np.asarray(a, np.float32))
    i32 = lambda a: np.ascontiguousarray(np.asarray(a, np.int32))
    hs = f32(L0_attn_hidden_states)

    Win = f32(L0_attn_in_proj_w)
    W1 = f32(L0_attn_dt_in_proj_w)
    W2 = f32(L0_attn_dt_proj_w)
    Wout = f32(L0_attn_out_proj_w)
    selx = np.zeros((4, 128, 128), np.float32)
    for r in range(4):
        for m in range(128):
            selx[r, 32 * r + 16 * (m // 64) + (m % 16), m] = 1.0
    shared = dict(
        win_t=np.ascontiguousarray(Win.reshape(80, 128, 16, 128).transpose(0, 2, 3, 1)),
        w1_t=np.ascontiguousarray(W1.reshape(128, 16, 128).transpose(1, 2, 0)),
        w2_t=np.ascontiguousarray(W2.reshape(32, 128, 128).transpose(0, 2, 1)),
        wout_t=np.ascontiguousarray(Wout.T.reshape(32, 128, 2048)),
        wq_up=np.ascontiguousarray(i32(L0_ffn_w_up_q).T.reshape(16, 128, 8192)),
        sc_up=np.ascontiguousarray(
            np.repeat(f32(L0_ffn_w_up_scale).T, 64, axis=0).reshape(16, 128, 8192)),
        wq_dn=np.ascontiguousarray(i32(L0_ffn_w_down_q).T.reshape(64, 128, 2048)),
        sc_dn=np.ascontiguousarray(
            np.repeat(f32(L0_ffn_w_down_scale).T, 64, axis=0).reshape(64, 128, 2048)),
        bup_h=np.ascontiguousarray(f32(L0_ffn_b_up).reshape(64, 128).T),
        bdn_h=np.ascontiguousarray(f32(L0_ffn_b_down).reshape(16, 128).T),
        aup_h=np.ascontiguousarray(f32(L0_ffn_w_up_lora_a).reshape(16, 128, 16)),
        bupl_h=f32(L0_ffn_w_up_lora_b),
        adn_h=np.ascontiguousarray(f32(L0_ffn_w_down_lora_a).reshape(64, 128, 16)),
        bdnl_h=f32(L0_ffn_w_down_lora_b),
        dtb_h=np.ascontiguousarray(f32(L0_attn_dt_proj_b).reshape(32, 128).T),
        cw_h=np.ascontiguousarray(f32(L0_attn_conv_w).reshape(32, 128, 4)
                                  .transpose(1, 0, 2).reshape(128, 128)),
        cb_h=np.ascontiguousarray(f32(L0_attn_conv_b).reshape(32, 128).T),
        dvec_h=np.ascontiguousarray(f32(L0_attn_D).reshape(32, 128).T),
        alog_h=np.ascontiguousarray(f32(L0_attn_A_log).reshape(32, 128, 16)
                                    .transpose(1, 0, 2).reshape(128, 512)),
        selx_h=selx,
    )
    nc = _get_nc()
    in_maps = []
    for c in range(8):
        m = dict(shared)
        m["x0"] = np.ascontiguousarray(hs[c])
        in_maps.append(m)
    res = run_bass_kernel_spmd(nc, in_maps, core_ids=list(range(8)))
    out = np.stack([np.asarray(r["out"], np.float32) for r in res.results], axis=0)
    return out


# version 18
# speedup vs baseline: 1.0607x; 1.0607x over previous
import numpy as np
from contextlib import ExitStack

import concourse.bass as bass
import concourse.mybir as mybir
from concourse.bass_utils import run_bass_kernel_spmd

F32 = mybir.dt.float32
BF16 = mybir.dt.bfloat16
I32 = mybir.dt.int32
AF = mybir.ActivationFunctionType
OP = mybir.AluOpType

L = 4096
DM = 2048
DI = 4096
DXB = 1024
NQ = 4
TQ = L // NQ
TH = TQ // 2
NBLK = DI // 128
EPS = 1e-5

ENGINES = ("sync", "tensor", "vector", "scalar", "gpsimd")


class Plan:
    def __init__(self, nc, sems):
        self.nc = nc
        self.sems = sems
        self.count = {k: 0 for k in sems}
        self.ops = {e: [] for e in ENGINES}

    def op(self, eng, fn, waits=(), inc=None, inc_amt=1):
        waits = [(s, v) for (s, v) in waits if v > 0]
        self.ops[eng].append((fn, tuple(waits), inc, inc_amt))
        if inc is not None:
            self.count[inc] += inc_amt
            return self.count[inc]
        return None

    def dma(self, queue, out, in_, waits=(), sem="dma"):
        def fn(e, out=out, in_=in_):
            return e.dma_start(out=out, in_=in_)
        return self.op(queue, fn, waits=waits, inc=sem, inc_amt=16)

    def val(self, sem):
        return self.count[sem]

    def flush(self):
        ops = self.ops
        self.ops = {e: [] for e in ENGINES}
        sems = self.sems
        with self.nc.Block() as blk:
            for ename in ENGINES:
                if not ops[ename]:
                    continue

                def body(eng, _ops=ops[ename]):
                    for fn, waits, inc, amt in _ops:
                        for (s, v) in waits:
                            eng.wait_ge(sems[s], v)
                        inst = fn(eng)
                        if inc is not None:
                            inst.then_inc(sems[inc], amt)

                getattr(blk, ename)(body)


def build_program(debug=False):
    nc = bass.Bass("TRN2", target_bir_lowering=False)

    d = {}
    def din(name, shape, dt):
        d[name] = nc.dram_tensor(name, list(shape), dt, kind="ExternalInput")
        return d[name]

    x0 = din("x0", (L, DM), F32)
    win_t = din("win_t", (80, 16, 128, 128), F32)
    w1_t = din("w1_t", (16, 128, 128), F32)
    w2_t = din("w2_t", (32, 128, 128), F32)
    wout_t = din("wout_t", (32, 128, 2048), F32)
    wq_up = din("wq_up", (16, 128, 8192), I32)
    sc_up = din("sc_up", (16, 128, 8192), F32)
    wq_dn = din("wq_dn", (64, 128, 2048), I32)
    sc_dn = din("sc_dn", (64, 128, 2048), F32)
    bup_h = din("bup_h", (128, 64), F32)
    bdn_h = din("bdn_h", (128, 16), F32)
    aup_h = din("aup_h", (16, 128, 16), F32)
    bupl_h = din("bupl_h", (16, 8192), F32)
    adn_h = din("adn_h", (64, 128, 16), F32)
    bdnl_h = din("bdnl_h", (16, 2048), F32)
    dtb_h = din("dtb_h", (128, 32), F32)
    cw_h = din("cw_h", (128, 128), F32)
    cb_h = din("cb_h", (128, 32), F32)
    dvec_h = din("dvec_h", (128, 32), F32)
    alog_h = din("alog_h", (128, 512), F32)
    selx_h = din("selx_h", (4, 128, 128), F32)

    out_d = nc.dram_tensor("out", [L, DM], F32, kind="ExternalOutput")

    def scr(name, shape, dt):
        if debug and dt == F32:
            return nc.dram_tensor(name, list(shape), dt, kind="ExternalOutput")
        return nc.dram_tensor(name, list(shape), dt)

    win_bf = scr("win_bf", (80, 128, 16, 128), BF16)
    wout_bf = scr("wout_bf", (8, 128, 32, 256), BF16)
    wup_bf = scr("wup_bf", (64, 128, 16, 128), BF16)
    wdn_bf = scr("wdn_bf", (16, 128, 64, 128), BF16)
    sz_scr = scr("sz_scr", (NQ, DI, TQ), BF16)
    xact_scr = scr("xact_scr", (NQ, DI, TQ), BF16)
    b_scr = scr("b_scr", (NQ, DXB, TQ), BF16)
    c_scr = scr("c_scr", (NQ, DI, TQ), BF16)
    yg_scr = scr("yg_scr", (NQ, DI, TQ), BF16)
    r1_scr = scr("r1_scr", (L, DM), F32)
    dt1_scr = scr("dt1_scr", (NQ, 128, TQ), BF16)
    FINAL_STORES = (["sout", "sr10", "sr11", "gst0", "gst1", "sdbg", "st"] +
                    [f"sev{i}" for i in range(4)] + ["sxao0", "sxao1", "dqs0", "dqs1"])
    if debug:
        dbg_hg = nc.dram_tensor("dbg_hg0", [128, 16 * TQ], BF16)
        dbg_ug = nc.dram_tensor("dbg_ug0", [128, 16 * TQ], BF16)
        dbg_pg = nc.dram_tensor("dbg_pg0", [128, 16 * TQ], BF16)
        dbg_t1 = nc.dram_tensor("dbg_t10", [128, 8 * TQ], BF16)
        dbg_yb = nc.dram_tensor("dbg_yb0", [128, 4 * TQ], F32, kind="ExternalOutput")
        dbg_dA = nc.dram_tensor("dbg_dA0", [128, 16 * TQ], F32, kind="ExternalOutput")
        dbg_dt = nc.dram_tensor("dbg_dt0", [128, TQ], F32, kind="ExternalOutput")
        dbg_oat = nc.dram_tensor("dbg_oat0", [2, 128, 4 * 2048], BF16)
        dbg_mv = nc.dram_tensor("dbg_mv0", [2, 128, 16], F32, kind="ExternalOutput")
        dbg_rr = nc.dram_tensor("dbg_rr0", [2, 128, 2048], F32, kind="ExternalOutput")
        dbg_sbn = nc.dram_tensor("dbg_sbn0", [128, 96], F32, kind="ExternalOutput")
    else:
        dbg_hg = dbg_ug = dbg_dA = dbg_dt = dbg_oat = dbg_mv = None

    stack = ExitStack()
    sem_names = (["dma", "tr", "pe", "dve", "act", "fA", "fB", "fC", "fU", "ln", "st", "gp",
                  "sdt1", "sgt", "swos", "sx0f", "sr1r", "sout", "sdbg"] +
                 [f"{p}{i}" for p in ("sx0t", "swb", "sxao", "bxz", "br", "cr", "gst",
                                      "tra", "trd", "sr1", "swf", "swd", "dqf", "dqs")
                  for i in range(2)] + [f"sev{i}" for i in range(4)])
    sems = {n: stack.enter_context(nc.semaphore(n)) for n in sem_names}
    P = Plan(nc, sems)

    _uid = [0]

    def sb(name, shape, dt, st=stack):
        _uid[0] += 1
        return st.enter_context(nc.sbuf_tensor(f"{name}_u{_uid[0]}", list(shape), dt))

    w1T = sb("w1T", (128, 16 * 128), BF16)
    w2T = sb("w2T", (128, 32 * 128), BF16)
    a_sc = sb("a_sc", (128, 512), F32)
    state = sb("state", (128, 512), F32)
    cw = sb("cw", (128, 128), F32)
    cb = sb("cb", (128, 32), F32)
    dvec = sb("dvec", (128, 32), F32)
    dtb = sb("dtb", (128, 32), F32)
    bupT = sb("bupT", (128, 64), F32)
    bdnT = sb("bdnT", (128, 16), F32)
    aupT = sb("aupT", (128, 16 * 16), BF16)
    bupL = sb("bupL", (16, 8192), BF16)
    adnT = sb("adnT", (128, 64 * 16), BF16)
    bdnL = sb("bdnL", (16, 2048), BF16)
    selx = sb("selx", (128, 4 * 128), BF16)
    halo = sb("halo", (128, 256), BF16)
    pA = [stack.enter_context(nc.psum_tensor(f"pA{i}", [128, 512], F32)) for i in range(2)]
    pB = [stack.enter_context(nc.psum_tensor(f"pB{i}", [128, 512], F32)) for i in range(2)]
    pC = [stack.enter_context(nc.psum_tensor(f"pC{i}", [128, 512], F32)) for i in range(2)]
    pU = [stack.enter_context(nc.psum_tensor(f"pU{i}", [128, 512], F32)) for i in range(2)]

    def stage_w():
        st = ExitStack()
        t1 = sb("w_t1", (128, 2048), F32, st)
        t2 = sb("w_t2", (128, 4096), F32, st)
        t3 = sb("w_t3", (128, 512), F32, st)
        t4 = sb("w_t4", (128, 1280), F32, st)
        t5 = sb("w_t5", (128, 512), F32, st)
        t6 = sb("w_t6", (16, 8192), F32, st)
        t7 = sb("w_t7", (16, 2048), F32, st)
        P.dma("sync", t1[:, :], bass.AP(w1_t, 0, [[128, 128], [128 * 128, 16], [1, 128]]))
        P.dma("sync", t2[:, :], bass.AP(w2_t, 0, [[128, 128], [128 * 128, 32], [1, 128]]))
        P.dma("sync", t3[:, :], alog_h.ap())
        P.dma("sync", t4[:, 0:256], bass.AP(aup_h, 0, [[16, 128], [128 * 16, 16], [1, 16]]))
        P.dma("sync", t4[:, 256:1280], bass.AP(adn_h, 0, [[16, 128], [128 * 16, 64], [1, 16]]))
        P.dma("sync", t5[:, :], bass.AP(selx_h, 0, [[128, 128], [128 * 128, 4], [1, 128]]))
        P.dma("sync", t6[:, :], bupl_h.ap())
        P.dma("sync", t7[:, :], bdnl_h.ap())
        for (dst, src) in [(cw, cw_h), (cb, cb_h), (dvec, dvec_h), (dtb, dtb_h),
                           (bupT, bup_h), (bdnT, bdn_h)]:
            P.dma("sync", dst[:, :], src.ap())
        v = P.val("dma")
        P.op("vector", lambda e: e.tensor_copy(out=w1T[:, :], in_=t1[:, :]), waits=[("dma", v)])
        P.op("vector", lambda e: e.tensor_copy(out=w2T[:, :], in_=t2[:, :]))
        P.op("vector", lambda e: e.tensor_copy(out=aupT[:, :], in_=t4[:, 0:256]))
        P.op("vector", lambda e: e.tensor_copy(out=adnT[:, :], in_=t4[:, 256:1280]))
        P.op("vector", lambda e: e.tensor_copy(out=selx[:, :], in_=t5[:, :]))
        P.op("vector", lambda e: e.tensor_copy(out=bupL[:, :], in_=t6[:, :]))
        P.op("vector", lambda e: e.tensor_copy(out=bdnL[:, :], in_=t7[:, :]))
        P.op("vector", lambda e: e.memset(state[:, :], 0.0))
        P.op("vector", lambda e: e.memset(halo[:, :], 0.0))
        av = P.op("scalar", lambda e: e.activation(out=a_sc[:, :], in_=t3[:, :], func=AF.Exp),
                  waits=[("dma", v)], inc="act")
        P.op("vector", lambda e: e.tensor_scalar(out=a_sc[:, :], in0=a_sc[:, :], scalar1=-1.0,
                                                 scalar2=None, op0=OP.mult),
             waits=[("act", av)])
        P.flush()
        st.close()

    class DqJobs:
        def __init__(self, buf_f, buf_b, qb, sc):
            self.buf_f, self.buf_b, self.qb, self.sc = buf_f, buf_b, qb, sc
            self.cast = {0: [], 1: []}
            self.store = {0: [], 1: []}
            self.n = 0

        def emit(self, loads, store_ap, quant, extra_lw=()):
            bi = self.n % 2
            self.n += 1
            lw = list(extra_lw)
            prev = self.cast[bi][-1] if self.cast[bi] else 0
            if prev:
                lw.append(("fA", prev))
            for i, (dst, src) in enumerate(loads(bi)):
                P.dma("sync", dst, src, waits=(lw if i == 0 else ()), sem=f"dqf{bi}")
            ldv = [(f"dqf{bi}", P.val(f"dqf{bi}"))]
            sw = [(f"dqs{bi}", self.store[bi][-1])] if self.store[bi] else []
            if quant:
                av = P.op("scalar", lambda e, bi=bi: e.activation(
                    out=self.buf_f[bi][:, :], in_=self.qb[bi][:, :], func=AF.Copy,
                    bias=-7.5), waits=ldv + ([("fA", prev)] if prev else []), inc="act")
                cv = P.op("vector", lambda e, bi=bi: e.tensor_tensor(
                    out=self.buf_b[bi][:, :], in0=self.buf_f[bi][:, :],
                    in1=self.sc[bi][:, :], op=OP.mult), waits=[("act", av)] + sw,
                    inc="fA")
            else:
                cv = P.op("vector", lambda e, bi=bi: e.tensor_copy(
                    out=self.buf_b[bi][:, :], in_=self.buf_f[bi][:, :]),
                    waits=ldv + sw, inc="fA")
            self.cast[bi].append(cv)
            sv = P.dma("gpsimd", store_ap, self.buf_b[bi][:, :], waits=[("fA", cv)],
                       sem=f"dqs{bi}")
            self.store[bi].append(sv)

    def dq_rest_jobs(dqj):
        jobs = []
        for ki in range(32):
            jobs.append(lambda lw=(), ki=ki: dqj.emit(
                lambda bi, ki=ki: [(dqj.buf_f[bi][:, :],
                                    bass.AP(wout_t, ki * 128 * 2048,
                                            [[2048, 128], [1, 2048]]))],
                bass.AP(wout_bf, ki * 256,
                        [[32 * 256, 128], [128 * 32 * 256, 8], [1, 256]]),
                quant=False, extra_lw=lw))
        for ki in range(16):
            for c4 in range(4):
                jobs.append(lambda lw=(), ki=ki, c4=c4: dqj.emit(
                    lambda bi, ki=ki, c4=c4: [
                        (dqj.qb[bi][:, :],
                         bass.AP(wq_up, ki * 128 * 8192 + c4 * 2048,
                                 [[8192, 128], [1, 2048]])),
                        (dqj.sc[bi][:, :],
                         bass.AP(sc_up, ki * 128 * 8192 + c4 * 2048,
                                 [[8192, 128], [1, 2048]]))],
                    bass.AP(wup_bf, (c4 * 16) * 128 * 2048 + ki * 128,
                            [[2048, 128], [128 * 2048, 16], [1, 128]]),
                    quant=True, extra_lw=lw))
        for ki in range(64):
            jobs.append(lambda lw=(), ki=ki: dqj.emit(
                lambda bi, ki=ki: [
                    (dqj.qb[bi][:, :], wq_dn[ki, :, :]),
                    (dqj.sc[bi][:, :], sc_dn[ki, :, :])],
                bass.AP(wdn_bf, ki * 128,
                        [[64 * 128, 128], [128 * 64 * 128, 16], [1, 128]]),
                quant=True, extra_lw=lw))
        return jobs

    def stage_dq_rest():
        st = ExitStack()
        buf_f = [sb(f"dqr_f{i}", (128, 2048), F32, st) for i in range(2)]
        buf_b = [sb(f"dqr_b{i}", (128, 2048), BF16, st) for i in range(2)]
        qb = [sb(f"dqr_q{i}", (128, 2048), I32, st) for i in range(2)]
        sc = [sb(f"dqr_s{i}", (128, 2048), F32, st) for i in range(2)]
        for job in dq_rest_jobs(DqJobs(buf_f, buf_b, qb, sc)):
            job()
        P.op("sync", lambda e: e.nop(),
             waits=[("dqs0", P.val("dqs0")), ("dqs1", P.val("dqs1"))])
        P.flush()
        st.close()

    def stage_dq_win():
        st = ExitStack()
        buf_f = [sb(f"dq_f{i}", (128, 2048), F32, st) for i in range(2)]
        buf_b = [sb(f"dq_b{i}", (128, 2048), BF16, st) for i in range(2)]
        dqj = DqJobs(buf_f, buf_b, None, None)
        for mi in range(80):
            dqj.emit(
                lambda bi, mi=mi: [(buf_f[bi][:, :],
                                    bass.AP(win_t, mi * 16 * 128 * 128,
                                            [[128, 128], [128 * 128, 16], [1, 128]]))],
                bass.AP(win_bf, mi * 128 * 2048, [[2048, 128], [1, 2048]]),
                quant=False)
        P.op("sync", lambda e: e.nop(),
             waits=[("dqs0", P.val("dqs0")), ("dqs1", P.val("dqs1"))])
        P.flush()
        st.close()

    def stage_a(q, with_dq=False):
        st = ExitStack()
        x0t = [sb(f"a_x0t{i}", (128, 2048), F32, st) for i in range(2)]
        xtm = [sb(f"a_xtm{i}", (128, 2048), BF16, st) for i in range(2)]
        xb = sb("a_xb", (128, 16 * TQ), BF16, st)
        wbuf = [sb(f"a_w{i}", (128, 4096), BF16, st) for i in range(2)]
        ev = [sb(f"a_ev{i}", (128, 512), BF16, st) for i in range(4)]
        xsm = sb("a_xsm", (128, 8 * TQ), BF16, st)
        xfe = [sb(f"a_xfe{i}", (128, TH + 8), BF16, st) for i in range(2)]
        ctmp = [sb(f"a_ct{i}", (128, TH), F32, st) for i in range(8)]
        xao = [sb(f"a_xao{i}", (128, TH), BF16, st) for i in range(2)]

        cast_hist = []
        for tt in range(8):
            bi = tt % 2
            lw = [("dve", cast_hist[-2])] if len(cast_hist) >= 2 else []
            P.dma("sync", x0t[bi][:, :], x0[(q * 8 + tt) * 128:(q * 8 + tt + 1) * 128, :],
                  waits=lw, sem=f"sx0t{bi}")
            trw = [("tr", P.val("tr"))] if tt >= 2 else []
            cv = P.op("vector", lambda e, bi=bi: e.tensor_copy(out=xtm[bi][:, :], in_=x0t[bi][:, :]),
                      waits=[(f"sx0t{bi}", P.val(f"sx0t{bi}"))] + trw, inc="dve")
            cast_hist.append(cv)
            for ki in range(16):
                def trf(e, bi=bi, ki=ki, tt=tt):
                    return e.dma_start_transpose(
                        xb[:, ki * TQ + tt * 128: ki * TQ + tt * 128 + 128],
                        xtm[bi][:, ki * 128:(ki + 1) * 128])
                P.op("scalar", trf, waits=[("dve", cv)] if ki == 0 else (), inc="tr", inc_amt=16)
        xb_ready = P.val("tr")

        dq_jobs = []
        if with_dq:
            qb2 = [sb(f"a_qb{i}", (128, 2048), I32, st) for i in range(2)]
            sc2 = [sb(f"a_scl{i}", (128, 2048), F32, st) for i in range(2)]
            dqf2 = [sb(f"a_dqf{i}", (128, 2048), F32, st) for i in range(2)]
            dqb2 = [sb(f"a_dqb{i}", (128, 2048), BF16, st) for i in range(2)]
            dq_jobs = dq_rest_jobs(DqJobs(dqf2, dqb2, qb2, sc2))
        dq_ji = [0]

        def dq_step():
            ji = dq_ji[0]
            if ji < len(dq_jobs):
                dq_ji[0] = ji + 1
                dq_jobs[ji]()

        mm_hist = []
        ev_hist = []
        st_by_ebi = {}
        xao_st = []
        for half in range(2):
            c0 = half * TH
            for mi in range(81):
                dq_step()
                par = mi % 2
                ebi = mi % 4
                wv = None
                if mi < 80:
                    if mi % 2 == 0:
                        ww = [("pe", mm_hist[-3])] if len(mm_hist) >= 3 else []
                        P.dma("sync", wbuf[(mi // 2) % 2][:, :],
                              bass.AP(win_bf, mi * 128 * 2048,
                                      [[2048, 128], [128 * 2048, 2], [1, 2048]]),
                              waits=ww, sem=f"swb{(mi // 2) % 2}")
                    wv = (f"swb{(mi // 2) % 2}", P.val(f"swb{(mi // 2) % 2}"))

                def mmf(e, mi=mi, par=par, c0=c0):
                    for ki in range(16):
                        lhs = (wbuf[(mi // 2) % 2][:, (mi % 2) * 2048 + ki * 128:
                                                   (mi % 2) * 2048 + (ki + 1) * 128]
                               if mi < 80 else w1T[:, ki * 128:(ki + 1) * 128])
                        inst = e.matmul(pA[par][:, :], lhs,
                                        xb[:, ki * TQ + c0: ki * TQ + c0 + TH],
                                        start=(ki == 0), stop=(ki == 15))
                    return inst
                waits = [("tr", xb_ready)]
                if len(ev_hist) >= 2:
                    waits.append(("fA", ev_hist[-2]))
                if wv is not None:
                    waits.append(wv)
                pv = P.op("tensor", mmf, waits=waits, inc="pe")
                mm_hist.append(pv)
                col = q * TQ + c0
                ebw = [(f"sev{ebi}", P.val(f"sev{ebi}"))] if ebi in st_by_ebi else []
                if mi < 32:
                    av = P.op("scalar", lambda e, par=par, ebi=ebi: e.activation(
                        out=ev[ebi][:, :], in_=pA[par][:, :], func=AF.Silu),
                        waits=[("pe", pv)] + ebw, inc="fA")
                    st_by_ebi[ebi] = P.dma(
                        "sync", bass.AP(sz_scr, q * DI * TQ + (mi * 128) * TQ + c0,
                                        [[TQ, 128], [1, TH]]),
                        ev[ebi][:, :], waits=[("fA", av)], sem=f"sev{ebi}")
                elif mi < 40:
                    av = P.op("vector", lambda e, par=par, mi=mi, c0=c0: e.tensor_copy(
                        out=xsm[:, (mi - 32) * TQ + c0:(mi - 32) * TQ + c0 + TH],
                        in_=pA[par][:, :]), waits=[("pe", pv)], inc="fA")
                elif mi < 48:
                    av = P.op("vector", lambda e, par=par, ebi=ebi: e.tensor_copy(
                        out=ev[ebi][:, :], in_=pA[par][:, :]), waits=[("pe", pv)] + ebw, inc="fA")
                    st_by_ebi[ebi] = P.dma(
                        "sync", bass.AP(b_scr, q * DXB * TQ + ((mi - 40) * 128) * TQ + c0,
                                        [[TQ, 128], [1, TH]]),
                        ev[ebi][:, :], waits=[("fA", av)], sem=f"sev{ebi}")
                elif mi < 80:
                    av = P.op("vector", lambda e, par=par, ebi=ebi: e.tensor_copy(
                        out=ev[ebi][:, :], in_=pA[par][:, :]), waits=[("pe", pv)] + ebw, inc="fA")
                    st_by_ebi[ebi] = P.dma(
                        "sync", bass.AP(c_scr, q * DI * TQ + ((mi - 48) * 128) * TQ + c0,
                                        [[TQ, 128], [1, TH]]),
                        ev[ebi][:, :], waits=[("fA", av)], sem=f"sev{ebi}")
                else:
                    av = P.op("vector", lambda e, par=par, ebi=ebi: e.tensor_copy(
                        out=ev[ebi][:, :], in_=pA[par][:, :]), waits=[("pe", pv)] + ebw, inc="fA")
                    st_by_ebi[ebi] = P.dma(
                        "sync", bass.AP(dt1_scr, q * 128 * TQ + c0, [[TQ, 128], [1, TH]]),
                        ev[ebi][:, :], waits=[("fA", av)], sem=f"sev{ebi}")
                ev_hist.append(av)

            silu_hist = []
            for bi2 in range(NBLK):
                par = (81 + bi2) % 2
                xpar = bi2 % 2
                xi = bi2 // 4

                def selmm(e, par=par, xi=xi, bi2=bi2, c0=c0):
                    return e.matmul(pA[par][:, :],
                                    selx[:, (bi2 % 4) * 128:(bi2 % 4) * 128 + 128],
                                    xsm[:, xi * TQ + c0: xi * TQ + c0 + TH],
                                    start=True, stop=True)
                waits = [("fA", ev_hist[-1])]
                if len(ev_hist) >= 2:
                    waits.append(("fB", max(0, P.count["fB"] - 1)))
                pv = P.op("tensor", selmm, waits=waits, inc="pe")
                mm_hist.append(pv)
                hw = [("act", silu_hist[-2])] if len(silu_hist) >= 2 else []
                P.op("vector", lambda e, xpar=xpar, bi2=bi2: e.tensor_copy(
                    out=xfe[xpar][:, 0:8], in_=halo[:, bi2 * 8:bi2 * 8 + 8]),
                    waits=hw, inc="dve")
                P.op("vector", lambda e, xpar=xpar, par=par: e.tensor_copy(
                    out=xfe[xpar][:, 8:8 + TH], in_=pA[par][:, :]),
                    waits=[("pe", pv)], inc="fB")
                P.op("vector", lambda e, par=par, bi2=bi2: e.tensor_copy(
                    out=halo[:, bi2 * 8:bi2 * 8 + 8], in_=pA[par][:, TH - 8:TH]), inc="dve")
                cbase = xpar * 4
                P.op("vector", lambda e, xpar=xpar, bi2=bi2, cbase=cbase: e.tensor_scalar(
                    out=ctmp[cbase][:, :], in0=xfe[xpar][:, 5:5 + TH],
                    scalar1=cw[:, bi2 * 4:bi2 * 4 + 1], scalar2=None, op0=OP.mult), inc="dve")
                for k in range(1, 4):
                    P.op("vector", lambda e, xpar=xpar, bi2=bi2, k=k, cbase=cbase:
                         e.scalar_tensor_tensor(
                             out=ctmp[cbase + k][:, :], in0=xfe[xpar][:, 5 + k:5 + k + TH],
                             scalar=cw[:, bi2 * 4 + k:bi2 * 4 + k + 1],
                             in1=ctmp[cbase + k - 1][:, :], op0=OP.mult, op1=OP.add), inc="dve")
                dv = P.val("dve")
                xw = [(f"sxao{xpar}", P.val(f"sxao{xpar}"))] if len(xao_st) >= 2 else []
                av = P.op("scalar", lambda e, xpar=xpar, bi2=bi2, cbase=cbase: e.activation(
                    out=xao[xpar][:, :], in_=ctmp[cbase + 3][:, :], func=AF.Silu,
                    bias=cb[:, bi2:bi2 + 1]), waits=[("dve", dv)] + xw, inc="act")
                silu_hist.append(av)
                xao_st.append(P.dma(
                    "sync", bass.AP(xact_scr, q * DI * TQ + (bi2 * 128) * TQ + c0,
                                    [[TQ, 128], [1, TH]]),
                    xao[xpar][:, :], waits=[("act", av)], sem=f"sxao{xpar}"))
        P.op("sync", lambda e: e.nop(),
             waits=[(f"sev{i}", P.val(f"sev{i}")) for i in range(4)] +
                   [("sxao0", P.val("sxao0")), ("sxao1", P.val("sxao1"))])
        P.flush()
        st.close()

    def stage_b(q):
        st = ExitStack()
        spA = sb("b_spA", (128, TQ), F32, st)
        spB = sb("b_spB", (128, TQ), F32, st)
        dt_sb = sb("b_dt", (128, TQ), F32, st)
        dtx = sb("b_dtx", (128, TQ), BF16, st)
        xactb = [sb(f"b_xa{i}", (128, TQ), BF16, st) for i in range(2)]
        szb = [sb(f"b_sz{i}", (128, TQ), BF16, st) for i in range(2)]
        brep = [sb(f"b_br{i}", (128, 4 * TQ), BF16, st) for i in range(2)]
        crep = [sb(f"b_cr{i}", (128, 4 * TQ), BF16, st) for i in range(2)]
        dAg = [sb(f"b_dA{i}", (128, 4 * TQ), F32, st) for i in range(2)]
        ug = sb("b_u", (128, 4 * TQ), BF16, st)
        hg = sb("b_h", (128, 4 * TQ), BF16, st)
        pg = sb("b_p", (128, 4 * TQ), BF16, st)
        t1 = sb("b_t1", (128, 2 * TQ), BF16, st)
        t2 = sb("b_t2", (128, TQ), F32, st)
        t3 = sb("b_t3", (128, TQ), F32, st)
        dt1b = sb("b_dt1", (128, TQ), BF16, st)
        bjunk = sb("b_junk", (128, 4), F32, st)
        yb = [sb(f"b_y{i}", (128, TQ), F32, st) for i in range(2)]
        gat = [sb(f"b_g{i}", (128, TQ), BF16, st) for i in range(2)]

        u_hist, p_hist, scan_hist, gate_hist, dtv_hist, dA_hist, store_hist = \
            [], [], [], [], [], [], []
        t1_hist, t3_hist = [], []
        br_vals, cr_vals = {}, {}

        def emit_bc_loads(k):
            bi2, g2 = divmod(k, 4)
            gp2 = g2 % 2
            n02 = g2 * 4
            bw = [("dve", u_hist[-1])] if u_hist else []
            P.dma("gpsimd", brep[gp2][:, :],
                  bass.AP(b_scr, q * DXB * TQ + (32 * bi2 + n02) * TQ,
                          [[16 * TQ, 2], [0, 64], [1, 4 * TQ]]), waits=bw, sem=f"br{gp2}")
            br_vals[k] = P.val(f"br{gp2}")
            cw_ = [("dve", p_hist[-1])] if p_hist else []
            P.dma("gpsimd", crep[gp2][:, :],
                  bass.AP(c_scr, q * DI * TQ + (128 * bi2 + n02) * TQ,
                          [[16 * TQ, 8], [0, 16], [1, 4 * TQ]]), waits=cw_, sem=f"cr{gp2}")
            cr_vals[k] = P.val(f"cr{gp2}")
        gpw = ([("pe", P.val("pe"))] +
               [(f"sev{i}", P.val(f"sev{i}")) for i in range(4)] +
               [("sxao0", P.val("sxao0")), ("sxao1", P.val("sxao1"))])
        if q == 0:
            bw0 = [("dqs0", P.val("dqs0")), ("dqs1", P.val("dqs1"))]
            P.op("sync", lambda e: e.nop(), waits=bw0)
            gpw += bw0
        else:
            gpw += [("sout", P.val("sout")), ("trd0", P.val("trd0")),
                    ("trd1", P.val("trd1"))]
        P.op("gpsimd", lambda e: e.nop(), waits=gpw)
        P.dma("sync", dt1b[:, :], bass.AP(dt1_scr, q * 128 * TQ, [[TQ, 128], [1, TQ]]),
              sem="sdt1")
        dt1v = P.val("sdt1")
        for bi in range(NBLK):
            par = bi % 2
            lw = [("dve", gate_hist[-2])] if len(gate_hist) >= 2 else []
            P.dma("sync", xactb[par][:, :],
                  bass.AP(xact_scr, q * DI * TQ + bi * 128 * TQ, [[TQ, 128], [1, TQ]]),
                  waits=lw, sem=f"bxz{par}")
            P.dma("sync", szb[par][:, :],
                  bass.AP(sz_scr, q * DI * TQ + bi * 128 * TQ, [[TQ, 128], [1, TQ]]),
                  sem=f"bxz{par}")
            xzv = (f"bxz{par}", P.val(f"bxz{par}"))

            def dtmm(e, bi=bi):
                for j in range(2):
                    inst = e.matmul(pB[j][:, :], w2T[:, bi * 128:(bi + 1) * 128],
                                    dt1b[:, j * 512:(j + 1) * 512],
                                    start=True, stop=True)
                return inst
            pw = [("act", dA_hist[-1])] if len(dA_hist) >= 1 else []
            if bi == 0:
                pw.append(("sdt1", dt1v))
            pv = P.op("tensor", dtmm, waits=pw, inc="pe")
            aw = [("pe", pv)]
            if len(dtv_hist) >= 1:
                aw.append(("dve", dtv_hist[-1]))
            for j in range(2):
                P.op("scalar", lambda e, j=j, bi=bi: e.activation(
                    out=spA[:, j * 512:(j + 1) * 512], in_=pB[j][:, :], func=AF.Abs,
                    bias=dtb[:, bi:bi + 1]), waits=aw if j == 0 else (), inc="act")
            P.op("scalar", lambda e: e.activation(out=spA[:, :], in_=spA[:, :], func=AF.Exp,
                                                  scale=-1.0), inc="act")
            P.op("scalar", lambda e: e.activation(out=spA[:, :], in_=spA[:, :], func=AF.Ln,
                                                  bias=1.0), inc="act")
            for j in range(2):
                P.op("scalar", lambda e, j=j, bi=bi: e.activation(
                    out=spB[:, j * 512:(j + 1) * 512], in_=pB[j][:, :], func=AF.Relu,
                    bias=dtb[:, bi:bi + 1]), inc="act")
            P.op("scalar", lambda e: e.activation(out=bjunk[:, 0:1], in_=bjunk[:, 1:2],
                                                  func=AF.Exp), inc="act")
            av = P.val("act")
            dw = [("act", av), xzv]
            P.op("vector", lambda e: e.tensor_tensor(out=dt_sb[:, :], in0=spA[:, :],
                                                     in1=spB[:, :], op=OP.add),
                 waits=dw, inc="dve")
            dtv = P.op("vector", lambda e, par=par: e.tensor_tensor(
                out=dtx[:, :], in0=dt_sb[:, :], in1=xactb[par][:, :], op=OP.mult), inc="dve")
            dtv_hist.append(dtv)

            for g in range(4):
                gp = g % 2
                n0 = g * 4
                gi = bi * 4 + g
                if gi == 0:
                    emit_bc_loads(0)
                    emit_bc_loads(1)
                bv = (f"br{gp}", br_vals[gi])
                cvv = (f"cr{gp}", cr_vals[gi])
                acts = []
                for n in range(4):
                    aw2 = []
                    if n == 0:
                        aw2.append(("dve", dtv))
                        if len(scan_hist) >= 2:
                            aw2.append(("dve", scan_hist[-2]))
                    a = P.op("scalar", lambda e, gp=gp, n=n, bi=bi, n0=n0: e.activation(
                        out=dAg[gp][:, n * TQ:(n + 1) * TQ], in_=dt_sb[:, :], func=AF.Exp,
                        scale=a_sc[:, bi * 16 + n0 + n: bi * 16 + n0 + n + 1]),
                        waits=aw2, inc="act")
                    acts.append(a)
                dA_hist.append(acts[-1])
                uv = P.op("vector", lambda e, gp=gp: e.tensor_tensor(
                    out=ug[:, :].rearrange("p (n t) -> p n t", n=4),
                    in0=dtx[:, :].unsqueeze(1).broadcast_to([128, 4, TQ]),
                    in1=brep[gp][:, :].rearrange("p (n t) -> p n t", n=4), op=OP.mult),
                    waits=[bv], inc="dve")
                u_hist.append(uv)
                sv = None
                for n in range(4):
                    sv = P.op("vector", lambda e, gp=gp, n=n, bi=bi, n0=n0: e.tensor_tensor_scan(
                        out=hg[:, n * TQ:(n + 1) * TQ], data0=dAg[gp][:, n * TQ:(n + 1) * TQ],
                        data1=ug[:, n * TQ:(n + 1) * TQ],
                        initial=state[:, bi * 16 + n0 + n: bi * 16 + n0 + n + 1],
                        op0=OP.mult, op1=OP.add),
                        waits=[("act", acts[n])], inc="dve")
                scan_hist.append(sv)
                P.op("vector", lambda e, bi=bi, n0=n0: e.tensor_copy(
                    out=state[:, bi * 16 + n0: bi * 16 + n0 + 4],
                    in_=hg[:, TQ - 1:4 * TQ:TQ]), inc="dve")
                pw2 = [cvv]
                pvv = P.op("vector", lambda e, gp=gp: e.tensor_tensor(
                    out=pg[:, :], in0=hg[:, :], in1=crep[gp][:, :], op=OP.mult),
                    waits=pw2, inc="dve")
                p_hist.append(pvv)
                if gi + 2 < 128:
                    emit_bc_loads(gi + 2)
                if debug and q == 0 and bi == 0:
                    sv1 = P.dma("sync", bass.AP(dbg_hg, g * 4 * TQ, [[16 * TQ, 128], [1, 4 * TQ]]),
                                hg[:, :], waits=[("dve", pvv)], sem="sdbg")
                    sv1 = P.dma("sync", bass.AP(dbg_ug, g * 4 * TQ, [[16 * TQ, 128], [1, 4 * TQ]]),
                                ug[:, :], sem="sdbg")
                    sv1 = P.dma("sync", bass.AP(dbg_dA, g * 4 * TQ, [[16 * TQ, 128], [1, 4 * TQ]]),
                                dAg[gp][:, :], sem="sdbg")
                    sv1 = P.dma("sync", bass.AP(dbg_pg, g * 4 * TQ, [[16 * TQ, 128], [1, 4 * TQ]]),
                                pg[:, :], sem="sdbg")
                    if g == 0:
                        sv1 = P.dma("sync", dbg_dt.ap(), dt_sb[:, :], sem="sdbg")
                    P.op("vector", lambda e: e.nop(), waits=[("sdbg", P.val("sdbg"))])
                tw1 = P.op("vector", lambda e: e.tensor_tensor(
                    out=t1[:, :], in0=pg[:, 0:2 * TQ], in1=pg[:, 2 * TQ:4 * TQ],
                    op=OP.add), inc="dve")
                t1_hist.append(tw1)
                if g == 0:
                    tv = P.op("vector", lambda e: e.tensor_tensor(
                        out=yb[0][:, :], in0=t1[:, 0:TQ], in1=t1[:, TQ:2 * TQ],
                        op=OP.add), inc="dve")
                else:
                    P.op("vector", lambda e: e.tensor_tensor(
                        out=t2[:, :], in0=t1[:, 0:TQ], in1=t1[:, TQ:2 * TQ],
                        op=OP.add), inc="dve")
                    tv = P.op("vector", lambda e, g=g: e.tensor_tensor(
                        out=yb[g % 2][:, :], in0=yb[(g - 1) % 2][:, :], in1=t2[:, :],
                        op=OP.add), inc="dve")
                if debug and q == 0 and bi == 0:
                    P.dma("sync", bass.AP(dbg_t1, g * 2 * TQ, [[8 * TQ, 128], [1, 2 * TQ]]),
                                t1[:, :], waits=[("dve", tv)], sem="sdbg")
                    P.dma("sync", bass.AP(dbg_yb, g * TQ, [[4 * TQ, 128], [1, TQ]]),
                                yb[g % 2][:, :], sem="sdbg")
                    P.op("vector", lambda e: e.nop(), waits=[("sdbg", P.val("sdbg"))])
            if debug and q == 0 and bi == 0:
                dvv = P.val("dve")
                avv = P.val("act")
                for g in range(4):
                    pass
            gw = []
            if len(store_hist) >= 2:
                gw.append((f"gst{par}", P.val(f"gst{par}")))
            t3v = P.op("vector", lambda e, par=par, bi=bi: e.scalar_tensor_tensor(
                out=t3[:, :], in0=xactb[par][:, :], scalar=dvec[:, bi:bi + 1],
                in1=yb[1][:, :], op0=OP.mult, op1=OP.add), waits=gw, inc="dve")
            t3_hist.append(t3v)
            gv = P.op("vector", lambda e, par=par: e.tensor_tensor(
                out=gat[par][:, :], in0=t3[:, :], in1=szb[par][:, :], op=OP.mult), inc="dve")
            gate_hist.append(gv)
            sv2 = P.dma("sync", bass.AP(yg_scr, q * DI * TQ + bi * 128 * TQ,
                                        [[TQ, 128], [1, TQ]]),
                        gat[par][:, :], waits=[("dve", gv)], sem=f"gst{par}")
            store_hist.append(sv2)
        P.op("sync", lambda e: e.nop(),
             waits=[("gst0", P.val("gst0")), ("gst1", P.val("gst1"))])
        P.flush()
        st.close()

    def stage_c(q):
        stO = ExitStack()
        r1T = sb("c_r1T", (128, 16 * TQ), BF16, stO)
        if q == 0:
            P.op("sync", lambda e: e.nop(),
                 waits=[("dqs0", P.val("dqs0")), ("dqs1", P.val("dqs1"))])

        st1 = ExitStack()
        gtq = sb("c_gt", (128, 32 * TH), BF16, st1)
        wos = sb("c_wos", (128, 32 * 256), BF16, st1)
        oat = sb("c_oat", (128, 4 * 2048), BF16, st1)
        x0f = [sb(f"c_x0{i}", (128, 2048), F32, st1) for i in range(1)]
        rr = [sb(f"c_rr{i}", (128, 2048), F32, st1) for i in range(2)]
        r1f = [sb(f"c_r1f{i}", (128, 2048), F32, st1) for i in range(2)]
        r1b = [sb(f"c_r1b{i}", (128, 2048), BF16, st1) for i in range(2)]
        sbn = sb("c_sbn", (128, 96), F32, st1)
        cjunk = sb("c_junk", (128, 1032), F32, st1)
        mvv = sb("c_mv", (128, 8), F32, st1)
        rstd = sb("c_rstd", (128, 4), F32, st1)
        lnt = sb("c_lnt", (128, 4), F32, st1)
        musc = sb("c_musc", (128, 4), F32, st1)

        opmm_hist = []
        rr_hist = []
        ap_hist_st = []
        ap_hist_tr = []
        for half in range(2):
            c0 = half * TH
            gw = [("pe", opmm_hist[-1])] if opmm_hist else []
            P.dma("sync", gtq[:, :],
                  bass.AP(yg_scr, q * DI * TQ + c0, [[TQ, 128], [128 * TQ, 32], [1, TH]]),
                  waits=gw, sem="sgt")
            gv = ("sgt", P.val("sgt"))
            evh = []
            for cs in range(8):
                ww = [("pe", opmm_hist[-1])] if opmm_hist else []
                P.dma("sync", wos[:, :],
                      bass.AP(wout_bf, cs * 128 * 32 * 256, [[8192, 128], [1, 8192]]),
                      waits=ww, sem="swos")
                wv = ("swos", P.val("swos"))
                for mt in range(4):
                    par = mt % 2

                    def opmm(e, mt=mt, par=par):
                        for ki in range(32):
                            inst = e.matmul(pC[par][:, 0:256],
                                            gtq[:, ki * TH + mt * 128: ki * TH + mt * 128 + 128],
                                            wos[:, ki * 256:(ki + 1) * 256],
                                            start=(ki == 0), stop=(ki == 31))
                        return inst
                    waits = [wv, gv]
                    if len(evh) >= 2:
                        waits.append(("fC", evh[-2]))
                    pv = P.op("tensor", opmm, waits=waits, inc="pe")
                    opmm_hist.append(pv)
                    e1 = P.op("vector", lambda e, mt=mt, par=par, cs=cs: e.tensor_copy(
                        out=oat[:, mt * 2048 + cs * 256: mt * 2048 + (cs + 1) * 256],
                        in_=pC[par][:, 0:256]), waits=[("pe", pv)], inc="fC")
                    evh.append(e1)
            opev = evh[-1]
            if debug and q == 0:
                P.dma("sync", bass.AP(dbg_oat, half * 128 * 8192, [[8192, 128], [1, 8192]]),
                      oat[:, :], waits=[("fC", opev)], sem="sdbg")
            for pr in range(2):
                mts = [2 * pr, 2 * pr + 1]
                for mt in mts:
                    par = mt % 2
                    row = q * TQ + c0 + mt * 128
                    lw = [("dve", rr_hist[-1])] if len(rr_hist) >= 1 else []
                    P.dma("sync", x0f[0][:, :], x0[row:row + 128, :], waits=lw, sem="sx0f")
                    rv = P.op("vector", lambda e, mt=mt, par=par: e.tensor_tensor(
                        out=rr[par][:, :], in0=x0f[0][:, :],
                        in1=oat[:, mt * 2048:(mt + 1) * 2048], op=OP.add),
                        waits=[("sx0f", P.val("sx0f")), ("fC", opev)], inc="dve")
                    for s4 in range(4):
                        P.op("vector", lambda e, mt=mt, par=par, s4=s4: e.bn_stats(
                            out=sbn[:, mt * 24 + s4 * 6: mt * 24 + (s4 + 1) * 6],
                            in_=rr[par][:, s4 * 512:(s4 + 1) * 512]), inc="dve")
                    rr_hist.append(rv)
                for _ in range(2):
                    P.op("vector", lambda e: e.tensor_copy(out=cjunk[:, 8:520], in_=cjunk[:, 520:1032]),
                         inc="dve")
                stats_done = P.val("dve")
                if debug and q == 1 and half == 0 and pr == 0:
                    P.dma("sync", bass.AP(dbg_rr, 0, [[2048, 128], [1, 2048]]),
                          rr[0][:, :], waits=[("dve", stats_done)], sem="sdbg")
                    P.dma("sync", bass.AP(dbg_rr, 128 * 2048, [[2048, 128], [1, 2048]]),
                          rr[1][:, :], sem="sdbg")
                    P.dma("sync", dbg_sbn.ap(), sbn[:, :], sem="sdbg")
                    P.op("vector", lambda e: e.nop(), waits=[("sdbg", P.val("sdbg"))])
                for mt in mts:
                    P.op("vector", lambda e, mt=mt: e.bn_aggr(
                        out=mvv[:, mt * 2:(mt + 1) * 2], in_=sbn[:, mt * 24:(mt + 1) * 24]),
                        waits=[("dve", stats_done)] if mt == mts[0] else (), inc="dve")
                for _ in range(2):
                    P.op("vector", lambda e: e.tensor_copy(out=cjunk[:, 8:520], in_=cjunk[:, 520:1032]),
                         inc="dve")
                P.op("vector", lambda e, pr=pr: e.tensor_scalar(
                    out=lnt[:, pr * 2:pr * 2 + 2], in0=mvv[:, 4 * pr + 1: 4 * pr + 4: 2],
                    scalar1=EPS, scalar2=None, op0=OP.add), inc="dve")
                for _ in range(2):
                    P.op("vector", lambda e: e.tensor_copy(out=cjunk[:, 8:520], in_=cjunk[:, 520:1032]),
                         inc="dve")
                bv = P.val("dve")
                P.op("scalar", lambda e, pr=pr: e.activation(
                    out=lnt[:, pr * 2:pr * 2 + 2], in_=lnt[:, pr * 2:pr * 2 + 2], func=AF.Ln),
                    waits=[("dve", bv)], inc="act")
                for _ in range(2):
                    P.op("scalar", lambda e: e.activation(out=cjunk[:, 8:520], in_=cjunk[:, 520:1032],
                                                          func=AF.Relu), inc="act")
                P.op("scalar", lambda e, pr=pr: e.activation(
                    out=rstd[:, pr * 2:pr * 2 + 2], in_=lnt[:, pr * 2:pr * 2 + 2], func=AF.Exp,
                    scale=-0.5), inc="act")
                for _ in range(2):
                    P.op("scalar", lambda e: e.activation(out=cjunk[:, 8:520], in_=cjunk[:, 520:1032],
                                                          func=AF.Relu), inc="act")
                agg_done = P.val("act")
                P.op("vector", lambda e, pr=pr: e.tensor_tensor(
                    out=musc[:, pr * 2:pr * 2 + 2], in0=mvv[:, 4 * pr: 4 * pr + 3: 2],
                    in1=rstd[:, pr * 2:pr * 2 + 2], op=OP.mult),
                    waits=[("act", agg_done)], inc="ln")
                for _ in range(2):
                    P.op("vector", lambda e: e.tensor_copy(out=cjunk[:, 8:520], in_=cjunk[:, 520:1032]),
                         inc="dve")
                lnv = P.val("ln")
                if debug and q == 1 and half == 0:
                    P.dma("sync", bass.AP(dbg_mv, pr * 128 * 16, [[16, 128], [1, 8]]),
                          mvv[:, :], waits=[("ln", P.val("ln"))], sem="sdbg")
                    P.dma("sync", bass.AP(dbg_mv, pr * 128 * 16 + 8, [[16, 128], [1, 4]]),
                          rstd[:, :], sem="sdbg")
                    P.dma("sync", bass.AP(dbg_mv, pr * 128 * 16 + 12, [[16, 128], [1, 4]]),
                          musc[:, :], sem="sdbg")
                if debug and q == 0 and pr == 1 and False:
                    P.dma("sync", bass.AP(dbg_mv, half * 128 * 16, [[16, 128], [1, 8]]),
                          mvv[:, :], waits=[("ln", lnv)], sem="sdbg")
                    P.dma("sync", bass.AP(dbg_mv, half * 128 * 16 + 8, [[16, 128], [1, 4]]),
                          rstd[:, :], waits=[("ln", lnv)], sem="sdbg")
                    P.dma("sync", bass.AP(dbg_mv, half * 128 * 16 + 12, [[16, 128], [1, 4]]),
                          musc[:, :], waits=[("ln", lnv)], sem="sdbg")
                for mt in mts:
                    par = mt % 2
                    row = q * TQ + c0 + mt * 128
                    aw = [("ln", lnv)] if mt == mts[0] else []
                    if len(ap_hist_st) >= 2:
                        aw.append((f"sr1{par}", P.val(f"sr1{par}")))
                    P.op("vector", lambda e, mt=mt, par=par: e.tensor_tensor(
                        out=r1f[par][:, :], in0=rr[par][:, :],
                        in1=rstd[:, mt:mt + 1].broadcast_to([128, 2048]), op=OP.mult),
                        waits=aw, inc="dve")
                    a1 = P.op("vector", lambda e, mt=mt, par=par: e.tensor_tensor(
                        out=r1f[par][:, :], in0=r1f[par][:, :],
                        in1=musc[:, mt:mt + 1].broadcast_to([128, 2048]), op=OP.subtract),
                        waits=(), inc="dve")
                    a2 = P.op("vector", lambda e, par=par: e.tensor_copy(
                        out=r1b[par][:, :], in_=r1f[par][:, :]),
                        waits=[(f"tra{par}", P.val(f"tra{par}"))] if len(ap_hist_tr) >= 2 else (),
                        inc="dve")
                    ap_hist_st.append(P.dma("sync", r1_scr[row:row + 128, :], r1f[par][:, :],
                                            waits=[("dve", a2)], sem=f"sr1{par}"))
                    for ki in range(16):
                        def trf(e, ki=ki, par=par, c0=c0, mt=mt):
                            return e.dma_start_transpose(
                                r1T[:, ki * TQ + c0 + mt * 128: ki * TQ + c0 + mt * 128 + 128],
                                r1b[par][:, ki * 128:(ki + 1) * 128])
                        P.op("scalar", trf, waits=[("dve", a2)] if ki == 0 else (),
                             inc=f"tra{par}", inc_amt=16)
                    ap_hist_tr.append(1)
        P.op("sync", lambda e: e.nop(),
             waits=[("sr10", P.val("sr10")), ("sr11", P.val("sr11")),
                    ("tra0", P.val("tra0")), ("tra1", P.val("tra1"))])
        P.flush()
        st1.close()
        r1T_ready = [("tra0", P.val("tra0")), ("tra1", P.val("tra1"))]

        st2 = ExitStack()
        h2sb = sb("c_h2", (128, 64 * TH), BF16, st2)
        wfb = [sb(f"c_wf{i}", (128, 16 * 128), BF16, st2) for i in range(2)]
        wdb = [sb(f"c_wd{i}", (128, 16 * 128), BF16, st2) for i in range(2)]
        xA = sb("c_xA", (16, TH), BF16, st2)
        xA2 = sb("c_xA2", (16, TH), BF16, st2)
        dcol = [sb(f"c_dc{i}", (128, 512), BF16, st2) for i in range(2)]
        dtok = sb("c_dtok", (128, 4 * 2048), BF16, st2)
        r1rd = [sb(f"c_r1r{i}", (128, 2048), F32, st2) for i in range(1)]
        r2f = [sb(f"c_r2{i}", (128, 2048), F32, st2) for i in range(2)]
        outf = [sb(f"c_of{i}", (128, 2048), F32, st2) for i in range(1)]
        sbn2 = sb("c_sbn2", (128, 96), F32, st2)
        cjunk2 = sb("c_junk2", (128, 520), F32, st2)
        mv2 = sb("c_mv2", (128, 8), F32, st2)
        rstd2 = sb("c_rstd2", (128, 4), F32, st2)
        lnt2 = sb("c_lnt2", (128, 4), F32, st2)
        musc2 = sb("c_musc2", (128, 4), F32, st2)

        dn_pv_hist = []
        r2_hist = []
        ost_hist = []
        for half in range(2):
            c0 = half * TH

            def xamm(e, c0=c0):
                for ki in range(16):
                    inst = e.matmul(pU[0][0:16, :], aupT[:, ki * 16:(ki + 1) * 16],
                                    r1T[:, ki * TQ + c0: ki * TQ + c0 + TH],
                                    start=(ki == 0), stop=(ki == 15))
                return inst
            xw = r1T_ready + [("fU", P.val("fU"))]
            pv = P.op("tensor", xamm, waits=xw, inc="pe")
            xev = P.op("vector", lambda e: e.tensor_copy(out=xA[:, :], in_=pU[0][0:16, :]),
                       waits=[("pe", pv)], inc="fU")
            up_mm_hist = []
            up_ev_hist = []
            for mi in range(64):
                par = mi % 2
                ww = [("pe", up_mm_hist[-2])] if len(up_mm_hist) >= 2 else []
                P.dma("sync", wfb[par][:, :],
                      bass.AP(wup_bf, mi * 128 * 2048, [[2048, 128], [1, 2048]]), waits=ww,
                      sem=f"swf{par}")
                wv = (f"swf{par}", P.val(f"swf{par}"))

                def upmm(e, mi=mi, par=par, c0=c0):
                    for ki in range(16):
                        e.matmul(pU[par][:, :], wfb[par][:, ki * 128:(ki + 1) * 128],
                                 r1T[:, ki * TQ + c0: ki * TQ + c0 + TH],
                                 start=(ki == 0), stop=False)
                    return e.matmul(pU[par][:, :], bupL[:, mi * 128:(mi + 1) * 128],
                                    xA[0:16, :], start=False, stop=True)
                waits = [wv, ("fU", xev)]
                if len(up_ev_hist) >= 2:
                    waits.append(("fU", up_ev_hist[-2]))
                if half > 0 and mi == 0 and dn_pv_hist:
                    waits.append(("pe", dn_pv_hist[-1]))
                pv = P.op("tensor", upmm, waits=waits, inc="pe")
                up_mm_hist.append(pv)
                e1 = P.op("scalar", lambda e, mi=mi, par=par: e.activation(
                    out=h2sb[:, mi * TH:(mi + 1) * TH], in_=pU[par][:, :], func=AF.Relu,
                    bias=bupT[:, mi:mi + 1]), waits=[("pe", pv)], inc="fU")
                up_ev_hist.append(e1)
            upv = up_ev_hist[-1]

            def xa2mm(e):
                for ki in range(64):
                    inst = e.matmul(pC[0][0:16, :], adnT[:, ki * 16:(ki + 1) * 16],
                                    h2sb[:, ki * TH:(ki + 1) * TH],
                                    start=(ki == 0), stop=(ki == 63))
                return inst
            pv = P.op("tensor", xa2mm, waits=[("fU", upv), ("fC", P.val("fC"))], inc="pe")
            xa2v = P.op("vector", lambda e: e.tensor_copy(out=xA2[:, :], in_=pC[0][0:16, :]),
                        waits=[("pe", pv)], inc="fC")
            dn_ev_hist = []
            dn_tr_hist = []
            for mi in range(16):
                par = mi % 2
                for kg in range(4):
                    ww = [("pe", dn_pv_hist[-2])] if len(dn_pv_hist) >= 2 else []
                    P.dma("sync", wdb[kg % 2][:, :],
                          bass.AP(wdn_bf, mi * 128 * 8192 + kg * 16 * 128,
                                  [[8192, 128], [1, 2048]]), waits=ww,
                          sem=f"swd{kg % 2}")
                    wv = (f"swd{kg % 2}", P.val(f"swd{kg % 2}"))

                    def dnmm(e, mi=mi, par=par, kg=kg):
                        for k2 in range(16):
                            ki = kg * 16 + k2
                            inst = e.matmul(pC[par][:, :],
                                            wdb[kg % 2][:, k2 * 128:(k2 + 1) * 128],
                                            h2sb[:, ki * TH:(ki + 1) * TH],
                                            start=(ki == 0), stop=False)
                        return inst
                    waits = [wv, ("fC", xa2v)]
                    if len(dn_ev_hist) >= 2 and kg == 0:
                        waits.append(("fC", dn_ev_hist[-2]))
                    pvk = P.op("tensor", dnmm, waits=waits, inc="pe")
                    dn_pv_hist.append(pvk)

                def dnlora(e, mi=mi, par=par):
                    return e.matmul(pC[par][:, :], bdnL[:, mi * 128:(mi + 1) * 128],
                                    xA2[0:16, :], start=False, stop=True)
                pv = P.op("tensor", dnlora, inc="pe")
                dn_pv_hist.append(pv)
                aw = [("pe", pv)]
                if P.val(f"trd{par}") > 0:
                    aw.append((f"trd{par}", P.val(f"trd{par}")))
                if half > 0 and mi == 0 and r2_hist:
                    aw.append(("dve", r2_hist[-1]))
                av = P.op("vector", lambda e, mi=mi, par=par: e.tensor_scalar(
                    out=dcol[par][:, :], in0=pC[par][:, :], scalar1=bdnT[:, mi:mi + 1],
                    scalar2=None, op0=OP.add), waits=aw, inc="fC")
                dn_ev_hist.append(av)
                for mtl in range(4):
                    def trf(e, mi=mi, par=par, mtl=mtl):
                        return e.dma_start_transpose(
                            dtok[:, mtl * 2048 + mi * 128: mtl * 2048 + (mi + 1) * 128],
                            dcol[par][:, mtl * 128:(mtl + 1) * 128])
                    P.op("scalar", trf, waits=[("fC", av)] if mtl == 0 else (),
                         inc=f"trd{par}", inc_amt=16)
                dn_tr_hist.append(1)
            dnv = [("trd0", P.val("trd0")), ("trd1", P.val("trd1"))]
            for pr in range(2):
                mtls = [2 * pr, 2 * pr + 1]
                for mtl in mtls:
                    par = mtl % 2
                    row = q * TQ + c0 + mtl * 128
                    lw = [("dve", r2_hist[-1])] if len(r2_hist) >= 1 else []
                    P.dma("sync", r1rd[0][:, :], r1_scr[row:row + 128, :], waits=lw, sem="sr1r")
                    vv = P.op("vector", lambda e, mtl=mtl, par=par: e.tensor_tensor(
                        out=r2f[par][:, :], in0=r1rd[0][:, :],
                        in1=dtok[:, mtl * 2048:(mtl + 1) * 2048], op=OP.add),
                        waits=[("sr1r", P.val("sr1r"))] + dnv, inc="dve")
                    for s4 in range(4):
                        P.op("vector", lambda e, mtl=mtl, par=par, s4=s4: e.bn_stats(
                            out=sbn2[:, mtl * 24 + s4 * 6: mtl * 24 + (s4 + 1) * 6],
                            in_=r2f[par][:, s4 * 512:(s4 + 1) * 512]), inc="dve")
                    r2_hist.append(vv)
                for _ in range(2):
                    P.op("vector", lambda e: e.tensor_copy(out=cjunk2[:, 8:264], in_=cjunk2[:, 264:520]),
                         inc="dve")
                stats2 = P.val("dve")
                for mtl in mtls:
                    P.op("vector", lambda e, mtl=mtl: e.bn_aggr(
                        out=mv2[:, mtl * 2:(mtl + 1) * 2], in_=sbn2[:, mtl * 24:(mtl + 1) * 24]),
                        waits=[("dve", stats2)] if mtl == mtls[0] else (), inc="dve")
                for _ in range(2):
                    P.op("vector", lambda e: e.tensor_copy(out=cjunk2[:, 8:264], in_=cjunk2[:, 264:520]),
                         inc="dve")
                P.op("vector", lambda e, pr=pr: e.tensor_scalar(
                    out=lnt2[:, pr * 2:pr * 2 + 2], in0=mv2[:, 4 * pr + 1: 4 * pr + 4: 2],
                    scalar1=EPS, scalar2=None, op0=OP.add), inc="dve")
                for _ in range(2):
                    P.op("vector", lambda e: e.tensor_copy(out=cjunk2[:, 8:264], in_=cjunk2[:, 264:520]),
                         inc="dve")
                bv = P.val("dve")
                P.op("scalar", lambda e, pr=pr: e.activation(
                    out=lnt2[:, pr * 2:pr * 2 + 2], in_=lnt2[:, pr * 2:pr * 2 + 2], func=AF.Ln),
                    waits=[("dve", bv)], inc="act")
                for _ in range(2):
                    P.op("scalar", lambda e: e.activation(out=cjunk2[:, 8:264], in_=cjunk2[:, 264:520],
                                                          func=AF.Relu), inc="act")
                P.op("scalar", lambda e, pr=pr: e.activation(
                    out=rstd2[:, pr * 2:pr * 2 + 2], in_=lnt2[:, pr * 2:pr * 2 + 2], func=AF.Exp,
                    scale=-0.5), inc="act")
                for _ in range(2):
                    P.op("scalar", lambda e: e.activation(out=cjunk2[:, 8:264], in_=cjunk2[:, 264:520],
                                                          func=AF.Relu), inc="act")
                agg2 = P.val("act")
                P.op("vector", lambda e, pr=pr: e.tensor_tensor(
                    out=musc2[:, pr * 2:pr * 2 + 2], in0=mv2[:, 4 * pr: 4 * pr + 3: 2],
                    in1=rstd2[:, pr * 2:pr * 2 + 2], op=OP.mult),
                    waits=[("act", agg2)], inc="ln")
                for _ in range(2):
                    P.op("vector", lambda e: e.tensor_copy(out=cjunk2[:, 8:264], in_=cjunk2[:, 264:520]),
                         inc="dve")
                ln2 = P.val("ln")
                for mtl in mtls:
                    par = mtl % 2
                    row = q * TQ + c0 + mtl * 128
                    aw = [("ln", ln2)] if mtl == mtls[0] else []
                    if len(ost_hist) >= 1:
                        aw.append(("sout", P.val("sout")))
                    P.op("vector", lambda e, mtl=mtl, par=par: e.tensor_tensor(
                        out=outf[0][:, :], in0=r2f[par][:, :],
                        in1=rstd2[:, mtl:mtl + 1].broadcast_to([128, 2048]), op=OP.mult),
                        waits=aw, inc="dve")
                    a1 = P.op("vector", lambda e, mtl=mtl, par=par: e.tensor_tensor(
                        out=outf[0][:, :], in0=outf[0][:, :],
                        in1=musc2[:, mtl:mtl + 1].broadcast_to([128, 2048]), op=OP.subtract),
                        waits=(), inc="dve")
                    ost_hist.append(P.dma("sync", out_d[row:row + 128, :], outf[0][:, :],
                                          waits=[("dve", a1)], sem="sout"))
        P.op("sync", lambda e: e.nop(),
             waits=[("sout", P.val("sout")), ("trd0", P.val("trd0")),
                    ("trd1", P.val("trd1"))])
        P.flush()
        st2.close()
        stO.close()

    stage_w()
    stage_dq_win()
    stage_dq_rest()
    for q in range(NQ):
        stage_a(q, with_dq=False)
        stage_b(q)
        stage_c(q)
    FINAL_STORES = (["sout", "sr10", "sr11", "gst0", "gst1", "sdbg", "st"] +
                    [f"sev{i}" for i in range(4)] + ["sxao0", "sxao1", "dqs0", "dqs1"])
    if debug:
        dbg_targets = [("sz_scr", sz_scr, NQ * DI * TQ), ("xact_scr", xact_scr, NQ * DI * TQ),
                       ("b_scr", b_scr, NQ * DXB * TQ), ("c_scr", c_scr, NQ * DI * TQ),
                       ("yg_scr", yg_scr, NQ * DI * TQ), ("dt1_scr", dt1_scr, NQ * 128 * TQ),
                       ("wup_bf", wup_bf, 64 * 16 * 128 * 128),
                       ("hg0", dbg_hg, 128 * 16 * TQ), ("ug0", dbg_ug, 128 * 16 * TQ),
                       ("pg0", dbg_pg, 128 * 16 * TQ), ("t10", dbg_t1, 128 * 8 * TQ),
                       ("oat0", dbg_oat, 2 * 128 * 8192)]
        st = ExitStack()
        db = sb("dbg_b", (128, 4096), BF16, st)
        df = sb("dbg_f", (128, 4096), F32, st)
        for name, t, nelem in dbg_targets:
            o = nc.dram_tensor("dbgo_" + name, [nelem // 4096, 4096], F32, kind="ExternalOutput")
            ntile = nelem // (128 * 4096)
            for i in range(ntile):
                P.dma("sync", db[:, :],
                      bass.AP(t, i * 128 * 4096, [[4096, 128], [1, 4096]]), sem="sdbg",
                      waits=[("sdbg", P.val("sdbg"))])
                cv = P.op("vector", lambda e: e.tensor_copy(out=df[:, :], in_=db[:, :]),
                          waits=[("sdbg", P.val("sdbg"))], inc="dve")
                P.dma("sync", bass.AP(o, i * 128 * 4096, [[4096, 128], [1, 4096]]),
                      df[:, :], waits=[("dve", cv)], sem="sdbg")
        P.op("sync", lambda e: e.nop(),
             waits=[(nm, P.val(nm)) for nm in FINAL_STORES])
        P.flush()
        st.close()
        stack.close()
        return nc
    P.op("sync", lambda e: e.nop(),
         waits=[(nm, P.val(nm)) for nm in FINAL_STORES])
    P.flush()
    stack.close()
    return nc


_NC_CACHE = None


def _get_nc():
    global _NC_CACHE
    if _NC_CACHE is None:
        _NC_CACHE = build_program()
    return _NC_CACHE


def kernel(L0_attn_hidden_states, L0_attn_in_proj_w, L0_attn_dt_in_proj_w,
           L0_attn_dt_proj_w, L0_attn_dt_proj_b, L0_attn_conv_w, L0_attn_conv_b,
           L0_attn_A_log, L0_attn_D, L0_attn_out_proj_w, L0_ffn_w_up_q,
           L0_ffn_w_up_scale, L0_ffn_b_up, L0_ffn_w_up_lora_a, L0_ffn_w_up_lora_b,
           L0_ffn_w_down_q, L0_ffn_w_down_scale, L0_ffn_b_down,
           L0_ffn_w_down_lora_a, L0_ffn_w_down_lora_b):
    f32 = lambda a: np.ascontiguousarray(np.asarray(a, np.float32))
    i32 = lambda a: np.ascontiguousarray(np.asarray(a, np.int32))
    hs = f32(L0_attn_hidden_states)

    Win = f32(L0_attn_in_proj_w)
    W1 = f32(L0_attn_dt_in_proj_w)
    W2 = f32(L0_attn_dt_proj_w)
    Wout = f32(L0_attn_out_proj_w)
    selx = np.zeros((4, 128, 128), np.float32)
    for r in range(4):
        for m in range(128):
            selx[r, 32 * r + 16 * (m // 64) + (m % 16), m] = 1.0
    shared = dict(
        win_t=np.ascontiguousarray(Win.reshape(80, 128, 16, 128).transpose(0, 2, 3, 1)),
        w1_t=np.ascontiguousarray(W1.reshape(128, 16, 128).transpose(1, 2, 0)),
        w2_t=np.ascontiguousarray(W2.reshape(32, 128, 128).transpose(0, 2, 1)),
        wout_t=np.ascontiguousarray(Wout.T.reshape(32, 128, 2048)),
        wq_up=np.ascontiguousarray(i32(L0_ffn_w_up_q).T.reshape(16, 128, 8192)),
        sc_up=np.ascontiguousarray(
            np.repeat(f32(L0_ffn_w_up_scale).T, 64, axis=0).reshape(16, 128, 8192)),
        wq_dn=np.ascontiguousarray(i32(L0_ffn_w_down_q).T.reshape(64, 128, 2048)),
        sc_dn=np.ascontiguousarray(
            np.repeat(f32(L0_ffn_w_down_scale).T, 64, axis=0).reshape(64, 128, 2048)),
        bup_h=np.ascontiguousarray(f32(L0_ffn_b_up).reshape(64, 128).T),
        bdn_h=np.ascontiguousarray(f32(L0_ffn_b_down).reshape(16, 128).T),
        aup_h=np.ascontiguousarray(f32(L0_ffn_w_up_lora_a).reshape(16, 128, 16)),
        bupl_h=f32(L0_ffn_w_up_lora_b),
        adn_h=np.ascontiguousarray(f32(L0_ffn_w_down_lora_a).reshape(64, 128, 16)),
        bdnl_h=f32(L0_ffn_w_down_lora_b),
        dtb_h=np.ascontiguousarray(f32(L0_attn_dt_proj_b).reshape(32, 128).T),
        cw_h=np.ascontiguousarray(f32(L0_attn_conv_w).reshape(32, 128, 4)
                                  .transpose(1, 0, 2).reshape(128, 128)),
        cb_h=np.ascontiguousarray(f32(L0_attn_conv_b).reshape(32, 128).T),
        dvec_h=np.ascontiguousarray(f32(L0_attn_D).reshape(32, 128).T),
        alog_h=np.ascontiguousarray(f32(L0_attn_A_log).reshape(32, 128, 16)
                                    .transpose(1, 0, 2).reshape(128, 512)),
        selx_h=selx,
    )
    nc = _get_nc()
    in_maps = []
    for c in range(8):
        m = dict(shared)
        m["x0"] = np.ascontiguousarray(hs[c])
        in_maps.append(m)
    res = run_bass_kernel_spmd(nc, in_maps, core_ids=list(range(8)))
    out = np.stack([np.asarray(r["out"], np.float32) for r in res.results], axis=0)
    return out


# version 26
# speedup vs baseline: 1.0770x; 1.0154x over previous
import numpy as np
from contextlib import ExitStack

import concourse.bass as bass
import concourse.mybir as mybir
from concourse.bass_utils import run_bass_kernel_spmd

F32 = mybir.dt.float32
BF16 = mybir.dt.bfloat16
I32 = mybir.dt.int32
AF = mybir.ActivationFunctionType
OP = mybir.AluOpType

L = 4096
DM = 2048
DI = 4096
DXB = 1024
NQ = 4
TQ = L // NQ
TH = TQ // 2
NBLK = DI // 128
EPS = 1e-5

ENGINES = ("sync", "tensor", "vector", "scalar", "gpsimd")


class Plan:
    def __init__(self, nc, sems):
        self.nc = nc
        self.sems = sems
        self.count = {k: 0 for k in sems}
        self.ops = {e: [] for e in ENGINES}

    def op(self, eng, fn, waits=(), inc=None, inc_amt=1):
        waits = [(s, v) for (s, v) in waits if v > 0]
        self.ops[eng].append((fn, tuple(waits), inc, inc_amt))
        if inc is not None:
            self.count[inc] += inc_amt
            return self.count[inc]
        return None

    def dma(self, queue, out, in_, waits=(), sem="dma"):
        def fn(e, out=out, in_=in_):
            return e.dma_start(out=out, in_=in_)
        return self.op(queue, fn, waits=waits, inc=sem, inc_amt=16)

    def val(self, sem):
        return self.count[sem]

    def flush(self):
        ops = self.ops
        self.ops = {e: [] for e in ENGINES}
        sems = self.sems
        with self.nc.Block() as blk:
            for ename in ENGINES:
                if not ops[ename]:
                    continue

                def body(eng, _ops=ops[ename]):
                    for fn, waits, inc, amt in _ops:
                        for (s, v) in waits:
                            eng.wait_ge(sems[s], v)
                        inst = fn(eng)
                        if inc is not None:
                            inst.then_inc(sems[inc], amt)

                getattr(blk, ename)(body)


def build_program(debug=False):
    nc = bass.Bass("TRN2", target_bir_lowering=False)

    d = {}
    def din(name, shape, dt):
        d[name] = nc.dram_tensor(name, list(shape), dt, kind="ExternalInput")
        return d[name]

    x0 = din("x0", (L, DM), F32)
    win_t = din("win_t", (80, 16, 128, 128), F32)
    w1_t = din("w1_t", (16, 128, 128), F32)
    w2_t = din("w2_t", (32, 128, 128), F32)
    wout_t = din("wout_t", (32, 128, 2048), F32)
    wq_up = din("wq_up", (16, 128, 8192), I32)
    sc_up = din("sc_up", (16, 128, 8192), F32)
    wq_dn = din("wq_dn", (64, 128, 2048), I32)
    sc_dn = din("sc_dn", (64, 128, 2048), F32)
    bup_h = din("bup_h", (128, 64), F32)
    bdn_h = din("bdn_h", (128, 16), F32)
    aup_h = din("aup_h", (16, 128, 16), F32)
    bupl_h = din("bupl_h", (16, 8192), F32)
    adn_h = din("adn_h", (64, 128, 16), F32)
    bdnl_h = din("bdnl_h", (16, 2048), F32)
    dtb_h = din("dtb_h", (128, 32), F32)
    cw_h = din("cw_h", (128, 128), F32)
    cb_h = din("cb_h", (128, 32), F32)
    dvec_h = din("dvec_h", (128, 32), F32)
    alog_h = din("alog_h", (128, 512), F32)
    selx_h = din("selx_h", (4, 128, 128), F32)

    out_d = nc.dram_tensor("out", [L, DM], F32, kind="ExternalOutput")

    def scr(name, shape, dt):
        if debug and dt == F32:
            return nc.dram_tensor(name, list(shape), dt, kind="ExternalOutput")
        return nc.dram_tensor(name, list(shape), dt)

    win_bf = scr("win_bf", (80, 128, 16, 128), BF16)
    wout_bf = scr("wout_bf", (8, 128, 32, 256), BF16)
    wup_bf = scr("wup_bf", (64, 128, 16, 128), BF16)
    bupl_bf = scr("bupl_bf", (16, 8192), BF16)
    bdnl_bf = scr("bdnl_bf", (16, 2048), BF16)
    wdn_bf = scr("wdn_bf", (16, 128, 64, 128), BF16)
    sz_scr = scr("sz_scr", (NQ, DI, TQ), BF16)
    xact_scr = scr("xact_scr", (NQ, DI, TQ), BF16)
    b_scr = scr("b_scr", (NQ, DXB, TQ), BF16)
    c_scr = scr("c_scr", (NQ, DI, TQ), BF16)
    yg_scr = scr("yg_scr", (NQ, DI, TQ), BF16)
    r1_scr = scr("r1_scr", (L, DM), F32)
    dt1_scr = scr("dt1_scr", (NQ, 128, TQ), BF16)
    FINAL_STORES = (["sout", "sr10", "sr11", "gst0", "gst1", "sdbg", "st"] +
                    [f"sev{i}" for i in range(4)] + ["sxao0", "sxao1", "dqs0", "dqs1"])
    if debug:
        dbg_hg = nc.dram_tensor("dbg_hg0", [128, 16 * TQ], BF16)
        dbg_ug = nc.dram_tensor("dbg_ug0", [128, 16 * TQ], BF16)
        dbg_pg = nc.dram_tensor("dbg_pg0", [128, 16 * TQ], BF16)
        dbg_t1 = nc.dram_tensor("dbg_t10", [128, 8 * TQ], BF16)
        dbg_yb = nc.dram_tensor("dbg_yb0", [128, 4 * TQ], F32, kind="ExternalOutput")
        dbg_dA = nc.dram_tensor("dbg_dA0", [128, 16 * TQ], F32, kind="ExternalOutput")
        dbg_dt = nc.dram_tensor("dbg_dt0", [128, TQ], F32, kind="ExternalOutput")
        dbg_oat = nc.dram_tensor("dbg_oat0", [2, 128, 4 * 2048], BF16)
        dbg_mv = nc.dram_tensor("dbg_mv0", [2, 128, 16], F32, kind="ExternalOutput")
        dbg_rr = nc.dram_tensor("dbg_rr0", [2, 128, 2048], F32, kind="ExternalOutput")
        dbg_sbn = nc.dram_tensor("dbg_sbn0", [128, 96], F32, kind="ExternalOutput")
    else:
        dbg_hg = dbg_ug = dbg_dA = dbg_dt = dbg_oat = dbg_mv = None

    stack = ExitStack()
    sem_names = (["dma", "tr", "pe", "dve", "act", "fA", "fB", "fC", "fU", "ln", "st", "gp", "fS",
                  "sdt1", "sgt", "swos", "sx0f", "sr1r", "sout", "sdbg"] +
                 [f"{p}{i}" for p in ("sx0t", "swb", "sxao", "bxz", "br", "cr", "gst",
                                      "tra", "trd", "sr1", "swf", "swd", "dqf", "dqs")
                  for i in range(2)] + [f"sev{i}" for i in range(4)])
    sems = {n: stack.enter_context(nc.semaphore(n)) for n in sem_names}
    P = Plan(nc, sems)

    _uid = [0]

    def sb(name, shape, dt, st=stack):
        _uid[0] += 1
        return st.enter_context(nc.sbuf_tensor(f"{name}_u{_uid[0]}", list(shape), dt))

    w1T = sb("w1T", (128, 16 * 128), BF16)
    w2T = sb("w2T", (128, 32 * 128), BF16)
    a_sc = sb("a_sc", (128, 512), F32)
    state = sb("state", (128, 512), F32)
    cw = sb("cw", (128, 128), F32)
    cb = sb("cb", (128, 32), F32)
    dvec = sb("dvec", (128, 32), F32)
    dtb = sb("dtb", (128, 32), F32)
    bupT = sb("bupT", (128, 64), F32)
    bdnT = sb("bdnT", (128, 16), F32)
    aupT = sb("aupT", (128, 16 * 16), BF16)
    bupL = sb("bupL", (16, 8192), BF16)
    adnT = sb("adnT", (128, 64 * 16), BF16)
    bdnL = sb("bdnL", (16, 2048), BF16)
    selx = sb("selx", (128, 4 * 128), BF16)
    halo = sb("halo", (128, 256), BF16)
    pA = [stack.enter_context(nc.psum_tensor(f"pA{i}", [128, 512], F32)) for i in range(2)]
    pB = [stack.enter_context(nc.psum_tensor(f"pB{i}", [128, 512], F32)) for i in range(2)]
    pC = [stack.enter_context(nc.psum_tensor(f"pC{i}", [128, 512], F32)) for i in range(2)]
    pU = [stack.enter_context(nc.psum_tensor(f"pU{i}", [128, 512], F32)) for i in range(2)]

    def stage_w():
        st = ExitStack()
        t1 = sb("w_t1", (128, 2048), F32, st)
        t2 = sb("w_t2", (128, 4096), F32, st)
        t3 = sb("w_t3", (128, 512), F32, st)
        t4 = sb("w_t4", (128, 1280), F32, st)
        t5 = sb("w_t5", (128, 512), F32, st)
        t6 = sb("w_t6", (16, 8192), F32, st)
        t7 = sb("w_t7", (16, 2048), F32, st)
        P.dma("sync", t1[:, :], bass.AP(w1_t, 0, [[128, 128], [128 * 128, 16], [1, 128]]))
        P.dma("sync", t2[:, :], bass.AP(w2_t, 0, [[128, 128], [128 * 128, 32], [1, 128]]))
        P.dma("sync", t3[:, :], alog_h.ap())
        P.dma("sync", t4[:, 0:256], bass.AP(aup_h, 0, [[16, 128], [128 * 16, 16], [1, 16]]))
        P.dma("sync", t4[:, 256:1280], bass.AP(adn_h, 0, [[16, 128], [128 * 16, 64], [1, 16]]))
        P.dma("sync", t5[:, :], bass.AP(selx_h, 0, [[128, 128], [128 * 128, 4], [1, 128]]))
        P.dma("sync", t6[:, :], bupl_h.ap())
        P.dma("sync", t7[:, :], bdnl_h.ap())
        for (dst, src) in [(cw, cw_h), (cb, cb_h), (dvec, dvec_h), (dtb, dtb_h),
                           (bupT, bup_h), (bdnT, bdn_h)]:
            P.dma("sync", dst[:, :], src.ap())
        v = P.val("dma")
        P.op("vector", lambda e: e.tensor_copy(out=w1T[:, :], in_=t1[:, :]), waits=[("dma", v)])
        P.op("vector", lambda e: e.tensor_copy(out=w2T[:, :], in_=t2[:, :]))
        P.op("vector", lambda e: e.tensor_copy(out=aupT[:, :], in_=t4[:, 0:256]))
        P.op("vector", lambda e: e.tensor_copy(out=adnT[:, :], in_=t4[:, 256:1280]))
        P.op("vector", lambda e: e.tensor_copy(out=selx[:, :], in_=t5[:, :]))
        P.op("vector", lambda e: e.tensor_copy(out=bupL[:, :], in_=t6[:, :]))
        P.op("vector", lambda e: e.tensor_copy(out=bdnL[:, :], in_=t7[:, :]))
        P.op("vector", lambda e: e.memset(state[:, :], 0.0))
        P.op("vector", lambda e: e.memset(halo[:, :], 0.0))
        av = P.op("scalar", lambda e: e.activation(out=a_sc[:, :], in_=t3[:, :], func=AF.Exp),
                  waits=[("dma", v)], inc="act")
        P.op("vector", lambda e: e.tensor_scalar(out=a_sc[:, :], in0=a_sc[:, :], scalar1=-1.0,
                                                 scalar2=None, op0=OP.mult),
             waits=[("act", av)])
        P.flush()
        st.close()

    class DqJobs:
        def __init__(self, buf_f, buf_b, qb, sc):
            self.buf_f, self.buf_b, self.qb, self.sc = buf_f, buf_b, qb, sc
            self.cast = {0: [], 1: []}
            self.store = {0: [], 1: []}
            self.n = 0

        def emit(self, loads, store_ap, quant, extra_lw=()):
            bi = self.n % 2
            self.n += 1
            lw = list(extra_lw)
            prev = self.cast[bi][-1] if self.cast[bi] else 0
            if prev:
                lw.append(("fA", prev))
            for i, (dst, src) in enumerate(loads(bi)):
                P.dma("sync", dst, src, waits=(lw if i == 0 else ()), sem=f"dqf{bi}")
            ldv = [(f"dqf{bi}", P.val(f"dqf{bi}"))]
            sw = [(f"dqs{bi}", self.store[bi][-1])] if self.store[bi] else []
            if quant:
                av = P.op("scalar", lambda e, bi=bi: e.activation(
                    out=self.buf_f[bi][:, :], in_=self.qb[bi][:, :], func=AF.Copy,
                    bias=-7.5), waits=ldv + ([("fA", prev)] if prev else []), inc="act")
                cv = P.op("vector", lambda e, bi=bi: e.tensor_tensor(
                    out=self.buf_b[bi][:, :], in0=self.buf_f[bi][:, :],
                    in1=self.sc[bi][:, :], op=OP.mult), waits=[("act", av)] + sw,
                    inc="fA")
            else:
                cv = P.op("vector", lambda e, bi=bi: e.tensor_copy(
                    out=self.buf_b[bi][:, :], in_=self.buf_f[bi][:, :]),
                    waits=ldv + sw, inc="fA")
            self.cast[bi].append(cv)
            sv = P.dma("gpsimd", store_ap, self.buf_b[bi][:, :], waits=[("fA", cv)],
                       sem=f"dqs{bi}")
            self.store[bi].append(sv)

    def dq_rest_jobs(dqj):
        jobs = []
        for ki in range(32):
            jobs.append(lambda lw=(), ki=ki: dqj.emit(
                lambda bi, ki=ki: [(dqj.buf_f[bi][:, :],
                                    bass.AP(wout_t, ki * 128 * 2048,
                                            [[2048, 128], [1, 2048]]))],
                bass.AP(wout_bf, ki * 256,
                        [[32 * 256, 128], [128 * 32 * 256, 8], [1, 256]]),
                quant=False, extra_lw=lw))
        for ki in range(16):
            for c4 in range(4):
                jobs.append(lambda lw=(), ki=ki, c4=c4: dqj.emit(
                    lambda bi, ki=ki, c4=c4: [
                        (dqj.qb[bi][:, :],
                         bass.AP(wq_up, ki * 128 * 8192 + c4 * 2048,
                                 [[8192, 128], [1, 2048]])),
                        (dqj.sc[bi][:, :],
                         bass.AP(sc_up, ki * 128 * 8192 + c4 * 2048,
                                 [[8192, 128], [1, 2048]]))],
                    bass.AP(wup_bf, (c4 * 16) * 128 * 2048 + ki * 128,
                            [[2048, 128], [128 * 2048, 16], [1, 128]]),
                    quant=True, extra_lw=lw))
        for ki in range(64):
            jobs.append(lambda lw=(), ki=ki: dqj.emit(
                lambda bi, ki=ki: [
                    (dqj.qb[bi][:, :], wq_dn[ki, :, :]),
                    (dqj.sc[bi][:, :], sc_dn[ki, :, :])],
                bass.AP(wdn_bf, ki * 128,
                        [[64 * 128, 128], [128 * 64 * 128, 16], [1, 128]]),
                quant=True, extra_lw=lw))
        return jobs

    def stage_dq_rest():
        st = ExitStack()
        buf_f = [sb(f"dqr_f{i}", (128, 2048), F32, st) for i in range(2)]
        buf_b = [sb(f"dqr_b{i}", (128, 2048), BF16, st) for i in range(2)]
        qb = [sb(f"dqr_q{i}", (128, 2048), I32, st) for i in range(2)]
        sc = [sb(f"dqr_s{i}", (128, 2048), F32, st) for i in range(2)]
        for job in dq_rest_jobs(DqJobs(buf_f, buf_b, qb, sc)):
            job()
        P.op("sync", lambda e: e.nop(),
             waits=[("dqs0", P.val("dqs0")), ("dqs1", P.val("dqs1"))])
        P.flush()
        st.close()

    def stage_dq_win():
        st = ExitStack()
        buf_f = [sb(f"dq_f{i}", (128, 2048), F32, st) for i in range(2)]
        buf_b = [sb(f"dq_b{i}", (128, 2048), BF16, st) for i in range(2)]
        dqj = DqJobs(buf_f, buf_b, None, None)
        for mi in range(80):
            dqj.emit(
                lambda bi, mi=mi: [(buf_f[bi][:, :],
                                    bass.AP(win_t, mi * 16 * 128 * 128,
                                            [[128, 128], [128 * 128, 16], [1, 128]]))],
                bass.AP(win_bf, mi * 128 * 2048, [[2048, 128], [1, 2048]]),
                quant=False)
        P.op("sync", lambda e: e.nop(),
             waits=[("dqs0", P.val("dqs0")), ("dqs1", P.val("dqs1"))])
        P.flush()
        st.close()

    class ACtx:
        pass

    def make_actx(q, st):
        a = ACtx()
        a.q = q
        a.x0t = sb("a_x0t", (128, 2048), F32, st)
        a.xtm = [sb(f"a_xtm{i}", (128, 2048), BF16, st) for i in range(2)]
        a.xbh = sb("a_xbh", (128, 16 * TH), BF16, st)
        a.wbuf = [sb(f"a_w{i}", (128, 4096), BF16, st) for i in range(2)]
        a.ev = [sb(f"a_ev{i}", (128, 512), BF16, st) for i in range(4)]
        a.xsm = sb("a_xsm", (128, 8 * TQ), BF16, st)
        a.xfe = [sb(f"a_xfe{i}", (128, TH + 8), BF16, st) for i in range(2)]
        a.ctmp = [sb(f"a_ct{i}", (128, TH), F32, st) for i in range(4)]
        a.xao = [sb(f"a_xao{i}", (128, TH), BF16, st) for i in range(2)]
        a.mm_hist, a.ev_hist, a.st_by_ebi, a.xao_st = [], [], {}, []
        a.silu_hist = []
        a.inline_conv = True
        return a

    def gen_a(a):
        q = a.q
        fb0 = P.count["fB"]
        cast_hist = []
        tr_hist = []
        for half in range(2):
            c0 = half * TH
            for tt4 in range(4):
                tt = half * 4 + tt4
                bi = tt % 2
                lw = [("dve", cast_hist[-1])] if cast_hist else []
                P.dma("sync", a.x0t[:, :],
                      x0[(q * 8 + tt) * 128:(q * 8 + tt + 1) * 128, :],
                      waits=lw, sem="sx0t0")
                trw = [("tr", tr_hist[-2])] if len(tr_hist) >= 2 else []
                if tt4 == 0 and half == 1:
                    trw.append(("pe", a.mm_hist[-1]))
                cv = P.op("vector", lambda e, bi=bi: e.tensor_copy(
                    out=a.xtm[bi][:, :], in_=a.x0t[:, :]),
                    waits=[("sx0t0", P.val("sx0t0"))] + trw, inc="dve")
                cast_hist.append(cv)
                for ki in range(16):
                    def trf(e, bi=bi, ki=ki, tt4=tt4):
                        return e.dma_start_transpose(
                            a.xbh[:, ki * TH + tt4 * 128: ki * TH + tt4 * 128 + 128],
                            a.xtm[bi][:, ki * 128:(ki + 1) * 128])
                    P.op("scalar", trf, waits=[("dve", cv)] if ki == 0 else (),
                         inc="tr", inc_amt=16)
                tr_hist.append(P.val("tr"))
                yield
            xbh_ready = P.val("tr")
            for mi in range(81):
                par = mi % 2
                ebi = mi % 4
                wv = None
                if mi < 80:
                    if mi % 2 == 0:
                        ww = [("pe", a.mm_hist[-3])] if len(a.mm_hist) >= 3 else []
                        P.dma("sync", a.wbuf[(mi // 2) % 2][:, :],
                              bass.AP(win_bf, mi * 128 * 2048,
                                      [[2048, 128], [128 * 2048, 2], [1, 2048]]),
                              waits=ww, sem=f"swb{(mi // 2) % 2}")
                    wv = (f"swb{(mi // 2) % 2}", P.val(f"swb{(mi // 2) % 2}"))

                def mmf(e, mi=mi, par=par):
                    for ki in range(16):
                        lhs = (a.wbuf[(mi // 2) % 2][:, (mi % 2) * 2048 + ki * 128:
                                                     (mi % 2) * 2048 + (ki + 1) * 128]
                               if mi < 80 else w1T[:, ki * 128:(ki + 1) * 128])
                        inst = e.matmul(pA[par][:, :], lhs,
                                        a.xbh[:, ki * TH: (ki + 1) * TH],
                                        start=(ki == 0), stop=(ki == 15))
                    return inst
                waits = [("tr", xbh_ready)]
                if len(a.ev_hist) >= 2:
                    waits.append(a.ev_hist[-2])
                else:
                    waits.append(("fB", fb0))
                if wv is not None:
                    waits.append(wv)
                pv = P.op("tensor", mmf, waits=waits, inc="pe")
                a.mm_hist.append(pv)
                ebw = ([(f"sev{ebi}", P.val(f"sev{ebi}"))]
                       if ebi in a.st_by_ebi else [])
                if mi < 32:
                    av = P.op("scalar", lambda e, par=par, ebi=ebi: e.activation(
                        out=a.ev[ebi][:, :], in_=pA[par][:, :], func=AF.Silu),
                        waits=[("pe", pv)] + ebw, inc="fA")
                    evt = ("fA", av)
                    dst = bass.AP(sz_scr, q * DI * TQ + (mi * 128) * TQ + c0,
                                  [[TQ, 128], [1, TH]])
                elif mi < 40:
                    av = P.op("vector", lambda e, par=par, mi=mi, c0=c0: e.tensor_copy(
                        out=a.xsm[:, (mi - 32) * TQ + c0:(mi - 32) * TQ + c0 + TH],
                        in_=pA[par][:, :]), waits=[("pe", pv)], inc="fA")
                    evt = ("fA", av)
                    dst = None
                else:
                    av = P.op("vector", lambda e, par=par, ebi=ebi: e.tensor_copy(
                        out=a.ev[ebi][:, :], in_=pA[par][:, :]),
                        waits=[("pe", pv)] + ebw, inc="fA")
                    evt = ("fA", av)
                    if mi < 48:
                        dst = bass.AP(b_scr, q * DXB * TQ + ((mi - 40) * 128) * TQ + c0,
                                      [[TQ, 128], [1, TH]])
                    elif mi < 80:
                        dst = bass.AP(c_scr, q * DI * TQ + ((mi - 48) * 128) * TQ + c0,
                                      [[TQ, 128], [1, TH]])
                    else:
                        dst = bass.AP(dt1_scr, q * 128 * TQ + c0, [[TQ, 128], [1, TH]])
                if dst is not None:
                    a.st_by_ebi[ebi] = P.dma("sync", dst, a.ev[ebi][:, :],
                                             waits=[evt], sem=f"sev{ebi}")
                a.ev_hist.append(evt)
                yield
            if a.inline_conv:
                conv_half(a, half)

    def conv_half(a, half):
        q = a.q
        silu_hist = a.silu_hist
        if True:
            c0 = half * TH
            for bi2 in range(NBLK):
                par = (81 + bi2) % 2
                xpar = bi2 % 2
                xi = bi2 // 4

                def selmm(e, par=par, xi=xi, bi2=bi2, c0=c0):
                    return e.matmul(pA[par][:, :],
                                    selx[:, (bi2 % 4) * 128:(bi2 % 4) * 128 + 128],
                                    a.xsm[:, xi * TQ + c0: xi * TQ + c0 + TH],
                                    start=True, stop=True)
                waits = [a.ev_hist[-1]]
                if len(a.ev_hist) >= 2:
                    waits.append(("fB", max(0, P.count["fB"] - 1)))
                pv = P.op("tensor", selmm, waits=waits, inc="pe")
                a.mm_hist.append(pv)
                hw = [("act", silu_hist[-2])] if len(silu_hist) >= 2 else []
                P.op("vector", lambda e, xpar=xpar, bi2=bi2: e.tensor_copy(
                    out=a.xfe[xpar][:, 0:8], in_=halo[:, bi2 * 8:bi2 * 8 + 8]),
                    waits=hw, inc="dve")
                P.op("vector", lambda e, xpar=xpar, par=par: e.tensor_copy(
                    out=a.xfe[xpar][:, 8:8 + TH], in_=pA[par][:, :]),
                    waits=[("pe", pv)], inc="fB")
                P.op("vector", lambda e, par=par, bi2=bi2: e.tensor_copy(
                    out=halo[:, bi2 * 8:bi2 * 8 + 8], in_=pA[par][:, TH - 8:TH]),
                    inc="dve")
                cw0 = [("act", silu_hist[-1])] if silu_hist else []
                P.op("vector", lambda e, xpar=xpar, bi2=bi2: e.tensor_scalar(
                    out=a.ctmp[0][:, :], in0=a.xfe[xpar][:, 5:5 + TH],
                    scalar1=cw[:, bi2 * 4:bi2 * 4 + 1], scalar2=None, op0=OP.mult),
                    waits=cw0, inc="dve")
                for k in range(1, 4):
                    P.op("vector", lambda e, xpar=xpar, bi2=bi2, k=k:
                         e.scalar_tensor_tensor(
                             out=a.ctmp[k][:, :], in0=a.xfe[xpar][:, 5 + k:5 + k + TH],
                             scalar=cw[:, bi2 * 4 + k:bi2 * 4 + k + 1],
                             in1=a.ctmp[k - 1][:, :], op0=OP.mult, op1=OP.add), inc="dve")
                dv = P.val("dve")
                xw = [(f"sxao{xpar}", P.val(f"sxao{xpar}"))] if len(a.xao_st) >= 2 else []
                av = P.op("scalar", lambda e, xpar=xpar, bi2=bi2: e.activation(
                    out=a.xao[xpar][:, :], in_=a.ctmp[3][:, :], func=AF.Silu,
                    bias=cb[:, bi2:bi2 + 1]), waits=[("dve", dv)] + xw, inc="act")
                silu_hist.append(av)
                a.xao_st.append(P.dma(
                    "sync", bass.AP(xact_scr, q * DI * TQ + (bi2 * 128) * TQ + c0,
                                    [[TQ, 128], [1, TH]]),
                    a.xao[xpar][:, :], waits=[("act", av)], sem=f"sxao{xpar}"))
    def emit_conv(a):
        if not a.inline_conv:
            conv_half(a, 0)
            conv_half(a, 1)
        P.op("sync", lambda e: e.nop(),
             waits=[(f"sev{i}", P.val(f"sev{i}")) for i in range(4)] +
                   [("sxao0", P.val("sxao0")), ("sxao1", P.val("sxao1"))])

    def stage_b(q, a_iter=None, a_total=170):
        st = ExitStack()
        spA = sb("b_spA", (128, TQ), F32, st)
        spB = sb("b_spB", (128, TQ), F32, st)
        dt_sb = sb("b_dt", (128, TQ), F32, st)
        dtx = sb("b_dtx", (128, TQ), BF16, st)
        xactb = [sb(f"b_xa{i}", (128, TQ), BF16, st) for i in range(2)]
        szb = [sb(f"b_sz{i}", (128, TQ), BF16, st) for i in range(2)]
        brep = [sb(f"b_br{i}", (128, 4 * TQ), BF16, st) for i in range(2)]
        crep = [sb(f"b_cr{i}", (128, 4 * TQ), BF16, st) for i in range(2)]
        dAg = [sb(f"b_dA{i}", (128, 4 * TQ), F32, st) for i in range(2)]
        a_done = [0]

        def adv_a(k):
            if a_iter is None:
                return
            target = ((k + 1) * a_total) // 128
            while a_done[0] < target:
                try:
                    next(a_iter)
                    a_done[0] += 1
                except StopIteration:
                    a_done[0] = a_total
                    break
        ug = sb("b_u", (128, 4 * TQ), BF16, st)
        hg = sb("b_h", (128, 4 * TQ), BF16, st)
        pg = sb("b_p", (128, 4 * TQ), BF16, st)
        t1 = sb("b_t1", (128, 2 * TQ), BF16, st)
        t2 = sb("b_t2", (128, TQ), F32, st)
        t3 = sb("b_t3", (128, TQ), F32, st)
        dt1b = sb("b_dt1", (128, TQ), BF16, st)
        bjunk = sb("b_junk", (128, 4), F32, st)
        yb = [sb(f"b_y{i}", (128, TQ), F32, st) for i in range(2)]
        gat = [sb(f"b_g{i}", (128, TQ), BF16, st) for i in range(2)]

        u_hist, p_hist, scan_hist, gate_hist, dtv_hist, dA_hist, store_hist = \
            [], [], [], [], [], [], []
        t1_hist, t3_hist = [], []
        br_vals, cr_vals = {}, {}

        def emit_bc_loads(k):
            bi2, g2 = divmod(k, 4)
            gp2 = g2 % 2
            n02 = g2 * 4
            bw = [("dve", u_hist[-1])] if u_hist else []
            P.dma("gpsimd", brep[gp2][:, :],
                  bass.AP(b_scr, q * DXB * TQ + (32 * bi2 + n02) * TQ,
                          [[16 * TQ, 2], [0, 64], [1, 4 * TQ]]), waits=bw, sem=f"br{gp2}")
            br_vals[k] = P.val(f"br{gp2}")
            cw_ = [("dve", p_hist[-1])] if p_hist else []
            P.dma("gpsimd", crep[gp2][:, :],
                  bass.AP(c_scr, q * DI * TQ + (128 * bi2 + n02) * TQ,
                          [[16 * TQ, 8], [0, 16], [1, 4 * TQ]]), waits=cw_, sem=f"cr{gp2}")
            cr_vals[k] = P.val(f"cr{gp2}")
        gpw = ([("pe", P.val("pe"))] +
               [(f"sev{i}", P.val(f"sev{i}")) for i in range(4)] +
               [("sxao0", P.val("sxao0")), ("sxao1", P.val("sxao1"))])
        if q == 0:
            bw0 = [("dqs0", P.val("dqs0")), ("dqs1", P.val("dqs1"))]
            P.op("sync", lambda e: e.nop(), waits=bw0)
            gpw += bw0
        else:
            gpw += [("sout", P.val("sout")), ("trd0", P.val("trd0")),
                    ("trd1", P.val("trd1"))]
        P.op("gpsimd", lambda e: e.nop(), waits=gpw)
        P.dma("sync", dt1b[:, :], bass.AP(dt1_scr, q * 128 * TQ, [[TQ, 128], [1, TQ]]),
              sem="sdt1")
        dt1v = P.val("sdt1")
        for bi in range(NBLK):
            par = bi % 2
            lw = [("dve", gate_hist[-2])] if len(gate_hist) >= 2 else []
            P.dma("sync", xactb[par][:, :],
                  bass.AP(xact_scr, q * DI * TQ + bi * 128 * TQ, [[TQ, 128], [1, TQ]]),
                  waits=lw, sem=f"bxz{par}")
            P.dma("sync", szb[par][:, :],
                  bass.AP(sz_scr, q * DI * TQ + bi * 128 * TQ, [[TQ, 128], [1, TQ]]),
                  sem=f"bxz{par}")
            xzv = (f"bxz{par}", P.val(f"bxz{par}"))

            def dtmm(e, bi=bi):
                for j in range(2):
                    inst = e.matmul(pB[j][:, :], w2T[:, bi * 128:(bi + 1) * 128],
                                    dt1b[:, j * 512:(j + 1) * 512],
                                    start=True, stop=True)
                return inst
            pw = [("act", dA_hist[-1])] if len(dA_hist) >= 1 else []
            if bi == 0:
                pw.append(("sdt1", dt1v))
            pv = P.op("tensor", dtmm, waits=pw, inc="pe")
            aw = [("pe", pv)]
            if len(dtv_hist) >= 1:
                aw.append(("dve", dtv_hist[-1]))
            if t3_hist:
                aw.append(("dve", t3_hist[-1]))
            for j in range(2):
                P.op("scalar", lambda e, j=j, bi=bi: e.activation(
                    out=spA[:, j * 512:(j + 1) * 512], in_=pB[j][:, :], func=AF.Abs,
                    bias=dtb[:, bi:bi + 1]), waits=aw if j == 0 else (), inc="act")
            P.op("scalar", lambda e: e.activation(out=spA[:, :], in_=spA[:, :], func=AF.Exp,
                                                  scale=-1.0), inc="act")
            P.op("scalar", lambda e: e.activation(out=spA[:, :], in_=spA[:, :], func=AF.Ln,
                                                  bias=1.0), inc="act")
            for j in range(2):
                P.op("scalar", lambda e, j=j, bi=bi: e.activation(
                    out=spB[:, j * 512:(j + 1) * 512], in_=pB[j][:, :], func=AF.Relu,
                    bias=dtb[:, bi:bi + 1]), inc="act")
            P.op("scalar", lambda e: e.activation(out=bjunk[:, 0:1], in_=bjunk[:, 1:2],
                                                  func=AF.Exp), inc="act")
            av = P.val("act")
            dw = [("act", av), xzv]
            P.op("vector", lambda e: e.tensor_tensor(out=dt_sb[:, :], in0=spA[:, :],
                                                     in1=spB[:, :], op=OP.add),
                 waits=dw, inc="dve")
            dtv = P.op("vector", lambda e, par=par: e.tensor_tensor(
                out=dtx[:, :], in0=dt_sb[:, :], in1=xactb[par][:, :], op=OP.mult), inc="dve")
            dtv_hist.append(dtv)

            for g in range(4):
                gp = g % 2
                n0 = g * 4
                gi = bi * 4 + g
                if gi == 0:
                    emit_bc_loads(0)
                    emit_bc_loads(1)
                adv_a(gi)
                bv = (f"br{gp}", br_vals[gi])
                cvv = (f"cr{gp}", cr_vals[gi])
                acts = []
                for n in range(4):
                    aw2 = []
                    if n == 0:
                        aw2.append(("dve", dtv))
                        if len(scan_hist) >= 2:
                            aw2.append(("dve", scan_hist[-2]))
                    a = P.op("scalar", lambda e, gp=gp, n=n, bi=bi, n0=n0: e.activation(
                        out=dAg[gp][:, n * TQ:(n + 1) * TQ], in_=dt_sb[:, :], func=AF.Exp,
                        scale=a_sc[:, bi * 16 + n0 + n: bi * 16 + n0 + n + 1]),
                        waits=aw2, inc="act")
                    acts.append(a)
                dA_hist.append(acts[-1])
                uv = P.op("vector", lambda e, gp=gp: e.tensor_tensor(
                    out=ug[:, :].rearrange("p (n t) -> p n t", n=4),
                    in0=dtx[:, :].unsqueeze(1).broadcast_to([128, 4, TQ]),
                    in1=brep[gp][:, :].rearrange("p (n t) -> p n t", n=4), op=OP.mult),
                    waits=[bv], inc="dve")
                u_hist.append(uv)
                sv = None
                for n in range(4):
                    sv = P.op("vector", lambda e, gp=gp, n=n, bi=bi, n0=n0: e.tensor_tensor_scan(
                        out=hg[:, n * TQ:(n + 1) * TQ], data0=dAg[gp][:, n * TQ:(n + 1) * TQ],
                        data1=ug[:, n * TQ:(n + 1) * TQ],
                        initial=state[:, bi * 16 + n0 + n: bi * 16 + n0 + n + 1],
                        op0=OP.mult, op1=OP.add),
                        waits=[("act", acts[n])], inc="dve")
                scan_hist.append(sv)
                P.op("vector", lambda e, bi=bi, n0=n0: e.tensor_copy(
                    out=state[:, bi * 16 + n0: bi * 16 + n0 + 4],
                    in_=hg[:, TQ - 1:4 * TQ:TQ]), inc="dve")
                pw2 = [cvv]
                pvv = P.op("vector", lambda e, gp=gp: e.tensor_tensor(
                    out=pg[:, :], in0=hg[:, :], in1=crep[gp][:, :], op=OP.mult),
                    waits=pw2, inc="dve")
                p_hist.append(pvv)
                if gi + 2 < 128:
                    emit_bc_loads(gi + 2)
                if debug and q == 0 and bi == 0:
                    sv1 = P.dma("sync", bass.AP(dbg_hg, g * 4 * TQ, [[16 * TQ, 128], [1, 4 * TQ]]),
                                hg[:, :], waits=[("dve", pvv)], sem="sdbg")
                    sv1 = P.dma("sync", bass.AP(dbg_ug, g * 4 * TQ, [[16 * TQ, 128], [1, 4 * TQ]]),
                                ug[:, :], sem="sdbg")
                    sv1 = P.dma("sync", bass.AP(dbg_dA, g * 4 * TQ, [[16 * TQ, 128], [1, 4 * TQ]]),
                                dAg[0][:, :], sem="sdbg")
                    sv1 = P.dma("sync", bass.AP(dbg_pg, g * 4 * TQ, [[16 * TQ, 128], [1, 4 * TQ]]),
                                pg[:, :], sem="sdbg")
                    if g == 0:
                        sv1 = P.dma("sync", dbg_dt.ap(), dt_sb[:, :], sem="sdbg")
                    P.op("vector", lambda e: e.nop(), waits=[("sdbg", P.val("sdbg"))])
                tw1 = P.op("vector", lambda e: e.tensor_tensor(
                    out=t1[:, :], in0=pg[:, 0:2 * TQ], in1=pg[:, 2 * TQ:4 * TQ],
                    op=OP.add), inc="dve")
                t1_hist.append(tw1)
                if g == 0:
                    tv = P.op("vector", lambda e: e.tensor_tensor(
                        out=yb[0][:, :], in0=t1[:, 0:TQ], in1=t1[:, TQ:2 * TQ],
                        op=OP.add), inc="dve")
                else:
                    P.op("vector", lambda e: e.tensor_tensor(
                        out=t2[:, :], in0=t1[:, 0:TQ], in1=t1[:, TQ:2 * TQ],
                        op=OP.add), inc="dve")
                    tv = P.op("vector", lambda e, g=g: e.tensor_tensor(
                        out=yb[g % 2][:, :], in0=yb[(g - 1) % 2][:, :], in1=t2[:, :],
                        op=OP.add), inc="dve")
                if debug and q == 0 and bi == 0:
                    P.dma("sync", bass.AP(dbg_t1, g * 2 * TQ, [[8 * TQ, 128], [1, 2 * TQ]]),
                                t1[:, :], waits=[("dve", tv)], sem="sdbg")
                    P.dma("sync", bass.AP(dbg_yb, g * TQ, [[4 * TQ, 128], [1, TQ]]),
                                yb[0][:, :], sem="sdbg")
                    P.op("vector", lambda e: e.nop(), waits=[("sdbg", P.val("sdbg"))])
            if debug and q == 0 and bi == 0:
                dvv = P.val("dve")
                avv = P.val("act")
                for g in range(4):
                    pass
            gw = []
            if len(store_hist) >= 2:
                gw.append((f"gst{par}", P.val(f"gst{par}")))
            t3v = P.op("vector", lambda e, par=par, bi=bi: e.scalar_tensor_tensor(
                out=t3[:, :], in0=xactb[par][:, :], scalar=dvec[:, bi:bi + 1],
                in1=yb[1][:, :], op0=OP.mult, op1=OP.add), waits=gw, inc="dve")
            t3_hist.append(t3v)
            gv = P.op("vector", lambda e, par=par: e.tensor_tensor(
                out=gat[par][:, :], in0=t3[:, :], in1=szb[par][:, :], op=OP.mult), inc="dve")
            gate_hist.append(gv)
            sv2 = P.dma("sync", bass.AP(yg_scr, q * DI * TQ + bi * 128 * TQ,
                                        [[TQ, 128], [1, TQ]]),
                        gat[par][:, :], waits=[("dve", gv)], sem=f"gst{par}")
            store_hist.append(sv2)
        while a_iter is not None and a_done[0] < a_total:
            adv_a(127)
            break
        P.op("sync", lambda e: e.nop(),
             waits=[("gst0", P.val("gst0")), ("gst1", P.val("gst1"))])
        P.flush()
        st.close()

    def stage_c(q):
        stO = ExitStack()
        r1T = sb("c_r1T", (128, 16 * TQ), BF16, stO)
        if q == 0:
            P.op("sync", lambda e: e.nop(),
                 waits=[("dqs0", P.val("dqs0")), ("dqs1", P.val("dqs1"))])

        st1 = ExitStack()
        gtq = sb("c_gt", (128, 32 * TH), BF16, st1)
        wos = sb("c_wos", (128, 32 * 256), BF16, st1)
        oat = sb("c_oat", (128, 4 * 2048), BF16, st1)
        x0f = [sb(f"c_x0{i}", (128, 2048), F32, st1) for i in range(1)]
        rr = [sb(f"c_rr{i}", (128, 2048), F32, st1) for i in range(2)]
        r1f = [sb(f"c_r1f{i}", (128, 2048), F32, st1) for i in range(2)]
        r1b = [sb(f"c_r1b{i}", (128, 2048), BF16, st1) for i in range(2)]
        sbn = sb("c_sbn", (128, 96), F32, st1)
        cjunk = sb("c_junk", (128, 1032), F32, st1)
        mvv = sb("c_mv", (128, 8), F32, st1)
        rstd = sb("c_rstd", (128, 4), F32, st1)
        lnt = sb("c_lnt", (128, 4), F32, st1)
        musc = sb("c_musc", (128, 4), F32, st1)

        opmm_hist = []
        rr_hist = []
        ap_hist_st = []
        ap_hist_tr = []
        for half in range(2):
            c0 = half * TH
            gw = [("pe", opmm_hist[-1])] if opmm_hist else []
            P.dma("sync", gtq[:, :],
                  bass.AP(yg_scr, q * DI * TQ + c0, [[TQ, 128], [128 * TQ, 32], [1, TH]]),
                  waits=gw, sem="sgt")
            gv = ("sgt", P.val("sgt"))
            evh = []
            for cs in range(8):
                ww = [("pe", opmm_hist[-1])] if opmm_hist else []
                P.dma("sync", wos[:, :],
                      bass.AP(wout_bf, cs * 128 * 32 * 256, [[8192, 128], [1, 8192]]),
                      waits=ww, sem="swos")
                wv = ("swos", P.val("swos"))
                for mt in range(4):
                    par = mt % 2

                    def opmm(e, mt=mt, par=par):
                        for ki in range(32):
                            inst = e.matmul(pC[par][:, 0:256],
                                            gtq[:, ki * TH + mt * 128: ki * TH + mt * 128 + 128],
                                            wos[:, ki * 256:(ki + 1) * 256],
                                            start=(ki == 0), stop=(ki == 31))
                        return inst
                    waits = [wv, gv]
                    if len(evh) >= 2:
                        waits.append(("fC", evh[-2]))
                    pv = P.op("tensor", opmm, waits=waits, inc="pe")
                    opmm_hist.append(pv)
                    e1 = P.op("vector", lambda e, mt=mt, par=par, cs=cs: e.tensor_copy(
                        out=oat[:, mt * 2048 + cs * 256: mt * 2048 + (cs + 1) * 256],
                        in_=pC[par][:, 0:256]), waits=[("pe", pv)], inc="fC")
                    evh.append(e1)
            opev = evh[-1]
            if debug and q == 0:
                P.dma("sync", bass.AP(dbg_oat, half * 128 * 8192, [[8192, 128], [1, 8192]]),
                      oat[:, :], waits=[("fC", opev)], sem="sdbg")
            for pr in range(2):
                mts = [2 * pr, 2 * pr + 1]
                for mt in mts:
                    par = mt % 2
                    row = q * TQ + c0 + mt * 128
                    lw = [("dve", rr_hist[-1])] if len(rr_hist) >= 1 else []
                    P.dma("sync", x0f[0][:, :], x0[row:row + 128, :], waits=lw, sem="sx0f")
                    rv = P.op("vector", lambda e, mt=mt, par=par: e.tensor_tensor(
                        out=rr[par][:, :], in0=x0f[0][:, :],
                        in1=oat[:, mt * 2048:(mt + 1) * 2048], op=OP.add),
                        waits=[("sx0f", P.val("sx0f")), ("fC", opev)], inc="dve")
                    for s4 in range(4):
                        P.op("vector", lambda e, mt=mt, par=par, s4=s4: e.bn_stats(
                            out=sbn[:, mt * 24 + s4 * 6: mt * 24 + (s4 + 1) * 6],
                            in_=rr[par][:, s4 * 512:(s4 + 1) * 512]), inc="dve")
                    rr_hist.append(rv)
                for _ in range(2):
                    P.op("vector", lambda e: e.tensor_copy(out=cjunk[:, 8:520], in_=cjunk[:, 520:1032]),
                         inc="dve")
                stats_done = P.val("dve")
                if debug and q == 1 and half == 0 and pr == 0:
                    P.dma("sync", bass.AP(dbg_rr, 0, [[2048, 128], [1, 2048]]),
                          rr[0][:, :], waits=[("dve", stats_done)], sem="sdbg")
                    P.dma("sync", bass.AP(dbg_rr, 128 * 2048, [[2048, 128], [1, 2048]]),
                          rr[1][:, :], sem="sdbg")
                    P.dma("sync", dbg_sbn.ap(), sbn[:, :], sem="sdbg")
                    P.op("vector", lambda e: e.nop(), waits=[("sdbg", P.val("sdbg"))])
                for mt in mts:
                    P.op("vector", lambda e, mt=mt: e.bn_aggr(
                        out=mvv[:, mt * 2:(mt + 1) * 2], in_=sbn[:, mt * 24:(mt + 1) * 24]),
                        waits=[("dve", stats_done)] if mt == mts[0] else (), inc="dve")
                for _ in range(2):
                    P.op("vector", lambda e: e.tensor_copy(out=cjunk[:, 8:520], in_=cjunk[:, 520:1032]),
                         inc="dve")
                P.op("vector", lambda e, pr=pr: e.tensor_scalar(
                    out=lnt[:, pr * 2:pr * 2 + 2], in0=mvv[:, 4 * pr + 1: 4 * pr + 4: 2],
                    scalar1=EPS, scalar2=None, op0=OP.add), inc="dve")
                for _ in range(2):
                    P.op("vector", lambda e: e.tensor_copy(out=cjunk[:, 8:520], in_=cjunk[:, 520:1032]),
                         inc="dve")
                bv = P.val("dve")
                P.op("scalar", lambda e, pr=pr: e.activation(
                    out=lnt[:, pr * 2:pr * 2 + 2], in_=lnt[:, pr * 2:pr * 2 + 2], func=AF.Ln),
                    waits=[("dve", bv)], inc="act")
                for _ in range(2):
                    P.op("scalar", lambda e: e.activation(out=cjunk[:, 8:520], in_=cjunk[:, 520:1032],
                                                          func=AF.Relu), inc="act")
                P.op("scalar", lambda e, pr=pr: e.activation(
                    out=rstd[:, pr * 2:pr * 2 + 2], in_=lnt[:, pr * 2:pr * 2 + 2], func=AF.Exp,
                    scale=-0.5), inc="act")
                for _ in range(2):
                    P.op("scalar", lambda e: e.activation(out=cjunk[:, 8:520], in_=cjunk[:, 520:1032],
                                                          func=AF.Relu), inc="act")
                agg_done = P.val("act")
                P.op("vector", lambda e, pr=pr: e.tensor_tensor(
                    out=musc[:, pr * 2:pr * 2 + 2], in0=mvv[:, 4 * pr: 4 * pr + 3: 2],
                    in1=rstd[:, pr * 2:pr * 2 + 2], op=OP.mult),
                    waits=[("act", agg_done)], inc="ln")
                for _ in range(2):
                    P.op("vector", lambda e: e.tensor_copy(out=cjunk[:, 8:520], in_=cjunk[:, 520:1032]),
                         inc="dve")
                lnv = P.val("ln")
                if debug and q == 1 and half == 0:
                    P.dma("sync", bass.AP(dbg_mv, pr * 128 * 16, [[16, 128], [1, 8]]),
                          mvv[:, :], waits=[("ln", P.val("ln"))], sem="sdbg")
                    P.dma("sync", bass.AP(dbg_mv, pr * 128 * 16 + 8, [[16, 128], [1, 4]]),
                          rstd[:, :], sem="sdbg")
                    P.dma("sync", bass.AP(dbg_mv, pr * 128 * 16 + 12, [[16, 128], [1, 4]]),
                          musc[:, :], sem="sdbg")
                if debug and q == 0 and pr == 1 and False:
                    P.dma("sync", bass.AP(dbg_mv, half * 128 * 16, [[16, 128], [1, 8]]),
                          mvv[:, :], waits=[("ln", lnv)], sem="sdbg")
                    P.dma("sync", bass.AP(dbg_mv, half * 128 * 16 + 8, [[16, 128], [1, 4]]),
                          rstd[:, :], waits=[("ln", lnv)], sem="sdbg")
                    P.dma("sync", bass.AP(dbg_mv, half * 128 * 16 + 12, [[16, 128], [1, 4]]),
                          musc[:, :], waits=[("ln", lnv)], sem="sdbg")
                for mt in mts:
                    par = mt % 2
                    row = q * TQ + c0 + mt * 128
                    aw = [("ln", lnv)] if mt == mts[0] else []
                    if len(ap_hist_st) >= 2:
                        aw.append((f"sr1{par}", P.val(f"sr1{par}")))
                    P.op("vector", lambda e, mt=mt, par=par: e.tensor_tensor(
                        out=r1f[par][:, :], in0=rr[par][:, :],
                        in1=rstd[:, mt:mt + 1].broadcast_to([128, 2048]), op=OP.mult),
                        waits=aw, inc="dve")
                    a1 = P.op("vector", lambda e, mt=mt, par=par: e.tensor_tensor(
                        out=r1f[par][:, :], in0=r1f[par][:, :],
                        in1=musc[:, mt:mt + 1].broadcast_to([128, 2048]), op=OP.subtract),
                        waits=(), inc="dve")
                    a2 = P.op("vector", lambda e, par=par: e.tensor_copy(
                        out=r1b[par][:, :], in_=r1f[par][:, :]),
                        waits=[(f"tra{par}", P.val(f"tra{par}"))] if len(ap_hist_tr) >= 2 else (),
                        inc="dve")
                    ap_hist_st.append(P.dma("sync", r1_scr[row:row + 128, :], r1f[par][:, :],
                                            waits=[("dve", a2)], sem=f"sr1{par}"))
                    for ki in range(16):
                        def trf(e, ki=ki, par=par, c0=c0, mt=mt):
                            return e.dma_start_transpose(
                                r1T[:, ki * TQ + c0 + mt * 128: ki * TQ + c0 + mt * 128 + 128],
                                r1b[par][:, ki * 128:(ki + 1) * 128])
                        P.op("scalar", trf, waits=[("dve", a2)] if ki == 0 else (),
                             inc=f"tra{par}", inc_amt=16)
                    ap_hist_tr.append(1)
        P.op("sync", lambda e: e.nop(),
             waits=[("sr10", P.val("sr10")), ("sr11", P.val("sr11")),
                    ("tra0", P.val("tra0")), ("tra1", P.val("tra1"))])
        P.flush()
        st1.close()
        r1T_ready = [("tra0", P.val("tra0")), ("tra1", P.val("tra1"))]

        st2 = ExitStack()
        h2sb = sb("c_h2", (128, 64 * TH), BF16, st2)
        lora_ld = ("swos", 0)
        wfb = [sb(f"c_wf{i}", (128, 16 * 128), BF16, st2) for i in range(2)]
        wdb = [sb(f"c_wd{i}", (128, 16 * 128), BF16, st2) for i in range(2)]
        xA = sb("c_xA", (16, TH), BF16, st2)
        xA2 = sb("c_xA2", (16, TH), BF16, st2)
        dcol = [sb(f"c_dc{i}", (128, 512), BF16, st2) for i in range(2)]
        dtok = sb("c_dtok", (128, 4 * 2048), BF16, st2)
        r1rd = [sb(f"c_r1r{i}", (128, 2048), F32, st2) for i in range(1)]
        r2f = [sb(f"c_r2{i}", (128, 2048), F32, st2) for i in range(2)]
        outf = [sb(f"c_of{i}", (128, 2048), F32, st2) for i in range(1)]
        sbn2 = sb("c_sbn2", (128, 96), F32, st2)
        cjunk2 = sb("c_junk2", (128, 520), F32, st2)
        mv2 = sb("c_mv2", (128, 8), F32, st2)
        rstd2 = sb("c_rstd2", (128, 4), F32, st2)
        lnt2 = sb("c_lnt2", (128, 4), F32, st2)
        musc2 = sb("c_musc2", (128, 4), F32, st2)

        dn_pv_hist = []
        r2_hist = []
        ost_hist = []
        for half in range(2):
            c0 = half * TH

            def xamm(e, c0=c0):
                for ki in range(16):
                    inst = e.matmul(pU[0][0:16, :], aupT[:, ki * 16:(ki + 1) * 16],
                                    r1T[:, ki * TQ + c0: ki * TQ + c0 + TH],
                                    start=(ki == 0), stop=(ki == 15))
                return inst
            xw = r1T_ready + [("fU", P.val("fU")), lora_ld]
            pv = P.op("tensor", xamm, waits=xw, inc="pe")
            xev = P.op("vector", lambda e: e.tensor_copy(out=xA[:, :], in_=pU[0][0:16, :]),
                       waits=[("pe", pv)], inc="fU")
            up_mm_hist = []
            up_ev_hist = []
            for mi in range(64):
                par = mi % 2
                ww = [("pe", up_mm_hist[-2])] if len(up_mm_hist) >= 2 else []
                P.dma("sync", wfb[par][:, :],
                      bass.AP(wup_bf, mi * 128 * 2048, [[2048, 128], [1, 2048]]), waits=ww,
                      sem=f"swf{par}")
                wv = (f"swf{par}", P.val(f"swf{par}"))

                def upmm(e, mi=mi, par=par, c0=c0):
                    for ki in range(16):
                        e.matmul(pU[par][:, :], wfb[par][:, ki * 128:(ki + 1) * 128],
                                 r1T[:, ki * TQ + c0: ki * TQ + c0 + TH],
                                 start=(ki == 0), stop=False)
                    return e.matmul(pU[par][:, :], bupL[:, mi * 128:(mi + 1) * 128],
                                    xA[0:16, :], start=False, stop=True)
                waits = [wv, ("fU", xev)]
                if len(up_ev_hist) >= 2:
                    waits.append(("fU", up_ev_hist[-2]))
                if half > 0 and mi == 0 and dn_pv_hist:
                    waits.append(("pe", dn_pv_hist[-1]))
                pv = P.op("tensor", upmm, waits=waits, inc="pe")
                up_mm_hist.append(pv)
                e1 = P.op("scalar", lambda e, mi=mi, par=par: e.activation(
                    out=h2sb[:, mi * TH:(mi + 1) * TH], in_=pU[par][:, :], func=AF.Relu,
                    bias=bupT[:, mi:mi + 1]), waits=[("pe", pv)], inc="fU")
                up_ev_hist.append(e1)
            upv = up_ev_hist[-1]

            def xa2mm(e):
                for ki in range(64):
                    inst = e.matmul(pC[0][0:16, :], adnT[:, ki * 16:(ki + 1) * 16],
                                    h2sb[:, ki * TH:(ki + 1) * TH],
                                    start=(ki == 0), stop=(ki == 63))
                return inst
            pv = P.op("tensor", xa2mm, waits=[("fU", upv), ("fC", P.val("fC"))], inc="pe")
            xa2v = P.op("vector", lambda e: e.tensor_copy(out=xA2[:, :], in_=pC[0][0:16, :]),
                        waits=[("pe", pv)], inc="fC")
            dn_ev_hist = []
            dn_tr_hist = []
            for mi in range(16):
                par = mi % 2
                for kg in range(4):
                    ww = [("pe", dn_pv_hist[-2])] if len(dn_pv_hist) >= 2 else []
                    P.dma("sync", wdb[kg % 2][:, :],
                          bass.AP(wdn_bf, mi * 128 * 8192 + kg * 16 * 128,
                                  [[8192, 128], [1, 2048]]), waits=ww,
                          sem=f"swd{kg % 2}")
                    wv = (f"swd{kg % 2}", P.val(f"swd{kg % 2}"))

                    def dnmm(e, mi=mi, par=par, kg=kg):
                        for k2 in range(16):
                            ki = kg * 16 + k2
                            inst = e.matmul(pC[par][:, :],
                                            wdb[kg % 2][:, k2 * 128:(k2 + 1) * 128],
                                            h2sb[:, ki * TH:(ki + 1) * TH],
                                            start=(ki == 0), stop=False)
                        return inst
                    waits = [wv, ("fC", xa2v)]
                    if len(dn_ev_hist) >= 2 and kg == 0:
                        waits.append(("fC", dn_ev_hist[-2]))
                    pvk = P.op("tensor", dnmm, waits=waits, inc="pe")
                    dn_pv_hist.append(pvk)

                def dnlora(e, mi=mi, par=par):
                    return e.matmul(pC[par][:, :], bdnL[:, mi * 128:(mi + 1) * 128],
                                    xA2[0:16, :], start=False, stop=True)
                pv = P.op("tensor", dnlora, inc="pe")
                dn_pv_hist.append(pv)
                aw = [("pe", pv)]
                if P.val(f"trd{par}") > 0:
                    aw.append((f"trd{par}", P.val(f"trd{par}")))
                if half > 0 and mi == 0 and r2_hist:
                    aw.append(("dve", r2_hist[-1]))
                av = P.op("vector", lambda e, mi=mi, par=par: e.tensor_scalar(
                    out=dcol[par][:, :], in0=pC[par][:, :], scalar1=bdnT[:, mi:mi + 1],
                    scalar2=None, op0=OP.add), waits=aw, inc="fC")
                dn_ev_hist.append(av)
                for mtl in range(4):
                    def trf(e, mi=mi, par=par, mtl=mtl):
                        return e.dma_start_transpose(
                            dtok[:, mtl * 2048 + mi * 128: mtl * 2048 + (mi + 1) * 128],
                            dcol[par][:, mtl * 128:(mtl + 1) * 128])
                    P.op("scalar", trf, waits=[("fC", av)] if mtl == 0 else (),
                         inc=f"trd{par}", inc_amt=16)
                dn_tr_hist.append(1)
            dnv = [("trd0", P.val("trd0")), ("trd1", P.val("trd1"))]
            for pr in range(2):
                mtls = [2 * pr, 2 * pr + 1]
                for mtl in mtls:
                    par = mtl % 2
                    row = q * TQ + c0 + mtl * 128
                    lw = [("dve", r2_hist[-1])] if len(r2_hist) >= 1 else []
                    P.dma("sync", r1rd[0][:, :], r1_scr[row:row + 128, :], waits=lw, sem="sr1r")
                    vv = P.op("vector", lambda e, mtl=mtl, par=par: e.tensor_tensor(
                        out=r2f[par][:, :], in0=r1rd[0][:, :],
                        in1=dtok[:, mtl * 2048:(mtl + 1) * 2048], op=OP.add),
                        waits=[("sr1r", P.val("sr1r"))] + dnv, inc="dve")
                    for s4 in range(4):
                        P.op("vector", lambda e, mtl=mtl, par=par, s4=s4: e.bn_stats(
                            out=sbn2[:, mtl * 24 + s4 * 6: mtl * 24 + (s4 + 1) * 6],
                            in_=r2f[par][:, s4 * 512:(s4 + 1) * 512]), inc="dve")
                    r2_hist.append(vv)
                for _ in range(2):
                    P.op("vector", lambda e: e.tensor_copy(out=cjunk2[:, 8:264], in_=cjunk2[:, 264:520]),
                         inc="dve")
                stats2 = P.val("dve")
                for mtl in mtls:
                    P.op("vector", lambda e, mtl=mtl: e.bn_aggr(
                        out=mv2[:, mtl * 2:(mtl + 1) * 2], in_=sbn2[:, mtl * 24:(mtl + 1) * 24]),
                        waits=[("dve", stats2)] if mtl == mtls[0] else (), inc="dve")
                for _ in range(2):
                    P.op("vector", lambda e: e.tensor_copy(out=cjunk2[:, 8:264], in_=cjunk2[:, 264:520]),
                         inc="dve")
                P.op("vector", lambda e, pr=pr: e.tensor_scalar(
                    out=lnt2[:, pr * 2:pr * 2 + 2], in0=mv2[:, 4 * pr + 1: 4 * pr + 4: 2],
                    scalar1=EPS, scalar2=None, op0=OP.add), inc="dve")
                for _ in range(2):
                    P.op("vector", lambda e: e.tensor_copy(out=cjunk2[:, 8:264], in_=cjunk2[:, 264:520]),
                         inc="dve")
                bv = P.val("dve")
                P.op("scalar", lambda e, pr=pr: e.activation(
                    out=lnt2[:, pr * 2:pr * 2 + 2], in_=lnt2[:, pr * 2:pr * 2 + 2], func=AF.Ln),
                    waits=[("dve", bv)], inc="act")
                for _ in range(2):
                    P.op("scalar", lambda e: e.activation(out=cjunk2[:, 8:264], in_=cjunk2[:, 264:520],
                                                          func=AF.Relu), inc="act")
                P.op("scalar", lambda e, pr=pr: e.activation(
                    out=rstd2[:, pr * 2:pr * 2 + 2], in_=lnt2[:, pr * 2:pr * 2 + 2], func=AF.Exp,
                    scale=-0.5), inc="act")
                for _ in range(2):
                    P.op("scalar", lambda e: e.activation(out=cjunk2[:, 8:264], in_=cjunk2[:, 264:520],
                                                          func=AF.Relu), inc="act")
                agg2 = P.val("act")
                P.op("vector", lambda e, pr=pr: e.tensor_tensor(
                    out=musc2[:, pr * 2:pr * 2 + 2], in0=mv2[:, 4 * pr: 4 * pr + 3: 2],
                    in1=rstd2[:, pr * 2:pr * 2 + 2], op=OP.mult),
                    waits=[("act", agg2)], inc="ln")
                for _ in range(2):
                    P.op("vector", lambda e: e.tensor_copy(out=cjunk2[:, 8:264], in_=cjunk2[:, 264:520]),
                         inc="dve")
                ln2 = P.val("ln")
                for mtl in mtls:
                    par = mtl % 2
                    row = q * TQ + c0 + mtl * 128
                    aw = [("ln", ln2)] if mtl == mtls[0] else []
                    if len(ost_hist) >= 1:
                        aw.append(("sout", P.val("sout")))
                    P.op("vector", lambda e, mtl=mtl, par=par: e.tensor_tensor(
                        out=outf[0][:, :], in0=r2f[par][:, :],
                        in1=rstd2[:, mtl:mtl + 1].broadcast_to([128, 2048]), op=OP.mult),
                        waits=aw, inc="dve")
                    a1 = P.op("vector", lambda e, mtl=mtl, par=par: e.tensor_tensor(
                        out=outf[0][:, :], in0=outf[0][:, :],
                        in1=musc2[:, mtl:mtl + 1].broadcast_to([128, 2048]), op=OP.subtract),
                        waits=(), inc="dve")
                    ost_hist.append(P.dma("sync", out_d[row:row + 128, :], outf[0][:, :],
                                          waits=[("dve", a1)], sem="sout"))
        P.op("sync", lambda e: e.nop(),
             waits=[("sout", P.val("sout")), ("trd0", P.val("trd0")),
                    ("trd1", P.val("trd1"))])
        P.flush()
        st2.close()
        stO.close()

    stage_w()
    stage_dq_win()
    stage_dq_rest()
    st_a = ExitStack()
    actx = make_actx(0, st_a)
    for _ in gen_a(actx):
        pass
    emit_conv(actx)
    P.flush()
    st_a.close()
    ZIP = False
    for q in range(NQ):
        if q + 1 < NQ:
            if ZIP:
                st_a = ExitStack()
                actx = make_actx(q + 1, st_a)
                stage_b(q, a_iter=gen_a(actx))
            else:
                stage_b(q)
                st_a = ExitStack()
                actx = make_actx(q + 1, st_a)
                for _ in gen_a(actx):
                    pass
            emit_conv(actx)
            P.flush()
            st_a.close()
        else:
            stage_b(q)
        stage_c(q)
    FINAL_STORES = (["sout", "sr10", "sr11", "gst0", "gst1", "sdbg", "st"] +
                    [f"sev{i}" for i in range(4)] + ["sxao0", "sxao1", "dqs0", "dqs1"])
    if debug:
        dbg_targets = [("sz_scr", sz_scr, NQ * DI * TQ), ("xact_scr", xact_scr, NQ * DI * TQ),
                       ("b_scr", b_scr, NQ * DXB * TQ), ("c_scr", c_scr, NQ * DI * TQ),
                       ("yg_scr", yg_scr, NQ * DI * TQ), ("dt1_scr", dt1_scr, NQ * 128 * TQ),
                       ("wup_bf", wup_bf, 64 * 16 * 128 * 128),
                       ("hg0", dbg_hg, 128 * 16 * TQ), ("ug0", dbg_ug, 128 * 16 * TQ),
                       ("pg0", dbg_pg, 128 * 16 * TQ), ("t10", dbg_t1, 128 * 8 * TQ),
                       ("oat0", dbg_oat, 2 * 128 * 8192)]
        st = ExitStack()
        db = sb("dbg_b", (128, 4096), BF16, st)
        df = sb("dbg_f", (128, 4096), F32, st)
        for name, t, nelem in dbg_targets:
            o = nc.dram_tensor("dbgo_" + name, [nelem // 4096, 4096], F32, kind="ExternalOutput")
            ntile = nelem // (128 * 4096)
            for i in range(ntile):
                P.dma("sync", db[:, :],
                      bass.AP(t, i * 128 * 4096, [[4096, 128], [1, 4096]]), sem="sdbg",
                      waits=[("sdbg", P.val("sdbg"))])
                cv = P.op("vector", lambda e: e.tensor_copy(out=df[:, :], in_=db[:, :]),
                          waits=[("sdbg", P.val("sdbg"))], inc="dve")
                P.dma("sync", bass.AP(o, i * 128 * 4096, [[4096, 128], [1, 4096]]),
                      df[:, :], waits=[("dve", cv)], sem="sdbg")
        P.op("sync", lambda e: e.nop(),
             waits=[(nm, P.val(nm)) for nm in FINAL_STORES])
        P.flush()
        st.close()
        stack.close()
        return nc
    P.op("sync", lambda e: e.nop(),
         waits=[(nm, P.val(nm)) for nm in FINAL_STORES])
    P.flush()
    stack.close()
    return nc


_NC_CACHE = None


def _get_nc():
    global _NC_CACHE
    if _NC_CACHE is None:
        _NC_CACHE = build_program()
    return _NC_CACHE


def kernel(L0_attn_hidden_states, L0_attn_in_proj_w, L0_attn_dt_in_proj_w,
           L0_attn_dt_proj_w, L0_attn_dt_proj_b, L0_attn_conv_w, L0_attn_conv_b,
           L0_attn_A_log, L0_attn_D, L0_attn_out_proj_w, L0_ffn_w_up_q,
           L0_ffn_w_up_scale, L0_ffn_b_up, L0_ffn_w_up_lora_a, L0_ffn_w_up_lora_b,
           L0_ffn_w_down_q, L0_ffn_w_down_scale, L0_ffn_b_down,
           L0_ffn_w_down_lora_a, L0_ffn_w_down_lora_b):
    f32 = lambda a: np.ascontiguousarray(np.asarray(a, np.float32))
    i32 = lambda a: np.ascontiguousarray(np.asarray(a, np.int32))
    hs = f32(L0_attn_hidden_states)

    Win = f32(L0_attn_in_proj_w)
    W1 = f32(L0_attn_dt_in_proj_w)
    W2 = f32(L0_attn_dt_proj_w)
    Wout = f32(L0_attn_out_proj_w)
    selx = np.zeros((4, 128, 128), np.float32)
    for r in range(4):
        for m in range(128):
            selx[r, 32 * r + 16 * (m // 64) + (m % 16), m] = 1.0
    shared = dict(
        win_t=np.ascontiguousarray(Win.reshape(80, 128, 16, 128).transpose(0, 2, 3, 1)),
        w1_t=np.ascontiguousarray(W1.reshape(128, 16, 128).transpose(1, 2, 0)),
        w2_t=np.ascontiguousarray(W2.reshape(32, 128, 128).transpose(0, 2, 1)),
        wout_t=np.ascontiguousarray(Wout.T.reshape(32, 128, 2048)),
        wq_up=np.ascontiguousarray(i32(L0_ffn_w_up_q).T.reshape(16, 128, 8192)),
        sc_up=np.ascontiguousarray(
            np.repeat(f32(L0_ffn_w_up_scale).T, 64, axis=0).reshape(16, 128, 8192)),
        wq_dn=np.ascontiguousarray(i32(L0_ffn_w_down_q).T.reshape(64, 128, 2048)),
        sc_dn=np.ascontiguousarray(
            np.repeat(f32(L0_ffn_w_down_scale).T, 64, axis=0).reshape(64, 128, 2048)),
        bup_h=np.ascontiguousarray(f32(L0_ffn_b_up).reshape(64, 128).T),
        bdn_h=np.ascontiguousarray(f32(L0_ffn_b_down).reshape(16, 128).T),
        aup_h=np.ascontiguousarray(f32(L0_ffn_w_up_lora_a).reshape(16, 128, 16)),
        bupl_h=f32(L0_ffn_w_up_lora_b),
        adn_h=np.ascontiguousarray(f32(L0_ffn_w_down_lora_a).reshape(64, 128, 16)),
        bdnl_h=f32(L0_ffn_w_down_lora_b),
        dtb_h=np.ascontiguousarray(f32(L0_attn_dt_proj_b).reshape(32, 128).T),
        cw_h=np.ascontiguousarray(f32(L0_attn_conv_w).reshape(32, 128, 4)
                                  .transpose(1, 0, 2).reshape(128, 128)),
        cb_h=np.ascontiguousarray(f32(L0_attn_conv_b).reshape(32, 128).T),
        dvec_h=np.ascontiguousarray(f32(L0_attn_D).reshape(32, 128).T),
        alog_h=np.ascontiguousarray(f32(L0_attn_A_log).reshape(32, 128, 16)
                                    .transpose(1, 0, 2).reshape(128, 512)),
        selx_h=selx,
    )
    nc = _get_nc()
    in_maps = []
    for c in range(8):
        m = dict(shared)
        m["x0"] = np.ascontiguousarray(hs[c])
        in_maps.append(m)
    res = run_bass_kernel_spmd(nc, in_maps, core_ids=list(range(8)))
    out = np.stack([np.asarray(r["out"], np.float32) for r in res.results], axis=0)
    return out
